# version 08
# baseline (speedup 1.0000x reference)
from contextlib import ExitStack

import numpy as np
import ml_dtypes

import concourse.bass as bass
import concourse.mybir as mybir
from concourse.bass_utils import run_bass_kernel_spmd

F32 = mybir.dt.float32
BF16 = mybir.dt.bfloat16
U8 = mybir.dt.uint8
AF = mybir.ActivationFunctionType
OP = mybir.AluOpType
AX = mybir.AxisListType.X

NCORE = 8
D = 2048
S = 16384
T = S // NCORE
DCN = 16
JCN = 48
EH = 8192
ECN = 64
TAU = 1024
NTAU = T // TAU
ROTC = 8
EPS_LN = 1e-5
EPS_Q = 1e-8
NW = float(EH * D)


class Plan:
    ENGINES = ("tensor", "vector", "scalar", "sync", "gpsimd")

    def __init__(self):
        self.ops = {e: [] for e in self.ENGINES}
        self.counts = {}
        self.sems = []
        self.last_ev = {}

    def sem(self, name):
        self.sems.append(name)
        self.counts[name] = 0
        return name

    def op(self, eng, fn, after=(), sig=True, tag=""):
        waits = {}
        for ev in after:
            if ev is None:
                continue
            s, v = ev
            if s == "eng_" + eng:
                continue
            waits[s] = max(waits.get(s, 0), v)
        sem, amt, ev = None, 0, None
        if sig:
            sem = "eng_" + eng
            self.counts[sem] += 1
            amt = 1
            ev = (sem, self.counts[sem])
            self.last_ev[eng] = ev
        self.ops[eng].append((sorted(waits.items()), fn, sem, amt, tag, False))
        return ev

    def dma(self, eng, sem, fn, after=(), n=1, tag=""):
        waits = {}
        for ev in after:
            if ev is None:
                continue
            s, v = ev
            waits[s] = max(waits.get(s, 0), v)
        self.counts[sem] += 16 * n
        ev = (sem, self.counts[sem])
        self.ops[eng].append((sorted(waits.items()), fn, sem, 16 * n, tag, True))
        self.last_ev["dma_" + sem] = ev
        return ev

    def barrier(self, extra=(), tag="bar"):
        evs = [e for e in self.last_ev.values() if e is not None]
        evs += [e for e in extra if e is not None]
        out = []
        for eng in self.ENGINES:
            out.append(self.op(eng, lambda E: E.nop(), after=evs, tag=tag))
        return out

    def check(self):
        pos = {e: 0 for e in self.ENGINES}
        cnt = {s: 0 for s in self.sems}
        progress = True
        while progress:
            progress = False
            for e in self.ENGINES:
                while pos[e] < len(self.ops[e]):
                    waits, fn, sem, amt, tag, is_dma = self.ops[e][pos[e]]
                    if all(cnt[s] >= v for s, v in waits):
                        pos[e] += 1
                        progress = True
                        if sem is not None:
                            cnt[sem] += amt
                    else:
                        break
        stuck = [(e, self.ops[e][pos[e]][4], self.ops[e][pos[e]][0])
                 for e in self.ENGINES if pos[e] < len(self.ops[e])]
        if stuck:
            raise RuntimeError(f"plan deadlock: {stuck[:4]}")


def build_nc():
    import os
    PH = int(os.environ.get("KMK_PHASES", "4"))
    DODOWN = int(os.environ.get("KMK_D", "3"))
    L3 = int(os.environ.get("KMK_L3", "4"))
    DBG = int(os.environ.get("KMK_DBG", "0"))
    DOSCALE = int(os.environ.get("KMK_SCALE", "1"))
    DOROT = int(os.environ.get("KMK_ROT", "1"))
    nc = bass.Bass("TRN2", target_bir_lowering=False)
    pl = Plan()
    es = ExitStack()

    def din(name, n, dt):
        return nc.dram_tensor(name, [n], dt, kind="ExternalInput").ap()

    x_in = din("x", T * D, F32).rearrange("(n p d) -> n p d", p=128, d=D)
    xT_in = din("xT", D * T, F32).rearrange("(n p t) -> n p t", p=128, t=T)
    w_in_t = din("w_in_t", JCN * 128 * D, F32).rearrange(
        "(j p q) -> j p q", p=128, q=D)
    w_out_t = din("w_out_t", DCN * 128 * D, F32).rearrange(
        "(j p q) -> j p q", p=128, q=D)
    wug_t = din("wug_t", ECN * 128 * 2 * D, F32).rearrange(
        "(e p q) -> e p q", p=128, q=2 * D)
    wd_t = din("wd_t", EH * D, F32).rearrange(
        "(n p q) -> n p q", p=128, q=1024)
    cos_in = din("cos_t", ROTC * 128 * T, BF16).rearrange(
        "(c p t) -> c p t", p=128, t=T)
    sin_in = din("sin_t", ROTC * 128 * T, BF16).rearrange(
        "(c p t) -> c p t", p=128, t=T)
    gamma_in = din("gamma_t", 128 * DCN, F32).rearrange("(p c) -> p c", c=DCN)
    beta_in = din("beta_t", 128 * DCN, F32).rearrange("(p c) -> p c", c=DCN)
    b_in_in = din("b_in_t", 128 * JCN, F32).rearrange("(p c) -> p c", c=JCN)
    b_out_in = din("b_out_t", 128 * DCN, F32).rearrange("(p c) -> p c", c=DCN)
    ident_in = din("ident", 128 * 128, F32).rearrange("(p q) -> p q", q=128)
    perm_in = din("perm", 128 * 128, BF16).rearrange("(p q) -> p q", q=128)

    y_out = nc.dram_tensor("y", [T * D], F32, kind="ExternalOutput").ap() \
        .rearrange("(n p d) -> n p d", p=128, d=D)
    cc_in = nc.dram_tensor("cc_in", [16, 128], F32)
    cc_out = nc.dram_tensor("cc_out", [16, 128], F32)
    dbg = {}
    if DBG:
        for nm, n, dt in [("d_z1", 128 * DCN * TAU, BF16),
                          ("d_qa0", 128 * DCN * TAU, BF16),
                          ("d_pbuf", 128 * DCN * TAU, BF16),
                          ("d_aacc", 128 * DCN, F32),
                          ("d_aprime", 128 * DCN, F32),
                          ("d_h1", 128 * DCN * TAU, F32),
                          ("d_h2", 128 * DCN * TAU, F32),
                          ("d_srow", 16, F32),
                          ("d_hid", 128 * 8 * TAU, BF16), ("d_stn", 128 * 16 * 8, F32), ("d_sq3", 128 * 9, F32), ("d_qd", 128 * 1024, BF16), ("d_dn", 128 * 1024, BF16)]:
            dbg[nm] = nc.dram_tensor(nm, [n], dt, kind="ExternalOutput").ap()

    def sb(name, shape, dt):
        return es.enter_context(nc.sbuf_tensor(name, shape, dt))

    ident = sb("c_ident", [128, 128], F32)
    perm = sb("c_perm", [128, 128], BF16)
    ones_col_bf = sb("c_ones_cb", [128, 1], BF16)
    ones_row_bf = sb("c_ones_rb", [1, 128], BF16)
    ones_row_f = sb("c_ones_rf", [1, 128], F32)
    ones_col_f = sb("c_ones_cf", [128, 1], F32)
    gamma = sb("c_gamma", [128, DCN], F32)
    beta = sb("c_beta", [128, DCN], F32)
    b_in_sb = sb("c_bin", [128, JCN], F32)
    b_out_sb = sb("c_bout", [128, DCN], F32)
    a_acc = sb("a_acc", [128, DCN], F32)
    a_tmp = sb("a_tmp", [128, DCN], F32)
    a_prime = sb("a_prime", [128, DCN], F32)
    aflat = sb("aflat", [16, 128], F32)

    stn = {k: sb("st_" + k, [128, 16], F32) for k in
           ("sx", "sq", "mu", "ex2", "veps", "sqr", "i0", "t1", "inv", "nmi")}
    srow = sb("srow", [1, 16], F32)

    sacc_u = sb("sacc_u", [128, ECN], F32)
    sacc_g = sb("sacc_g", [128, ECN], F32)
    sacc_d = sb("sacc_d", [128, 64], F32)
    scol = sb("scol", [128, 3], F32)
    s_thr = sb("s_thr", [128, 3], F32)
    s_nthr = sb("s_nthr", [128, 3], F32)
    s_scale = sb("s_scale", [128, 3], F32)

    qa_es = [ExitStack() for _ in range(NTAU)]
    qa = [None, None]
    qa[1] = qa_es[1].enter_context(
        nc.sbuf_tensor("qa1", [128, DCN * TAU], BF16))
    qa[0] = qa_es[0].enter_context(
        nc.sbuf_tensor("qa0", [128, DCN * TAU], BF16))
    down_acc = None

    for e in Plan.ENGINES:
        pl.sem("eng_" + e)
    SD = {}
    for name, k in [("mx", 1), ("x", 2), ("tb", 2), ("wi", 2), ("sc", 2),
                    ("x2", 2), ("wo", 2), ("wu", 2), ("wd", 2), ("ou", 2),
                    ("cc", 1)]:
        SD[name] = [pl.sem(f"dma_{name}{i}") for i in range(k)]
    dstream_idx = {k: 0 for k in SD}

    def dma(eng, stream, fn, after=(), n=1, tag=""):
        i = dstream_idx[stream]
        dstream_idx[stream] += 1
        sem = SD[stream][i % len(SD[stream])]
        return pl.dma(eng, sem, fn, after=after, n=n, tag=tag)

    def T_(fn, after=(), tag="t"):
        return pl.op("tensor", fn, after, tag=tag)

    def V_(fn, after=(), tag="v"):
        return pl.op("vector", fn, after, tag=tag)

    def S_(fn, after=(), tag="s"):
        return pl.op("scalar", fn, after, tag=tag)

    def G_(fn, after=(), tag="g"):
        return pl.op("gpsimd", fn, after, tag=tag)

    ev_mx = dma("sync", "mx", lambda E: [
        E.dma_start(out=ident[:], in_=ident_in),
        E.dma_start(out=perm[:], in_=perm_in),
        E.dma_start(out=gamma[:], in_=gamma_in),
        E.dma_start(out=beta[:], in_=beta_in),
        E.dma_start(out=b_in_sb[:], in_=b_in_in),
        E.dma_start(out=b_out_sb[:], in_=b_out_in),
    ], n=6, tag="mx")

    def fn_consts(E):
        E.memset(ones_col_bf[:], 1.0)
        E.memset(ones_row_bf[:], 1.0)
        E.memset(ones_row_f[:], 1.0)
        E.memset(ones_col_f[:], 1.0)
        E.memset(a_acc[:], 0.0)
        return E.memset(sacc_d[:], 0.0)
    ev_const = V_(fn_consts, tag="consts")

    scale_es = ExitStack()
    scaleb = [scale_es.enter_context(
        nc.sbuf_tensor(f"scaleb{i}", [128, 2048], F32, side="right")) for i in range(2)]
    scale_jobs = ([("u", ec) for ec in range(ECN)]
                  + [("g", ec) for ec in range(ECN)]
                  + [("d", n2) for n2 in range(64)])
    if not DOSCALE:
        scale_jobs = scale_jobs[:2]
    sst = {"iss": 0, "con": 0, "ld": [None, None], "use": [None, None]}

    def scale_issue():
        i = sst["iss"]
        if i >= len(scale_jobs):
            return
        sst["iss"] = i + 1
        kind, idx = scale_jobs[i]
        buf = scaleb[i % 2]
        if kind == "u":
            ev = dma("scalar", "sc", lambda E, idx=idx, buf=buf: [
                E.dma_start(out=buf[:], in_=wug_t[idx][:, 0:2048])],
                after=(sst["use"][i % 2],), tag=f"scl{i}")
        elif kind == "g":
            ev = dma("scalar", "sc", lambda E, idx=idx, buf=buf: [
                E.dma_start(out=buf[:], in_=wug_t[idx][:, 2048:4096])],
                after=(sst["use"][i % 2],), tag=f"scl{i}")
        else:
            ev = dma("scalar", "sc", lambda E, idx=idx, buf=buf: [
                E.dma_start(out=buf[:, 1024 * j:1024 * (j + 1)],
                            in_=wd_t[2 * idx + j]) for j in range(2)],
                after=(sst["use"][i % 2],), n=2, tag=f"scl{i}")
        sst["ld"][i % 2] = ev

    def scale_consume():
        i = sst["con"]
        if i >= len(scale_jobs):
            return
        sst["con"] = i + 1
        kind, idx = scale_jobs[i]
        buf = scaleb[i % 2]
        ev_ld = sst["ld"][i % 2]
        dst = {"u": sacc_u, "g": sacc_g, "d": sacc_d}[kind]
        if i % 2 == 0:
            sst["use"][i % 2] = S_(lambda E, buf=buf, c=idx, dst=dst:
                                   E.activation(buf[:], buf[:], AF.Abs,
                                                accum_out=dst[:, c:c + 1]),
                                   after=(ev_ld,), tag=f"sa{i}")
        else:
            sst["use"][i % 2] = V_(lambda E, buf=buf, c=idx, dst=dst:
                                   E.reduce_sum(dst[:, c:c + 1], buf[:],
                                                axis=AX,
                                                apply_absolute_value=True),
                                   after=(ev_ld,), tag=f"sv{i}")

    def scale_step():
        while sst["iss"] - sst["con"] < 2 and sst["iss"] < len(scale_jobs):
            scale_issue()
        scale_consume()

    def scale_flush():
        while sst["con"] < len(scale_jobs):
            scale_step()

    ev_qa_done = [None, None]

    for ta in range(NTAU):
        sa = ExitStack()
        z1 = sa.enter_context(nc.sbuf_tensor(f"z1_{ta}", [128, DCN * TAU], BF16, side="right"))
        pbuf = sa.enter_context(nc.sbuf_tensor(f"pbuf_{ta}", [128, DCN * TAU], BF16, side="right"))

        sp = ExitStack()
        xb = [sp.enter_context(nc.sbuf_tensor(f"xb{i}_{ta}", [128, D], F32, side="right"))
              for i in range(2)]
        z0b = [sp.enter_context(nc.sbuf_tensor(f"z0b{i}_{ta}", [128, D], F32, side="right"))
               for i in range(2)]
        psA = [sp.enter_context(nc.psum_tensor(f"psA{i}_{ta}", [128, 256], F32))
               for i in range(2)]

        ev_z0 = [None] * 8
        prev_x_user = [ev_mx, ev_mx]
        prev_z0_user = [None, None]
        ev_evac = {}
        for g in range(4):
            tcs = [2 * g, 2 * g + 1]
            ev_x = {}
            for tc in tcs:
                gi = ta * 8 + tc
                ev_x[tc] = dma("sync", "x", lambda E, gi=gi, b=xb[tc % 2]: [
                    E.dma_start(out=b[:], in_=x_in[gi])],
                    after=(prev_x_user[tc % 2],), tag=f"x{ta}_{tc}")
            ev_st = []
            for tc in tcs:
                e1 = S_(lambda E, tc=tc, z0b=z0b, xb=xb: E.activation(
                    z0b[tc % 2][:], xb[tc % 2][:], AF.Square,
                    accum_out=stn["sq"][:, tc:tc + 1]),
                    after=(ev_x[tc], prev_z0_user[tc % 2]), tag=f"l1q{ta}{tc}")
                e2 = V_(lambda E, tc=tc, xb=xb: E.reduce_sum(
                    stn["sx"][:, tc:tc + 1], xb[tc % 2][:], axis=AX),
                    after=(ev_x[tc],), tag=f"l1x{ta}{tc}")
                ev_st += [e1, e2]
            c0, c1 = 2 * g, 2 * g + 2

            def fn_v1(E, c0=c0, c1=c1):
                E.tensor_scalar(stn["mu"][:, c0:c1], stn["sx"][:, c0:c1],
                                1.0 / D, None, OP.mult)
                E.tensor_scalar(stn["ex2"][:, c0:c1], stn["sq"][:, c0:c1],
                                1.0 / D, None, OP.mult)
                E.drain()
                E.tensor_mul(stn["veps"][:, c0:c1], stn["mu"][:, c0:c1],
                             stn["mu"][:, c0:c1])
                E.drain()
                E.tensor_sub(stn["veps"][:, c0:c1], stn["ex2"][:, c0:c1],
                             stn["veps"][:, c0:c1])
                E.drain()
                E.tensor_scalar(stn["veps"][:, c0:c1],
                                stn["veps"][:, c0:c1], EPS_LN, None,
                                OP.add)
                return E.drain()
            ev_v1 = V_(fn_v1, after=ev_st, tag=f"l1v{ta}{g}")
            ev_s1 = S_(lambda E, c0=c0, c1=c1: E.activation(
                stn["sqr"][:, c0:c1], stn["veps"][:, c0:c1], AF.Sqrt),
                after=(ev_v1,), tag=f"l1s{ta}{g}")

            def fn_v2(E, c0=c0, c1=c1):
                E.drain()
                E.reciprocal(stn["i0"][:, c0:c1], stn["sqr"][:, c0:c1])
                E.drain()
                E.tensor_mul(stn["t1"][:, c0:c1], stn["i0"][:, c0:c1],
                             stn["i0"][:, c0:c1])
                E.drain()
                E.tensor_mul(stn["t1"][:, c0:c1], stn["t1"][:, c0:c1],
                             stn["veps"][:, c0:c1])
                E.drain()
                E.tensor_scalar(stn["t1"][:, c0:c1], stn["t1"][:, c0:c1],
                                -0.5, 1.5, OP.mult, OP.add)
                E.drain()
                E.tensor_mul(stn["inv"][:, c0:c1], stn["i0"][:, c0:c1],
                             stn["t1"][:, c0:c1])
                E.drain()
                E.tensor_mul(stn["nmi"][:, c0:c1], stn["mu"][:, c0:c1],
                             stn["inv"][:, c0:c1])
                E.drain()
                E.tensor_scalar(stn["nmi"][:, c0:c1],
                                stn["nmi"][:, c0:c1], -1.0, None,
                                OP.mult)
                return E.drain()
            ev_v2 = V_(fn_v2, after=(ev_s1,), tag=f"l1w{ta}{g}")
            for tc in tcs:
                ev_z0[tc] = S_(lambda E, tc=tc, z0b=z0b, xb=xb: E.activation(
                    z0b[tc % 2][:], xb[tc % 2][:], AF.Identity,
                    scale=stn["inv"][:, tc:tc + 1],
                    bias=stn["nmi"][:, tc:tc + 1]),
                    after=(ev_v2,), tag=f"z0{ta}{tc}")
                prev_x_user[tc % 2] = ev_z0[tc]
            ev_tT = {}
            for dc in range(DCN):
                par = dc % 2

                def fn_tr(E, dc=dc, par=par, tcs=tcs, psA=psA, z0b=z0b):
                    last = None
                    for i, tc in enumerate(tcs):
                        last = E.transpose(
                            psA[par][:, 128 * i:128 * (i + 1)],
                            z0b[tc % 2][:, 128 * dc:128 * (dc + 1)], ident[:])
                    return last
                ev_tT[dc] = T_(fn_tr, after=(ev_z0[tcs[0]], ev_z0[tcs[1]],
                                             ev_mx, ev_evac.get((g - 1, dc)),
                                             ev_evac.get((g, dc - 2))),
                               tag=f"tr{ta}{g}_{dc}")
                col0 = dc * TAU + g * 256
                ev_evac[(g, dc)] = S_(lambda E, par=par, dc=dc, col0=col0,
                                      z1=z1, psA=psA:
                                      E.activation(z1[:, col0:col0 + 256],
                                                   psA[par][:], AF.Identity,
                                                   scale=gamma[:, dc:dc + 1],
                                                   bias=beta[:, dc:dc + 1]),
                                      after=(ev_tT[dc],), tag=f"ze{ta}{g}_{dc}")
            prev_z0_user[0] = ev_tT[DCN - 1]
            prev_z0_user[1] = ev_tT[DCN - 1]
            scale_step()
        bz = pl.barrier(tag=f"z1b{ta}")
        sp.close()

        sm = ExitStack()

        def sbm(name, shape, dt):
            return sm.enter_context(nc.sbuf_tensor(f"{name}_{ta}", shape, dt, side="right"))
        wjc = [sbm(f"wjc{i}", [128, D], BF16) for i in range(2)]
        tabc = [sbm(f"tabc{i}", [128, TAU], BF16) for i in range(2)]
        tabs = [sbm(f"tabs{i}", [128, TAU], BF16) for i in range(2)]
        kbuf = sbm("kbuf", [128, TAU], BF16)
        krot = sbm("krot", [128, TAU], BF16)
        vbuf = sbm("vbuf", [128, TAU], BF16)
        vrot = sbm("vrot", [128, TAU], BF16)
        t1b = sbm("t1b", [128, TAU], BF16)
        ksqb = sbm("ksqb", [128, TAU], BF16)
        nrmk = sbm("nrmk", [1, TAU], F32)
        nrmv = sbm("nrmv", [1, TAU], F32)
        rrow = sbm("rrow", [1, TAU], F32)
        rbf = sbm("rbf", [1, TAU], BF16)
        rbc = sbm("rbc", [128, TAU], BF16)
        adump = sbm("adump", [128, TAU], BF16)
        psM = [[sm.enter_context(nc.psum_tensor(f"psM{s}{t}_{ta}", [128, 512], F32))
                for t in range(2)] for s in range(2)]
        psP = [sm.enter_context(nc.psum_tensor(f"psP{t}_{ta}", [128, 512], F32))
               for t in range(2)]
        psN = [sm.enter_context(nc.psum_tensor(f"psN{t}_{ta}", [1, 512], F32))
               for t in range(2)]

        ev_nrm0 = V_(lambda E: [E.memset(nrmk[:], 0.0),
                                E.memset(nrmv[:], 0.0)][-1], tag=f"n0{ta}")

        jorder = []
        for c in range(16):
            jorder += [16 + c, 32 + c]
        jorder += list(range(16))

        ev_mm = {}
        ev_free = {}
        ev_n_acc = None
        ev_perm_free = None
        ev_tab_free = [None, None]
        ev_tab = [None, None]
        ev_kfree = None
        ev_vfree = None
        ev_krot_free = None
        ev_vrot_free = None
        ev_t1_free = None
        k_src = {}
        k_ev = {}
        for ji, jc in enumerate(jorder):
            par = ji % 2
            ev_w = dma("gpsimd", "wi", lambda E, jc=jc, par=par: [
                E.dma_start(out=wjc[par][:], in_=w_in_t[jc])],
                after=(ev_mm.get(ji - 2),), tag=f"wi{ta}_{ji}")

            def fn_mm(E, par=par, wjc=wjc, z1=z1, psM=psM):
                last = None
                for tt in range(2):
                    for dc in range(DCN):
                        last = E.matmul(
                            psM[par][tt][:],
                            wjc[par][:, 128 * dc:128 * (dc + 1)],
                            z1[:, dc * TAU + tt * 512:dc * TAU + (tt + 1) * 512],
                            start=(dc == 0), stop=(dc == DCN - 1))
                return last
            ev_mm[ji] = T_(fn_mm, after=[ev_w, ev_free.get(ji - 2)] + bz,
                           tag=f"qkv{ta}_{ji}")

            if jc < 16:
                dc_q = jc

                def fn_qe(E, par=par, jc=jc, dc_q=dc_q, ta=ta):
                    last = None
                    for tt in range(2):
                        last = E.activation(
                            qa[ta][:, dc_q * TAU + tt * 512:
                                   dc_q * TAU + (tt + 1) * 512],
                            psM[par][tt][:], AF.Identity,
                            bias=b_in_sb[:, jc:jc + 1])
                    return last
                ev_free[ji] = S_(fn_qe, after=(ev_mm[ji],), tag=f"qe{ta}{jc}")
                continue

            is_v = jc >= 32
            c = jc - (32 if is_v else 16)
            dstb = vbuf if is_v else kbuf
            ev_dst_free = ev_vfree if is_v else ev_kfree

            def fn_kv(E, par=par, jc=jc, dstb=dstb, psM=psM):
                last = None
                for tt in range(2):
                    last = E.activation(dstb[:, tt * 512:(tt + 1) * 512],
                                        psM[par][tt][:], AF.Identity,
                                        bias=b_in_sb[:, jc:jc + 1])
                return last
            ev_ev = S_(fn_kv, after=(ev_mm[ji], ev_dst_free), tag=f"kv{ta}{jc}")

            def fn_sq(E, par=par, jc=jc, psM=psM, ksqb=ksqb):
                last = None
                for tt in range(2):
                    last = E.activation(ksqb[:, tt * 512:(tt + 1) * 512],
                                        psM[par][tt][:], AF.Square,
                                        bias=b_in_sb[:, jc:jc + 1])
                return last
            ev_sq = S_(fn_sq, after=(ev_mm[ji], ev_n_acc), tag=f"kq{ta}{jc}")
            ev_free[ji] = ev_sq

            def fn_nm(E, psN=psN, ksqb=ksqb):
                last = None
                for tt in range(2):
                    last = E.matmul(psN[tt][0:1, :], ones_col_bf[:],
                                    ksqb[:, tt * 512:(tt + 1) * 512],
                                    start=True, stop=True)
                return last
            ev_nm = T_(fn_nm, after=(ev_sq, ev_const, ev_n_acc),
                       tag=f"nm{ta}{jc}")
            nrm_dst = nrmv if is_v else nrmk

            def fn_na(E, nrm_dst=nrm_dst, psN=psN):
                for tt in range(2):
                    E.scalar_tensor_tensor(
                        nrm_dst[0:1, tt * 512:(tt + 1) * 512],
                        psN[tt][0:1, :], 1.0,
                        nrm_dst[0:1, tt * 512:(tt + 1) * 512],
                        OP.mult, OP.add)
                return E.drain()
            ev_n_acc = V_(fn_na, after=(ev_nm, ev_nrm0), tag=f"na{ta}{jc}")

            if c < ROTC and DOROT:
                tp = c % 2
                if not is_v:
                    ev_tab[tp] = dma("sync", "tb", lambda E, c=c, tp=tp, ta=ta: [
                        E.dma_start(out=tabc[tp][:],
                                    in_=cos_in[c][:, ta * TAU:(ta + 1) * TAU]),
                        E.dma_start(out=tabs[tp][:],
                                    in_=sin_in[c][:, ta * TAU:(ta + 1) * TAU]),
                    ], after=(ev_tab_free[tp],), n=2, tag=f"tb{ta}{c}")

                def fn_pm(E, dstb=dstb, psP=psP):
                    last = None
                    for tt in range(2):
                        last = E.matmul(psP[tt][:], perm[:],
                                        dstb[:, tt * 512:(tt + 1) * 512],
                                        start=True, stop=True)
                    return last
                ev_pm = T_(fn_pm, after=(ev_ev, ev_mx, ev_perm_free),
                           tag=f"pm{ta}{jc}")
                rotdst = vrot if is_v else krot
                rot_free = ev_vrot_free if is_v else ev_krot_free

                def fn_rot(E, dstb=dstb, rotdst=rotdst, tp=tp, tabc=tabc, tabs=tabs, psP=psP, t1b=t1b):
                    E.tensor_mul(t1b[:], dstb[:], tabc[tp][:])
                    for tt in range(2):
                        E.tensor_mul(rotdst[:, tt * 512:(tt + 1) * 512],
                                     psP[tt][:],
                                     tabs[tp][:, tt * 512:(tt + 1) * 512])
                    return E.tensor_add(rotdst[:], rotdst[:], t1b[:])
                ev_rot = V_(fn_rot, after=(ev_pm, ev_tab[tp], rot_free,
                                           ev_t1_free), tag=f"ro{ta}{jc}")
                ev_perm_free = ev_rot
                ev_t1_free = ev_rot
                if is_v:
                    ev_tab_free[tp] = ev_rot
                src, src_ev = rotdst, ev_rot
                if is_v:
                    ev_vfree = ev_rot
                else:
                    ev_kfree = ev_rot
            else:
                src, src_ev = dstb, ev_ev

            if not is_v:
                k_src[c] = src
                k_ev[c] = src_ev
            else:
                ev_p = V_(lambda E, c=c, a=k_src[c], b=src:
                          E.tensor_mul(pbuf[:, c * TAU:(c + 1) * TAU],
                                       a[:], b[:]),
                          after=(k_ev[c], src_ev), tag=f"p{ta}{c}")
                if c < ROTC:
                    ev_krot_free = ev_p
                    ev_vrot_free = ev_p
                else:
                    ev_kfree = ev_p
                    ev_vfree = ev_p
            scale_step()

        ev_prod = V_(lambda E: E.tensor_mul(rrow[:], nrmk[:], nrmv[:]),
                     after=(ev_n_acc,), tag=f"pr{ta}")
        ev_sqr = S_(lambda E: E.activation(rrow[:], rrow[:], AF.Sqrt),
                    after=(ev_prod,), tag=f"rs{ta}")

        def fn_rr(E, nrmk=nrmk, rrow=rrow, rbf=rbf):
            E.drain()
            E.reciprocal(nrmk[:], rrow[:])
            E.drain()
            E.tensor_copy(rbf[:], nrmk[:])
            return E.drain()
        ev_rr = V_(fn_rr, after=(ev_sqr,), tag=f"rr{ta}")

        def fn_rb(E, psP=psP, rbf=rbf):
            last = None
            for tt in range(2):
                last = E.matmul(psP[tt][:], ones_row_bf[:],
                                rbf[0:1, tt * 512:(tt + 1) * 512],
                                start=True, stop=True)
            return last
        ev_rb = T_(fn_rb, after=(ev_rr, ev_perm_free, ev_const), tag=f"rb{ta}")

        def fn_rbc(E, rbc=rbc, psP=psP):
            last = None
            for tt in range(2):
                last = E.activation(rbc[:, tt * 512:(tt + 1) * 512],
                                    psP[tt][:], AF.Identity)
            return last
        ev_rbc = S_(fn_rbc, after=(ev_rb,), tag=f"rc{ta}")

        ev_as = []
        for c in range(16):
            ev_as.append(V_(lambda E, c=c, adump=adump, pbuf=pbuf, rbc=rbc:
                E.scalar_tensor_tensor(
                    adump[:], pbuf[:, c * TAU:(c + 1) * TAU], 0.0, rbc[:],
                    OP.add, OP.mult, accum_out=a_tmp[:, c:c + 1]),
                after=(ev_rbc,), tag=f"as{ta}{c}"))
        def fn_aacc(E):
            E.drain()
            E.scalar_tensor_tensor(a_acc[:], a_tmp[:], 1.0, a_acc[:],
                                   OP.mult, OP.add)
            return E.drain()
        V_(fn_aacc, after=ev_as + [ev_const], tag=f"aa{ta}")

        ev_qa_done[ta] = pl.barrier(tag=f"Ab{ta}")
        if DBG and ta == 0:
            dma("sync", "mx", lambda E, z1=z1, pbuf=pbuf: [
                E.dma_start(out=dbg["d_z1"], in_=z1[:]),
                E.dma_start(out=dbg["d_pbuf"], in_=pbuf[:]),
                E.dma_start(out=dbg["d_qa0"], in_=qa[0][:]),
                E.dma_start(out=dbg["d_aacc"], in_=a_acc[:]),
                E.dma_start(out=dbg["d_stn"].rearrange("(k p c) -> k p c", p=128, c=16)[0], in_=stn["sx"][:]),
                E.dma_start(out=dbg["d_stn"].rearrange("(k p c) -> k p c", p=128, c=16)[1], in_=stn["sq"][:]),
                E.dma_start(out=dbg["d_stn"].rearrange("(k p c) -> k p c", p=128, c=16)[2], in_=stn["mu"][:]),
                E.dma_start(out=dbg["d_stn"].rearrange("(k p c) -> k p c", p=128, c=16)[3], in_=stn["veps"][:]),
                E.dma_start(out=dbg["d_stn"].rearrange("(k p c) -> k p c", p=128, c=16)[4], in_=stn["sqr"][:]),
                E.dma_start(out=dbg["d_stn"].rearrange("(k p c) -> k p c", p=128, c=16)[5], in_=stn["i0"][:]),
                E.dma_start(out=dbg["d_stn"].rearrange("(k p c) -> k p c", p=128, c=16)[6], in_=stn["inv"][:]),
                E.dma_start(out=dbg["d_stn"].rearrange("(k p c) -> k p c", p=128, c=16)[7], in_=stn["nmi"][:]),
            ], after=ev_qa_done[ta], n=12, tag="dbgA")
            pl.barrier(tag="dbgAb")
        sm.close()
        sa.close()

    sf = ExitStack()
    ps_s1 = sf.enter_context(nc.psum_tensor("ps_s1", [1, 16], F32))
    ps_sb2 = sf.enter_context(nc.psum_tensor("ps_sb2", [128, 16], F32))
    scale_flush()

    def fn_sfr(E):
        E.reduce_sum(scol[:, 0:1], sacc_u[:], axis=AX)
        E.reduce_sum(scol[:, 1:2], sacc_g[:], axis=AX)
        E.reduce_sum(scol[:, 2:3], sacc_d[:], axis=AX)
        return E.drain()
    ev_sfr = V_(fn_sfr, after=(sst["use"][0], sst["use"][1]), tag="sfr")

    def fn_sfm(E):
        last = None
        for j in range(3):
            last = E.matmul(ps_s1[0:1, j:j + 1], scol[:, j:j + 1],
                            ones_col_f[:], start=(j == 0), stop=(j == 2))
        return last
    ev_sfm = T_(fn_sfm, after=(ev_sfr, ev_const), tag="sfm")

    def fn_sfs(E):
        E.tensor_scalar(srow[0:1, 0:3], ps_s1[0:1, 0:3], 1.0 / NW, EPS_Q,
                        OP.mult, OP.add)
        E.drain()
        E.tensor_scalar(srow[0:1, 3:6], srow[0:1, 0:3], 0.5, None, OP.mult)
        E.drain()
        E.tensor_scalar(srow[0:1, 6:9], srow[0:1, 3:6], -1.0, None,
                        OP.mult)
        return E.drain()
    ev_sfs = V_(fn_sfs, after=(ev_sfm,), tag="sfs")

    def fn_sfb(E):
        E.matmul(ps_sb2[:, 0:3], ones_row_f[:], srow[0:1, 0:3],
                 start=True, stop=False)
        E.matmul(ps_sb2[:, 3:6], ones_row_f[:], srow[0:1, 3:6],
                 start=False, stop=False)
        return E.matmul(ps_sb2[:, 6:9], ones_row_f[:], srow[0:1, 6:9],
                        start=False, stop=True)
    ev_sfb = T_(fn_sfb, after=(ev_sfs,), tag="sfb")

    def fn_sfe(E):
        E.tensor_copy(s_scale[:, 0:3], ps_sb2[:, 0:3])
        E.tensor_copy(s_thr[:, 0:3], ps_sb2[:, 3:6])
        E.tensor_copy(s_nthr[:, 0:3], ps_sb2[:, 6:9])
        return E.drain()
    ev_scale_done = V_(fn_sfe, after=(ev_sfb,), tag="sfe")

    if PH < 2:
        ev_aprime = None
    ps_at = sf.enter_context(nc.psum_tensor("ps_at", [16, 128], F32))
    ps_atb = sf.enter_context(nc.psum_tensor("ps_atb", [128, 16], F32))

    if PH < 2:
        ev_t1 = None
    ev_t1 = T_(lambda E: E.transpose(ps_at[:], a_acc[:], ident[:]),
               after=ev_qa_done[1], tag="art") if PH >= 2 else None
    if PH >= 2:
        ev_v1 = V_(lambda E: E.tensor_copy(aflat[:], ps_at[:]), after=(ev_t1,),
                   tag="arv")
        ev_g1 = dma("gpsimd", "cc", lambda E: [
            E.dma_start(out=cc_in.ap(), in_=aflat[:])], after=(ev_v1,), tag="ar1")
        if PH >= 3:
            ev_cc = G_(lambda E: E.collective_compute(
                "AllReduce", OP.add, replica_groups=[list(range(NCORE))],
                ins=[cc_in.ap()], outs=[cc_out.ap()]), after=(ev_g1,), tag="arcc")
        else:
            ev_cc = dma("gpsimd", "cc", lambda E: [
                E.dma_start(out=cc_out.ap(), in_=cc_in.ap())], after=(ev_g1,),
                tag="arcc_bypass")
        ev_g2 = dma("gpsimd", "cc", lambda E: [
            E.dma_start(out=aflat[:], in_=cc_out.ap())], after=(ev_cc,), tag="ar2")
        ev_t2 = T_(lambda E: E.transpose(ps_atb[:], aflat[:],
                                         ident[0:16, 0:16]),
                   after=(ev_g2,), tag="art2")
        ev_aprime = V_(lambda E: E.tensor_scalar(
            a_prime[:], ps_atb[:], 0.5, 0.5, OP.mult, OP.add),
            after=(ev_t2,), tag="arp")

    if DBG and PH >= 2:
        dma("sync", "mx", lambda E: [
            E.dma_start(out=dbg["d_aprime"], in_=a_prime[:]),
            E.dma_start(out=dbg["d_srow"], in_=srow[0:1, :]),
            E.dma_start(out=dbg["d_sq3"].rearrange("(k p c) -> k p c", p=128, c=3)[0], in_=s_scale[:]),
            E.dma_start(out=dbg["d_sq3"].rearrange("(k p c) -> k p c", p=128, c=3)[1], in_=s_thr[:]),
            E.dma_start(out=dbg["d_sq3"].rearrange("(k p c) -> k p c", p=128, c=3)[2], in_=s_nthr[:]),
        ], after=(ev_aprime, ev_scale_done), n=5, tag="dbgAR")
    bar_ar = pl.barrier(tag="arb")
    sf.close()
    scale_es.close()
    down_acc = es.enter_context(
        nc.sbuf_tensor("down_acc", [128, DCN * TAU], F32, side="right"))

    ev_prev_store = [None, None]
    for tau in range(NTAU if PH >= 4 else 0):
        sc = ExitStack()

        def sbc(name, shape, dt):
            return sc.enter_context(nc.sbuf_tensor(f"{name}_c{tau}", shape, dt, side="right"))
        x2b = [sbc(f"x2b{i}", [128, TAU], F32) for i in range(2)]
        wob = [sbc(f"wob{i}", [128, D], BF16) for i in range(2)]
        y2b = [sbc(f"y2b{i}", [128, TAU], F32) for i in range(2)]
        dumpb = sbc("dumpb", [128, TAU], BF16)
        dumpq = sbc("dumpq", [128, TAU], BF16)
        mubc = sbc("mubc", [128, TAU], F32)
        invbc = sbc("invbc", [128, TAU], F32)
        tmphb = sbc("tmphb", [128, TAU], F32)
        st2 = {k: sbc("st2_" + k, [1, TAU], F32)
               for k in ("sx", "sq", "veps", "i0", "t1")}
        scps = ExitStack()
        psO = [[scps.enter_context(nc.psum_tensor(f"psO{s}{t}_c{tau}", [128, 512], F32))
                for t in range(2)] for s in range(2)]
        psL2x = [scps.enter_context(nc.psum_tensor(f"psL2x{t}_c{tau}", [1, 512], F32))
                 for t in range(2)]
        psL2q = [scps.enter_context(nc.psum_tensor(f"psL2q{t}_c{tau}", [1, 512], F32))
                 for t in range(2)]

        def fn_qs(E, tau=tau):
            last = None
            for dc in range(DCN):
                last = E.tensor_scalar(
                    qa[tau][:, dc * TAU:(dc + 1) * TAU],
                    qa[tau][:, dc * TAU:(dc + 1) * TAU],
                    a_prime[:, dc:dc + 1], None, OP.mult)
            return last
        ev_qs = V_(fn_qs, after=[ev_aprime] + ev_qa_done[tau], tag=f"qs{tau}")

        ev_mmO = {}
        pend_l2 = []
        ev_freeO = {}
        ev_h = {}
        ev_dump = {}
        ev_l2mm = None
        for jc in range(DCN):
            par = jc % 2
            ev_w = dma("gpsimd", "wo", lambda E, jc=jc, par=par: [
                E.dma_start(out=wob[par][:], in_=w_out_t[jc])],
                after=(ev_mmO.get(jc - 2),), tag=f"wo{tau}_{jc}")
            ev_x2 = dma("sync", "x2", lambda E, jc=jc, par=par, tau=tau: [
                E.dma_start(out=x2b[par][:],
                            in_=xT_in[jc][:, tau * TAU:(tau + 1) * TAU])],
                after=(ev_h.get(jc - 2),), tag=f"x2{tau}_{jc}")

            def fn_mmo(E, par=par, tau=tau):
                last = None
                for tt in range(2):
                    for dc in range(DCN):
                        last = E.matmul(
                            psO[par][tt][:],
                            wob[par][:, 128 * dc:128 * (dc + 1)],
                            qa[tau][:, dc * TAU + tt * 512:
                                     dc * TAU + (tt + 1) * 512],
                            start=(dc == 0), stop=(dc == DCN - 1))
                return last
            ev_mmO[jc] = T_(fn_mmo, after=(ev_w, ev_qs,
                                           ev_freeO.get(jc - 2)),
                            tag=f"op{tau}_{jc}")

            def fn_y2(E, par=par, jc=jc):
                last = None
                for tt in range(2):
                    last = E.activation(y2b[par][:, tt * 512:(tt + 1) * 512],
                                        psO[par][tt][:], AF.Identity,
                                        bias=b_out_sb[:, jc:jc + 1])
                return last
            ev_y2 = S_(fn_y2, after=(ev_mmO[jc], ev_h.get(jc - 2)),
                       tag=f"y2{tau}_{jc}")
            ev_freeO[jc] = ev_y2

            ev_h[jc] = V_(lambda E, par=par, jc=jc: E.scalar_tensor_tensor(
                down_acc[:, jc * TAU:(jc + 1) * TAU], x2b[par][:], 2.0,
                y2b[par][:], OP.mult, OP.add),
                after=(ev_x2, ev_y2, ev_prev_store[tau]), tag=f"h{tau}_{jc}")

            if pend_l2:
                fn_p, ev_d, jp = pend_l2.pop(0)
                ev_l2mm = T_(fn_p, after=(ev_d, ev_const), tag=f"lm{tau}_{jp}")

            def fn_dmp(E, jc=jc):
                E.activation(dumpb[:], down_acc[:, jc * TAU:(jc + 1) * TAU],
                             AF.Identity)
                return E.activation(dumpq[:],
                                    down_acc[:, jc * TAU:(jc + 1) * TAU],
                                    AF.Square)
            ev_dump[jc] = S_(fn_dmp, after=(ev_h[jc], ev_l2mm),
                             tag=f"dp{tau}_{jc}")

            def fn_l2(E, jc=jc):
                last = None
                for tt in range(2):
                    E.matmul(psL2x[tt][0:1, :], ones_col_bf[:],
                             dumpb[:, tt * 512:(tt + 1) * 512],
                             start=(jc == 0), stop=(jc == DCN - 1))
                    last = E.matmul(psL2q[tt][0:1, :], ones_col_bf[:],
                                    dumpq[:, tt * 512:(tt + 1) * 512],
                                    start=(jc == 0), stop=(jc == DCN - 1))
                return last
            pend_l2.append((fn_l2, ev_dump[jc], jc))
            scale_step()

        while pend_l2:
            fn_p, ev_d, jp = pend_l2.pop(0)
            ev_l2mm = T_(fn_p, after=(ev_d, ev_const), tag=f"lm{tau}_{jp}")

        def fn_l2e(E):
            for tt in range(2):
                E.tensor_copy(st2["sx"][0:1, tt * 512:(tt + 1) * 512],
                              psL2x[tt][0:1, :])
            for tt in range(2):
                E.tensor_copy(st2["sq"][0:1, tt * 512:(tt + 1) * 512],
                              psL2q[tt][0:1, :])
            E.drain()
            E.tensor_scalar(st2["sx"][:], st2["sx"][:], 1.0 / D, None, OP.mult)
            E.tensor_scalar(st2["sq"][:], st2["sq"][:], 1.0 / D, None, OP.mult)
            E.drain()
            E.tensor_mul(st2["veps"][:], st2["sx"][:], st2["sx"][:])
            E.drain()
            E.tensor_sub(st2["veps"][:], st2["sq"][:], st2["veps"][:])
            E.drain()
            E.tensor_scalar(st2["veps"][:], st2["veps"][:], EPS_LN,
                            None, OP.add)
            return E.drain()
        ev_l2e = V_(fn_l2e, after=(ev_l2mm,), tag=f"l2e{tau}")
        ev_l2s = S_(lambda E: E.activation(st2["sq"][:], st2["veps"][:],
                                           AF.Sqrt), after=(ev_l2e,),
                    tag=f"l2s{tau}")

        def fn_l2n(E):
            E.drain()
            E.reciprocal(st2["i0"][:], st2["sq"][:])
            E.drain()
            E.tensor_mul(st2["t1"][:], st2["i0"][:], st2["i0"][:])
            E.drain()
            E.tensor_mul(st2["t1"][:], st2["t1"][:], st2["veps"][:])
            E.drain()
            E.tensor_scalar(st2["t1"][:], st2["t1"][:], -0.5, 1.5, OP.mult,
                            OP.add)
            E.drain()
            E.scalar_tensor_tensor(st2["t1"][:], st2["i0"][:], 1.0,
                                   st2["t1"][:], OP.mult, OP.mult)
            return E.drain()
        ev_l2n = V_(fn_l2n, after=(ev_l2s,), tag=f"l2n{tau}")

        bar_c1 = pl.barrier(tag=f"cb{tau}")
        scps.close()
        scps2 = ExitStack()
        psB = [scps2.enter_context(nc.psum_tensor(f"psB{t}_c{tau}", [128, 512], F32))
               for t in range(2)]
        psB2 = [scps2.enter_context(nc.psum_tensor(f"psB2{t}_c{tau}", [128, 512], F32))
                for t in range(2)]

        def fn_bc(E):
            last = None
            for tt in range(2):
                E.matmul(psB[tt][:], ones_row_f[:],
                         st2["sx"][0:1, tt * 512:(tt + 1) * 512],
                         start=True, stop=True)
                last = E.matmul(psB2[tt][:], ones_row_f[:],
                                st2["t1"][0:1, tt * 512:(tt + 1) * 512],
                                start=True, stop=True)
            return last
        ev_bc = T_(fn_bc, after=[ev_l2n] + bar_c1, tag=f"bc{tau}")

        def fn_bce(E):
            for tt in range(2):
                E.activation(mubc[:, tt * 512:(tt + 1) * 512], psB[tt][:],
                             AF.Identity)
            last = None
            for tt in range(2):
                last = E.activation(invbc[:, tt * 512:(tt + 1) * 512],
                                    psB2[tt][:], AF.Identity)
            return last
        ev_bce = S_(fn_bce, after=(ev_bc,), tag=f"be{tau}")

        ev_apply = []
        for jc in range(DCN):
            def fn_ap(E, jc=jc):
                E.tensor_sub(tmphb[:], down_acc[:, jc * TAU:(jc + 1) * TAU],
                             mubc[:])
                return E.tensor_mul(down_acc[:, jc * TAU:(jc + 1) * TAU],
                                    tmphb[:], invbc[:])
            ev_apply.append(V_(fn_ap, after=(ev_bce,), tag=f"ap{tau}_{jc}"))

        bar_c2 = pl.barrier(tag=f"cb2{tau}")
        if DBG and tau == 0:
            dma("sync", "mx", lambda E: [
                E.dma_start(out=dbg["d_h1"], in_=down_acc[:])],
                after=bar_c2, tag="dbgC")
            bar_c2 = pl.barrier(tag="dbgCb")
        scps2.close()
        sc.close()
        qa_es[tau].close()

        if not DODOWN:
            continue
        sd_ = ExitStack()

        def sbd(name, shape, dt):
            return sd_.enter_context(nc.sbuf_tensor(f"{name}_d{tau}", shape, dt, side="right"))
        h1T = sbd("h1T", [128, DCN * TAU], BF16)
        hid = sbd("hid", [128, 8 * TAU], BF16)
        wuf = sbd("wuf", [128, D], F32)
        wgf = sbd("wgf", [128, D], F32)
        wdf = [sbd(f"wdf{i}", [128, 1024], F32) for i in range(2)]
        quq = [sbd(f"quq{i}", [128, D], BF16) for i in range(2)]
        qgq = [sbd(f"qgq{i}", [128, D], BF16) for i in range(2)]
        qdq = [sbd(f"qdq{i}", [128, 1024], BF16) for i in range(2)]
        mpos = sbd("mpos", [128, D], U8)
        mneg = sbd("mneg", [128, D], U8)
        silub = sbd("silub", [128, TAU], BF16)
        outb = sbd("outb", [128, D], F32)
        sdps = ExitStack()
        psU = [sdps.enter_context(nc.psum_tensor(f"psU{t}_d{tau}", [128, 512], F32))
               for t in range(2)]
        psG = [sdps.enter_context(nc.psum_tensor(f"psG{t}_d{tau}", [128, 512], F32))
               for t in range(2)]
        psDn = [sdps.enter_context(nc.psum_tensor(f"psD{t}_d{tau}", [128, 512], F32))
                for t in range(4)]

        def fn_h1t(E):
            last = None
            for dc in range(DCN):
                last = E.tensor_copy(h1T[:, dc * TAU:(dc + 1) * TAU],
                                     down_acc[:, dc * TAU:(dc + 1) * TAU])
            return last
        ev_h1t = V_(fn_h1t, after=ev_apply + bar_c2, tag=f"h1t{tau}")

        ev_wu_dma = None
        ev_q_ug = {}
        ev_mmU = {}
        ev_silu = {}
        ev_hid = {}
        ev_dn_ev = {}
        ev_dn_mm = {}

        def emit_ug_load_quant(ec):
            ev_d = dma("sync", "wu", lambda E, ec=ec: [
                E.dma_start(out=wuf[:], in_=wug_t[ec][:, 0:D]),
                E.dma_start(out=wgf[:], in_=wug_t[ec][:, D:2 * D])],
                after=(ev_q_ug.get(ec - 1), ev_scale_done), n=2,
                tag=f"wu{tau}_{ec}")

            def fn_qug(E, ec=ec):
                qp = ec % 2
                E.tensor_scalar(mpos[:], wuf[:], s_thr[:, 0:1], None,
                                OP.is_gt)
                E.tensor_scalar(mneg[:], wuf[:], s_nthr[:, 0:1], None,
                                OP.is_lt)
                E.tensor_sub(quq[qp][:], mpos[:], mneg[:])
                E.tensor_scalar(mpos[:], wgf[:], s_thr[:, 1:2], None,
                                OP.is_gt)
                E.tensor_scalar(mneg[:], wgf[:], s_nthr[:, 1:2], None,
                                OP.is_lt)
                return E.tensor_sub(qgq[qp][:], mpos[:], mneg[:])
            assert ec < 2 or (ec - 2) in ev_mmU
            ev_q_ug[ec] = V_(fn_qug, after=(ev_d, ev_scale_done,
                                            ev_mmU.get(ec - 2)),
                             tag=f"qq{tau}_{ec}")

        emit_ug_load_quant(0)
        for s in range(8):
            for ei in range(8):
                ec = 8 * s + ei
                if ec + 1 < ECN:
                    emit_ug_load_quant(ec + 1)

                def fn_mmu(E, ec=ec, tau=tau):
                    qp = ec % 2
                    last = None
                    for tt in range(2):
                        for dc in range(DCN):
                            E.matmul(psU[tt][:],
                                     quq[qp][:, 128 * dc:128 * (dc + 1)],
                                     h1T[:, dc * TAU + tt * 512:
                                         dc * TAU + (tt + 1) * 512],
                                     start=(dc == 0), stop=(dc == DCN - 1))
                        for dc in range(DCN):
                            last = E.matmul(
                                psG[tt][:],
                                qgq[qp][:, 128 * dc:128 * (dc + 1)],
                                h1T[:, dc * TAU + tt * 512:
                                    dc * TAU + (tt + 1) * 512],
                                start=(dc == 0), stop=(dc == DCN - 1))
                    return last
                ev_mmU[ec] = T_(fn_mmu, after=(ev_q_ug[ec], ev_h1t,
                                               ev_hid.get(ec - 1),
                                               ev_dn_mm.get(s - 1)),
                                tag=f"ug{tau}_{ec}")

                def fn_silu(E, ec=ec):
                    last = None
                    for tt in range(2):
                        last = E.activation(silub[:, tt * 512:(tt + 1) * 512],
                                            psU[tt][:], AF.Silu,
                                            scale=s_scale[:, 0:1])
                    return last
                ev_silu[ec] = S_(fn_silu, after=(ev_mmU[ec], ev_scale_done,
                                                 ev_dn_ev.get((s - 1, "sil"))),
                                 tag=f"si{tau}_{ec}")

                def fn_hid(E, ei=ei):
                    last = None
                    for tt in range(2):
                        last = E.scalar_tensor_tensor(
                            hid[:, ei * TAU + tt * 512:ei * TAU + (tt + 1) * 512],
                            psG[tt][:], s_scale[:, 1:2],
                            silub[:, tt * 512:(tt + 1) * 512],
                            OP.mult, OP.mult)
                    return last
                ev_hid[ec] = V_(fn_hid, after=(ev_silu[ec],), tag=f"hi{tau}_{ec}")

            def emit_wd_load_quant(s2, dc2):
                dp = dc2 % 2
                prev_rd = ev_q_ug.get(("d", s2, dc2 - 2))
                if prev_rd is None:
                    prev_rd = ev_q_ug.get(("d", s2 - 1, DCN + dc2 - 2))
                ev_d = dma("sync", "wd", lambda E, s2=s2, dc2=dc2, dp=dp: [
                    E.dma_start(out=wdf[dp][:], in_=wd_t[s2 * 16 + dc2])],
                    after=(prev_rd, ev_scale_done),
                    tag=f"wd{tau}_{s2}_{dc2}")

                def fn_qd(E, dp=dp):
                    E.tensor_scalar(mpos[:, 0:1024], wdf[dp][:],
                                    s_thr[:, 2:3], None, OP.is_gt)
                    E.tensor_scalar(mneg[:, 0:1024], wdf[dp][:],
                                    s_nthr[:, 2:3], None, OP.is_lt)
                    return E.tensor_sub(qdq[dp][:], mpos[:, 0:1024],
                                        mneg[:, 0:1024])
                ev_q_ug[("d", s2, dc2)] = V_(
                    fn_qd, after=(ev_d, ev_scale_done,
                                  ev_dn_mm.get((s2, dc2 - 2)),
                                  ev_dn_mm.get((s2 - 1, DCN - 2 + dc2))),
                    tag=f"qd{tau}_{s2}_{dc2}")

            if s == 0:
                emit_wd_load_quant(0, 0)
            ev_dn_ev_last = ev_dn_ev.get((s - 1, DCN - 1))
            for dc in range(DCN if DODOWN >= 2 else 0):
                dpar = dc % 2
                if dc + 1 < DCN:
                    emit_wd_load_quant(s, dc + 1)
                elif s + 1 < 8:
                    emit_wd_load_quant(s + 1, 0)

                def fn_dmm(E, s=s, dc=dc, dpar=dpar):
                    last = None
                    for tt in range(2):
                        for ei in range(8):
                            last = E.matmul(
                                psDn[2 * dpar + tt][:],
                                qdq[dpar][:, 128 * ei:128 * (ei + 1)],
                                hid[:, ei * TAU + tt * 512:
                                    ei * TAU + (tt + 1) * 512],
                                start=(ei == 0), stop=(ei == 7))
                    return last
                prev_same_par = ev_dn_ev.get((s, dc - 2))
                if dc < 2:
                    prev_same_par = ev_dn_ev.get((s - 1, DCN - 2 + dc))
                ev_dn_mm[(s, dc)] = T_(
                    fn_dmm, after=(ev_q_ug[("d", s, dc)], ev_hid[8 * s + 7],
                                   prev_same_par, ev_dn_ev_last,
                                   ev_mmU.get(8 * s + 7)),
                    tag=f"dn{tau}_{s}_{dc}")
                ev_dn_ev_last = None

                def fn_dev(E, dc=dc, dpar=dpar):
                    last = None
                    for tt in range(2):
                        sl = slice(dc * TAU + tt * 512,
                                   dc * TAU + (tt + 1) * 512)
                        E.tensor_scalar(silub[:, tt * 512:(tt + 1) * 512],
                                        psDn[2 * dpar + tt][:],
                                        s_scale[:, 2:3], None, OP.mult)
                        last = E.scalar_tensor_tensor(
                            down_acc[:, sl],
                            silub[:, tt * 512:(tt + 1) * 512], 1.0,
                            down_acc[:, sl], OP.mult, OP.add)
                    return last
                ev_dn_ev[(s, dc)] = V_(fn_dev, after=(ev_dn_mm[(s, dc)],),
                                       tag=f"de{tau}_{s}_{dc}")
            ev_dn_ev[(s, "sil")] = ev_dn_ev.get((s, DCN - 1))
            ev_dn_mm[s] = ev_dn_mm.get((s, DCN - 1))
            scale_step()

        bar_d1 = pl.barrier(tag=f"db{tau}")
        if DBG and tau == 0:
            dma("sync", "mx", lambda E, qdq=qdq, silub=silub: [
                E.dma_start(out=dbg["d_h2"], in_=down_acc[:]),
                E.dma_start(out=dbg["d_hid"], in_=hid[:]),
                E.dma_start(out=dbg["d_qd"], in_=qdq[1][:]),
                E.dma_start(out=dbg["d_dn"], in_=silub[:])],
                after=bar_d1, n=4, tag="dbgD")
            bar_d1 = pl.barrier(tag="dbgDb")
        sdps.close()
        sdps2 = ExitStack()
        psL3 = [sdps2.enter_context(nc.psum_tensor(f"psL3{i}_d{tau}", [128, 2048], F32))
                for i in range(2)]

        ev_l3 = {}
        ev_store = None
        ev_outb_free = None
        ev_gate = {}
        psx = sdps2.enter_context(nc.sbuf_tensor(f"l3px_d{tau}", [128, 8], F32,
                                                 side="right"))
        psq = sdps2.enter_context(nc.sbuf_tensor(f"l3pq_d{tau}", [128, 8], F32,
                                                 side="right"))
        for tc in range(8 if DODOWN >= 3 else 0):
            lpar = tc % 2
            ev_tb = {}
            for b in range(4):
                def fn_l3t(E, tc=tc, lpar=lpar, b=b, tau=tau):
                    last = None
                    for j in range(4):
                        dc = 4 * b + j
                        last = E.matmul(
                            psL3[lpar][:, 128 * dc:128 * (dc + 1)],
                            down_acc[:, dc * TAU + tc * 128:
                                     dc * TAU + (tc + 1) * 128],
                            ident[:], start=True, stop=True)
                    return last
                ev_tb[b] = T_(fn_l3t,
                              after=[ev_gate.get((tc, b - 2)),
                                     ev_gate.get((tc - 1, b + 2)),
                                     ev_l3.get(tc - 2), ev_mx] + bar_d1,
                              tag=f"l3t{tau}_{tc}_{b}")
                ev_rx = V_(lambda E, lpar=lpar, b=b: E.reduce_sum(
                    psx[:, b + 4 * lpar:b + 4 * lpar + 1],
                    psL3[lpar][:, 512 * b:512 * (b + 1)], axis=AX),
                    after=(ev_tb[b],), tag=f"l3x{tau}_{tc}_{b}")
                ev_rq = S_(lambda E, lpar=lpar, b=b: E.activation(
                    outb[:, 512 * b:512 * (b + 1)],
                    psL3[lpar][:, 512 * b:512 * (b + 1)], AF.Square,
                    accum_out=psq[:, b + 4 * lpar:b + 4 * lpar + 1]),
                    after=(ev_tb[b], ev_rx, ev_outb_free),
                    tag=f"l3q{tau}_{tc}_{b}")
                ev_gate[(tc, b)] = ev_rq

            def fn_l3v(E, tc=tc, lpar=lpar):
                E.drain()
                E.reduce_sum(stn["sx"][:, tc:tc + 1],
                             psx[:, 4 * lpar:4 * lpar + 4], axis=AX)
                E.reduce_sum(stn["sq"][:, tc:tc + 1],
                             psq[:, 4 * lpar:4 * lpar + 4], axis=AX)
                E.drain()
                E.tensor_scalar(stn["mu"][:, tc:tc + 1], stn["sx"][:, tc:tc + 1],
                                1.0 / D, None, OP.mult)
                E.tensor_scalar(stn["ex2"][:, tc:tc + 1], stn["sq"][:, tc:tc + 1],
                                1.0 / D, None, OP.mult)
                E.drain()
                E.tensor_mul(stn["veps"][:, tc:tc + 1], stn["mu"][:, tc:tc + 1],
                             stn["mu"][:, tc:tc + 1])
                E.drain()
                E.tensor_sub(stn["veps"][:, tc:tc + 1], stn["ex2"][:, tc:tc + 1],
                             stn["veps"][:, tc:tc + 1])
                E.drain()
                E.tensor_scalar(stn["veps"][:, tc:tc + 1],
                                stn["veps"][:, tc:tc + 1], EPS_LN,
                                None, OP.add)
                return E.drain()
            ev_l3v = V_(fn_l3v, after=[ev_gate[(tc, b)] for b in range(4)],
                        tag=f"l3v{tau}_{tc}")
            ev_l3s = S_(lambda E, tc=tc: E.activation(
                stn["sqr"][:, tc:tc + 1], stn["veps"][:, tc:tc + 1], AF.Sqrt),
                after=(ev_l3v,), tag=f"l3s{tau}_{tc}")

            def fn_l3n(E, tc=tc):
                E.drain()
                E.reciprocal(stn["i0"][:, tc:tc + 1], stn["sqr"][:, tc:tc + 1])
                E.drain()
                E.tensor_mul(stn["t1"][:, tc:tc + 1], stn["i0"][:, tc:tc + 1],
                             stn["i0"][:, tc:tc + 1])
                E.drain()
                E.tensor_mul(stn["t1"][:, tc:tc + 1], stn["t1"][:, tc:tc + 1],
                             stn["veps"][:, tc:tc + 1])
                E.drain()
                E.tensor_scalar(stn["t1"][:, tc:tc + 1], stn["t1"][:, tc:tc + 1],
                                -0.5, 1.5, OP.mult, OP.add)
                E.drain()
                E.tensor_mul(stn["inv"][:, tc:tc + 1], stn["i0"][:, tc:tc + 1],
                             stn["t1"][:, tc:tc + 1])
                E.drain()
                E.tensor_mul(stn["nmi"][:, tc:tc + 1], stn["mu"][:, tc:tc + 1],
                             stn["inv"][:, tc:tc + 1])
                E.drain()
                E.tensor_scalar(stn["nmi"][:, tc:tc + 1],
                                stn["nmi"][:, tc:tc + 1], -1.0, None,
                                OP.mult)
                return E.drain()
            ev_l3n = V_(fn_l3n, after=(ev_l3s,), tag=f"l3n{tau}_{tc}")

            ev_ap3 = S_(lambda E, lpar=lpar, tc=tc: E.activation(
                outb[:], psL3[lpar][:], AF.Identity,
                scale=stn["inv"][:, tc:tc + 1], bias=stn["nmi"][:, tc:tc + 1]),
                after=(ev_l3n,), tag=f"l3a{tau}_{tc}")
            ev_l3[tc] = ev_ap3
            gi = tau * 8 + tc
            ev_store = dma("scalar", "ou", lambda E, gi=gi: [
                E.dma_start(out=y_out[gi], in_=outb[:])],
                after=(ev_ap3,), tag=f"st{tau}_{tc}")
            ev_outb_free = ev_store
        ev_prev_store[(tau + 1) % 2] = ev_store

        bar_d2 = pl.barrier(tag=f"db2{tau}")
        sdps2.close()
        sd_.close()

    def fn_fin(E):
        return E.nop()
    n_ou_0 = pl.counts[SD["ou"][0]]
    n_ou_1 = pl.counts[SD["ou"][1]]
    if n_ou_0 or n_ou_1:
        pl.op("sync", fn_fin,
              after=((SD["ou"][0], n_ou_0), (SD["ou"][1], n_ou_1)), tag="fin")
    else:
        pl.barrier(tag="fin")

    if PH < 4:
        qa_es[0].close()
        qa_es[1].close()

    pl.check()

    semh = {}
    for name in pl.sems:
        semh[name] = es.enter_context(nc.semaphore(name))
    block = es.enter_context(nc.Block())

    def make_runner(eng):
        def run(E):
            for waits, fn, sem, amt, tag, is_dma in pl.ops[eng]:
                for s, v in waits:
                    E.wait_ge(semh[s], v)
                ret = fn(E)
                if sem is None:
                    continue
                if is_dma:
                    assert isinstance(ret, list) and ret, tag
                    for inst in ret:
                        inst.then_inc(semh[sem], 16)
                else:
                    if ret is None:
                        ret = E.nop()
                    ret.then_inc(semh[sem], amt)
        return run

    block.tensor(make_runner("tensor"))
    block.vector(make_runner("vector"))
    block.scalar(make_runner("scalar"))
    block.sync(make_runner("sync"))
    block.gpsimd(make_runner("gpsimd"))

    es.close()
    return nc


def _prep_host(inputs):
    X = np.asarray(inputs["L0_attn_X"], dtype=np.float32).reshape(S, D)
    W_in = np.ascontiguousarray(np.asarray(inputs["L0_attn_W_in"], np.float32))
    b_in = np.asarray(inputs["L0_attn_b_in"], np.float32)
    W_out = np.ascontiguousarray(np.asarray(inputs["L0_attn_W_out"], np.float32))
    b_out = np.asarray(inputs["L0_attn_b_out"], np.float32)
    gam = np.asarray(inputs["L0_attn_gamma"], np.float32)
    bet = np.asarray(inputs["L0_attn_beta"], np.float32)
    W_up = np.asarray(inputs["L0_ffn_W_up"], np.float32)
    W_gate = np.asarray(inputs["L0_ffn_W_gate"], np.float32)
    W_down = np.asarray(inputs["L0_ffn_W_down"], np.float32)

    w_in_t = np.ascontiguousarray(
        W_in.reshape(DCN, 128, JCN, 128).transpose(2, 1, 0, 3)).ravel()
    w_out_t = np.ascontiguousarray(
        W_out.reshape(DCN, 128, DCN, 128).transpose(2, 1, 0, 3)).ravel()
    tu = W_up.reshape(ECN, 128, DCN, 128).transpose(0, 3, 2, 1)
    tg = W_gate.reshape(ECN, 128, DCN, 128).transpose(0, 3, 2, 1)
    wug = np.ascontiguousarray(
        np.stack([tu, tg], axis=1).transpose(0, 2, 1, 3, 4)).ravel()
    wd = np.ascontiguousarray(
        W_down.reshape(DCN, 128, 8, 8, 128).transpose(2, 0, 4, 3, 1)).ravel()

    gamma_t = np.ascontiguousarray(gam.reshape(DCN, 128).T).ravel()
    beta_t = np.ascontiguousarray(bet.reshape(DCN, 128).T).ravel()
    b_in_t = np.ascontiguousarray(b_in.reshape(JCN, 128).T).ravel()
    b_out_t = np.ascontiguousarray(b_out.reshape(DCN, 128).T).ravel()
    ident = np.eye(128, dtype=np.float32).ravel()
    permm = np.zeros((128, 128), dtype=np.float64)
    for i in range(64):
        permm[2 * i + 1, 2 * i] = -1.0
        permm[2 * i, 2 * i + 1] = 1.0
    permm = permm.astype(ml_dtypes.bfloat16).ravel()

    inv_freq = 1.0 / (10000.0 ** (np.arange(0, 1024, 2, dtype=np.float64) / 1024.0))
    per_core = []
    for c0 in range(NCORE):
        pos = np.arange(c0 * T, (c0 + 1) * T, dtype=np.float64)
        ang = pos[None, :] * inv_freq[:, None]
        cosf = np.cos(ang)
        sinf = np.sin(ang)
        cos_full = np.repeat(cosf, 2, axis=0)
        sin_full = np.repeat(sinf, 2, axis=0)
        cos_t = np.ascontiguousarray(
            cos_full.reshape(ROTC, 128, T)).astype(ml_dtypes.bfloat16).ravel()
        sin_t = np.ascontiguousarray(
            sin_full.reshape(ROTC, 128, T)).astype(ml_dtypes.bfloat16).ravel()
        xs = np.ascontiguousarray(X[c0 * T:(c0 + 1) * T, :])
        xT = np.ascontiguousarray(xs.T)
        per_core.append({
            "x": xs.ravel(), "xT": xT.ravel(),
            "w_in_t": w_in_t, "w_out_t": w_out_t, "wug_t": wug, "wd_t": wd,
            "cos_t": cos_t, "sin_t": sin_t,
            "gamma_t": gamma_t, "beta_t": beta_t, "b_in_t": b_in_t,
            "b_out_t": b_out_t, "ident": ident, "perm": permm,
        })
    return per_core


def kernel(**inputs):
    in_maps = _prep_host(inputs)
    nc = build_nc()
    res = run_bass_kernel_spmd(nc, in_maps, core_ids=list(range(NCORE)))
    out = np.concatenate(
        [res.results[c]["y"].reshape(T, D) for c in range(NCORE)], axis=0)
    return out.reshape(1, S, D).astype(np.float32)


# version 09
# speedup vs baseline: 1.0013x; 1.0013x over previous
from contextlib import ExitStack

import numpy as np
import ml_dtypes

import concourse.bass as bass
import concourse.mybir as mybir
from concourse.bass_utils import run_bass_kernel_spmd

F32 = mybir.dt.float32
BF16 = mybir.dt.bfloat16
U8 = mybir.dt.uint8
AF = mybir.ActivationFunctionType
OP = mybir.AluOpType
AX = mybir.AxisListType.X

NCORE = 8
D = 2048
S = 16384
T = S // NCORE
DCN = 16
JCN = 48
EH = 8192
ECN = 64
TAU = 1024
NTAU = T // TAU
ROTC = 8
EPS_LN = 1e-5
EPS_Q = 1e-8
NW = float(EH * D)


class Plan:
    ENGINES = ("tensor", "vector", "scalar", "sync", "gpsimd")

    def __init__(self):
        self.ops = {e: [] for e in self.ENGINES}
        self.counts = {}
        self.sems = []
        self.last_ev = {}

    def sem(self, name):
        self.sems.append(name)
        self.counts[name] = 0
        return name

    def op(self, eng, fn, after=(), sig=True, tag=""):
        waits = {}
        for ev in after:
            if ev is None:
                continue
            s, v = ev
            if s == "eng_" + eng:
                continue
            waits[s] = max(waits.get(s, 0), v)
        sem, amt, ev = None, 0, None
        if sig:
            sem = "eng_" + eng
            self.counts[sem] += 1
            amt = 1
            ev = (sem, self.counts[sem])
            self.last_ev[eng] = ev
        self.ops[eng].append((sorted(waits.items()), fn, sem, amt, tag, False))
        return ev

    def dma(self, eng, sem, fn, after=(), n=1, tag=""):
        waits = {}
        for ev in after:
            if ev is None:
                continue
            s, v = ev
            waits[s] = max(waits.get(s, 0), v)
        self.counts[sem] += 16 * n
        ev = (sem, self.counts[sem])
        self.ops[eng].append((sorted(waits.items()), fn, sem, 16 * n, tag, True))
        self.last_ev["dma_" + sem] = ev
        return ev

    def barrier(self, extra=(), tag="bar"):
        evs = [e for e in self.last_ev.values() if e is not None]
        evs += [e for e in extra if e is not None]
        out = []
        for eng in self.ENGINES:
            out.append(self.op(eng, lambda E: E.nop(), after=evs, tag=tag))
        return out

    def check(self):
        pos = {e: 0 for e in self.ENGINES}
        cnt = {s: 0 for s in self.sems}
        progress = True
        while progress:
            progress = False
            for e in self.ENGINES:
                while pos[e] < len(self.ops[e]):
                    waits, fn, sem, amt, tag, is_dma = self.ops[e][pos[e]]
                    if all(cnt[s] >= v for s, v in waits):
                        pos[e] += 1
                        progress = True
                        if sem is not None:
                            cnt[sem] += amt
                    else:
                        break
        stuck = [(e, self.ops[e][pos[e]][4], self.ops[e][pos[e]][0])
                 for e in self.ENGINES if pos[e] < len(self.ops[e])]
        if stuck:
            raise RuntimeError(f"plan deadlock: {stuck[:4]}")


def build_nc():
    import os
    PH = int(os.environ.get("KMK_PHASES", "4"))
    DODOWN = int(os.environ.get("KMK_D", "3"))
    L3 = int(os.environ.get("KMK_L3", "4"))
    DBG = int(os.environ.get("KMK_DBG", "0"))
    DOSCALE = int(os.environ.get("KMK_SCALE", "1"))
    DOROT = int(os.environ.get("KMK_ROT", "1"))
    nc = bass.Bass("TRN2", target_bir_lowering=False)
    pl = Plan()
    es = ExitStack()

    def din(name, n, dt):
        return nc.dram_tensor(name, [n], dt, kind="ExternalInput").ap()

    x_in = din("x", T * D, F32).rearrange("(n p d) -> n p d", p=128, d=D)
    xT_in = din("xT", D * T, F32).rearrange("(n p t) -> n p t", p=128, t=T)
    w_in_t = din("w_in_t", JCN * 128 * D, F32).rearrange(
        "(j p q) -> j p q", p=128, q=D)
    w_out_t = din("w_out_t", DCN * 128 * D, F32).rearrange(
        "(j p q) -> j p q", p=128, q=D)
    wug_t = din("wug_t", ECN * 128 * 2 * D, F32).rearrange(
        "(e p q) -> e p q", p=128, q=2 * D)
    wd_t = din("wd_t", EH * D, F32).rearrange(
        "(n p q) -> n p q", p=128, q=1024)
    cos_in = din("cos_t", ROTC * 128 * T, BF16).rearrange(
        "(c p t) -> c p t", p=128, t=T)
    sin_in = din("sin_t", ROTC * 128 * T, BF16).rearrange(
        "(c p t) -> c p t", p=128, t=T)
    gamma_in = din("gamma_t", 128 * DCN, F32).rearrange("(p c) -> p c", c=DCN)
    beta_in = din("beta_t", 128 * DCN, F32).rearrange("(p c) -> p c", c=DCN)
    b_in_in = din("b_in_t", 128 * JCN, F32).rearrange("(p c) -> p c", c=JCN)
    b_out_in = din("b_out_t", 128 * DCN, F32).rearrange("(p c) -> p c", c=DCN)
    ident_in = din("ident", 128 * 128, F32).rearrange("(p q) -> p q", q=128)
    perm_in = din("perm", 128 * 128, BF16).rearrange("(p q) -> p q", q=128)

    y_out = nc.dram_tensor("y", [T * D], F32, kind="ExternalOutput").ap() \
        .rearrange("(n p d) -> n p d", p=128, d=D)
    cc_in = nc.dram_tensor("cc_in", [16, 128], F32)
    cc_out = nc.dram_tensor("cc_out", [16, 128], F32)
    dbg = {}
    if DBG:
        for nm, n, dt in [("d_z1", 128 * DCN * TAU, BF16),
                          ("d_qa0", 128 * DCN * TAU, BF16),
                          ("d_pbuf", 128 * DCN * TAU, BF16),
                          ("d_aacc", 128 * DCN, F32),
                          ("d_aprime", 128 * DCN, F32),
                          ("d_h1", 128 * DCN * TAU, F32),
                          ("d_h2", 128 * DCN * TAU, F32),
                          ("d_srow", 16, F32),
                          ("d_hid", 128 * 8 * TAU, BF16), ("d_stn", 128 * 16 * 8, F32), ("d_sq3", 128 * 9, F32), ("d_qd", 128 * 1024, BF16), ("d_dn", 128 * 1024, BF16)]:
            dbg[nm] = nc.dram_tensor(nm, [n], dt, kind="ExternalOutput").ap()

    def sb(name, shape, dt):
        return es.enter_context(nc.sbuf_tensor(name, shape, dt))

    ident = sb("c_ident", [128, 128], F32)
    perm = sb("c_perm", [128, 128], BF16)
    ones_col_bf = sb("c_ones_cb", [128, 1], BF16)
    ones_row_bf = sb("c_ones_rb", [1, 128], BF16)
    ones_row_f = sb("c_ones_rf", [1, 128], F32)
    ones_col_f = sb("c_ones_cf", [128, 1], F32)
    gamma = sb("c_gamma", [128, DCN], F32)
    beta = sb("c_beta", [128, DCN], F32)
    b_in_sb = sb("c_bin", [128, JCN], F32)
    b_out_sb = sb("c_bout", [128, DCN], F32)
    a_acc = sb("a_acc", [128, DCN], F32)
    a_tmp = sb("a_tmp", [128, DCN], F32)
    a_prime = sb("a_prime", [128, DCN], F32)
    aflat = sb("aflat", [16, 128], F32)

    stn = {k: sb("st_" + k, [128, 16], F32) for k in
           ("sx", "sq", "mu", "ex2", "veps", "sqr", "i0", "t1", "inv", "nmi")}
    srow = sb("srow", [1, 16], F32)

    sacc_u = sb("sacc_u", [128, ECN], F32)
    sacc_g = sb("sacc_g", [128, ECN], F32)
    sacc_d = sb("sacc_d", [128, 64], F32)
    scol = sb("scol", [128, 3], F32)
    s_thr = sb("s_thr", [128, 3], F32)
    s_nthr = sb("s_nthr", [128, 3], F32)
    s_scale = sb("s_scale", [128, 3], F32)

    qa_es = [ExitStack() for _ in range(NTAU)]
    qa = [None, None]
    qa[1] = qa_es[1].enter_context(
        nc.sbuf_tensor("qa1", [128, DCN * TAU], BF16))
    qa[0] = qa_es[0].enter_context(
        nc.sbuf_tensor("qa0", [128, DCN * TAU], BF16))
    down_acc = None

    for e in Plan.ENGINES:
        pl.sem("eng_" + e)
    SD = {}
    for name, k in [("mx", 1), ("x", 2), ("tb", 2), ("wi", 2), ("sc", 2),
                    ("x2", 2), ("wo", 2), ("wu", 2), ("wd", 2), ("ou", 2),
                    ("cc", 1)]:
        SD[name] = [pl.sem(f"dma_{name}{i}") for i in range(k)]
    dstream_idx = {k: 0 for k in SD}

    def dma(eng, stream, fn, after=(), n=1, tag=""):
        i = dstream_idx[stream]
        dstream_idx[stream] += 1
        sem = SD[stream][i % len(SD[stream])]
        return pl.dma(eng, sem, fn, after=after, n=n, tag=tag)

    def T_(fn, after=(), tag="t"):
        return pl.op("tensor", fn, after, tag=tag)

    def V_(fn, after=(), tag="v"):
        return pl.op("vector", fn, after, tag=tag)

    def S_(fn, after=(), tag="s"):
        return pl.op("scalar", fn, after, tag=tag)

    def G_(fn, after=(), tag="g"):
        return pl.op("gpsimd", fn, after, tag=tag)

    ev_mx = dma("sync", "mx", lambda E: [
        E.dma_start(out=ident[:], in_=ident_in),
        E.dma_start(out=perm[:], in_=perm_in),
        E.dma_start(out=gamma[:], in_=gamma_in),
        E.dma_start(out=beta[:], in_=beta_in),
        E.dma_start(out=b_in_sb[:], in_=b_in_in),
        E.dma_start(out=b_out_sb[:], in_=b_out_in),
    ], n=6, tag="mx")

    def fn_consts(E):
        E.memset(ones_col_bf[:], 1.0)
        E.memset(ones_row_bf[:], 1.0)
        E.memset(ones_row_f[:], 1.0)
        E.memset(ones_col_f[:], 1.0)
        E.memset(a_acc[:], 0.0)
        return E.memset(sacc_d[:], 0.0)
    ev_const = V_(fn_consts, tag="consts")

    scale_es = ExitStack()
    scaleb = [scale_es.enter_context(
        nc.sbuf_tensor(f"scaleb{i}", [128, 2048], F32, side="right")) for i in range(2)]
    scale_jobs = ([("u", ec) for ec in range(ECN)]
                  + [("g", ec) for ec in range(ECN)]
                  + [("d", n2) for n2 in range(64)])
    if not DOSCALE:
        scale_jobs = scale_jobs[:2]
    sst = {"iss": 0, "con": 0, "ld": [None, None], "use": [None, None]}

    def scale_issue():
        i = sst["iss"]
        if i >= len(scale_jobs):
            return
        sst["iss"] = i + 1
        kind, idx = scale_jobs[i]
        buf = scaleb[i % 2]
        if kind == "u":
            ev = dma("scalar", "sc", lambda E, idx=idx, buf=buf: [
                E.dma_start(out=buf[:], in_=wug_t[idx][:, 0:2048])],
                after=(sst["use"][i % 2],), tag=f"scl{i}")
        elif kind == "g":
            ev = dma("scalar", "sc", lambda E, idx=idx, buf=buf: [
                E.dma_start(out=buf[:], in_=wug_t[idx][:, 2048:4096])],
                after=(sst["use"][i % 2],), tag=f"scl{i}")
        else:
            ev = dma("scalar", "sc", lambda E, idx=idx, buf=buf: [
                E.dma_start(out=buf[:, 1024 * j:1024 * (j + 1)],
                            in_=wd_t[2 * idx + j]) for j in range(2)],
                after=(sst["use"][i % 2],), n=2, tag=f"scl{i}")
        sst["ld"][i % 2] = ev

    def scale_consume():
        i = sst["con"]
        if i >= len(scale_jobs):
            return
        sst["con"] = i + 1
        kind, idx = scale_jobs[i]
        buf = scaleb[i % 2]
        ev_ld = sst["ld"][i % 2]
        dst = {"u": sacc_u, "g": sacc_g, "d": sacc_d}[kind]
        if i % 2 == 0:
            sst["use"][i % 2] = S_(lambda E, buf=buf, c=idx, dst=dst:
                                   E.activation(buf[:], buf[:], AF.Abs,
                                                accum_out=dst[:, c:c + 1]),
                                   after=(ev_ld,), tag=f"sa{i}")
        else:
            sst["use"][i % 2] = V_(lambda E, buf=buf, c=idx, dst=dst:
                                   E.reduce_sum(dst[:, c:c + 1], buf[:],
                                                axis=AX,
                                                apply_absolute_value=True),
                                   after=(ev_ld,), tag=f"sv{i}")

    def scale_step():
        while sst["iss"] - sst["con"] < 2 and sst["iss"] < len(scale_jobs):
            scale_issue()
        scale_consume()

    def scale_flush():
        while sst["con"] < len(scale_jobs):
            scale_step()

    ev_qa_done = [None, None]

    for ta in range(NTAU):
        sa = ExitStack()
        z1 = sa.enter_context(nc.sbuf_tensor(f"z1_{ta}", [128, DCN * TAU], BF16, side="right"))
        pbuf = sa.enter_context(nc.sbuf_tensor(f"pbuf_{ta}", [128, DCN * TAU], BF16, side="right"))

        sp = ExitStack()
        xb = [sp.enter_context(nc.sbuf_tensor(f"xb{i}_{ta}", [128, D], F32, side="right"))
              for i in range(2)]
        z0b = [sp.enter_context(nc.sbuf_tensor(f"z0b{i}_{ta}", [128, D], F32, side="right"))
               for i in range(2)]
        psA = [sp.enter_context(nc.psum_tensor(f"psA{i}_{ta}", [128, 256], F32))
               for i in range(2)]

        ev_z0 = [None] * 8
        prev_x_user = [ev_mx, ev_mx]
        prev_z0_user = [None, None]
        ev_evac = {}
        for g in range(4):
            tcs = [2 * g, 2 * g + 1]
            ev_x = {}
            for tc in tcs:
                gi = ta * 8 + tc
                ev_x[tc] = dma("sync", "x", lambda E, gi=gi, b=xb[tc % 2]: [
                    E.dma_start(out=b[:], in_=x_in[gi])],
                    after=(prev_x_user[tc % 2],), tag=f"x{ta}_{tc}")
            ev_st = []
            for tc in tcs:
                e1 = S_(lambda E, tc=tc, z0b=z0b, xb=xb: E.activation(
                    z0b[tc % 2][:], xb[tc % 2][:], AF.Square,
                    accum_out=stn["sq"][:, tc:tc + 1]),
                    after=(ev_x[tc], prev_z0_user[tc % 2]), tag=f"l1q{ta}{tc}")
                e2 = V_(lambda E, tc=tc, xb=xb: E.reduce_sum(
                    stn["sx"][:, tc:tc + 1], xb[tc % 2][:], axis=AX),
                    after=(ev_x[tc],), tag=f"l1x{ta}{tc}")
                ev_st += [e1, e2]
                scale_step()
            c0, c1 = 2 * g, 2 * g + 2

            def fn_v1(E, c0=c0, c1=c1):
                E.tensor_scalar(stn["mu"][:, c0:c1], stn["sx"][:, c0:c1],
                                1.0 / D, None, OP.mult)
                E.tensor_scalar(stn["ex2"][:, c0:c1], stn["sq"][:, c0:c1],
                                1.0 / D, None, OP.mult)
                E.drain()
                E.tensor_mul(stn["veps"][:, c0:c1], stn["mu"][:, c0:c1],
                             stn["mu"][:, c0:c1])
                E.drain()
                E.tensor_sub(stn["veps"][:, c0:c1], stn["ex2"][:, c0:c1],
                             stn["veps"][:, c0:c1])
                E.drain()
                E.tensor_scalar(stn["veps"][:, c0:c1],
                                stn["veps"][:, c0:c1], EPS_LN, None,
                                OP.add)
                return E.drain()
            ev_v1 = V_(fn_v1, after=ev_st, tag=f"l1v{ta}{g}")
            ev_s1 = S_(lambda E, c0=c0, c1=c1: E.activation(
                stn["sqr"][:, c0:c1], stn["veps"][:, c0:c1], AF.Sqrt),
                after=(ev_v1,), tag=f"l1s{ta}{g}")

            def fn_v2(E, c0=c0, c1=c1):
                E.drain()
                E.reciprocal(stn["i0"][:, c0:c1], stn["sqr"][:, c0:c1])
                E.drain()
                E.tensor_mul(stn["t1"][:, c0:c1], stn["i0"][:, c0:c1],
                             stn["i0"][:, c0:c1])
                E.drain()
                E.tensor_mul(stn["t1"][:, c0:c1], stn["t1"][:, c0:c1],
                             stn["veps"][:, c0:c1])
                E.drain()
                E.tensor_scalar(stn["t1"][:, c0:c1], stn["t1"][:, c0:c1],
                                -0.5, 1.5, OP.mult, OP.add)
                E.drain()
                E.tensor_mul(stn["inv"][:, c0:c1], stn["i0"][:, c0:c1],
                             stn["t1"][:, c0:c1])
                E.drain()
                E.tensor_mul(stn["nmi"][:, c0:c1], stn["mu"][:, c0:c1],
                             stn["inv"][:, c0:c1])
                E.drain()
                E.tensor_scalar(stn["nmi"][:, c0:c1],
                                stn["nmi"][:, c0:c1], -1.0, None,
                                OP.mult)
                return E.drain()
            ev_v2 = V_(fn_v2, after=(ev_s1,), tag=f"l1w{ta}{g}")
            for tc in tcs:
                ev_z0[tc] = S_(lambda E, tc=tc, z0b=z0b, xb=xb: E.activation(
                    z0b[tc % 2][:], xb[tc % 2][:], AF.Identity,
                    scale=stn["inv"][:, tc:tc + 1],
                    bias=stn["nmi"][:, tc:tc + 1]),
                    after=(ev_v2,), tag=f"z0{ta}{tc}")
                prev_x_user[tc % 2] = ev_z0[tc]
            ev_tT = {}
            for dc in range(DCN):
                par = dc % 2

                def fn_tr(E, dc=dc, par=par, tcs=tcs, psA=psA, z0b=z0b):
                    last = None
                    for i, tc in enumerate(tcs):
                        last = E.transpose(
                            psA[par][:, 128 * i:128 * (i + 1)],
                            z0b[tc % 2][:, 128 * dc:128 * (dc + 1)], ident[:])
                    return last
                ev_tT[dc] = T_(fn_tr, after=(ev_z0[tcs[0]], ev_z0[tcs[1]],
                                             ev_mx, ev_evac.get((g - 1, dc)),
                                             ev_evac.get((g, dc - 2))),
                               tag=f"tr{ta}{g}_{dc}")
                col0 = dc * TAU + g * 256
                ev_evac[(g, dc)] = S_(lambda E, par=par, dc=dc, col0=col0,
                                      z1=z1, psA=psA:
                                      E.activation(z1[:, col0:col0 + 256],
                                                   psA[par][:], AF.Identity,
                                                   scale=gamma[:, dc:dc + 1],
                                                   bias=beta[:, dc:dc + 1]),
                                      after=(ev_tT[dc],), tag=f"ze{ta}{g}_{dc}")
            prev_z0_user[0] = ev_tT[DCN - 1]
            prev_z0_user[1] = ev_tT[DCN - 1]
            scale_step()
        bz = pl.barrier(tag=f"z1b{ta}")
        sp.close()

        sm = ExitStack()

        def sbm(name, shape, dt):
            return sm.enter_context(nc.sbuf_tensor(f"{name}_{ta}", shape, dt, side="right"))
        wjc = [sbm(f"wjc{i}", [128, D], BF16) for i in range(2)]
        tabc = [sbm(f"tabc{i}", [128, TAU], BF16) for i in range(2)]
        tabs = [sbm(f"tabs{i}", [128, TAU], BF16) for i in range(2)]
        kbuf = sbm("kbuf", [128, TAU], BF16)
        krot = sbm("krot", [128, TAU], BF16)
        vbuf = sbm("vbuf", [128, TAU], BF16)
        vrot = sbm("vrot", [128, TAU], BF16)
        t1b = sbm("t1b", [128, TAU], BF16)
        ksqb = sbm("ksqb", [128, TAU], BF16)
        nrmk = sbm("nrmk", [1, TAU], F32)
        nrmv = sbm("nrmv", [1, TAU], F32)
        rrow = sbm("rrow", [1, TAU], F32)
        rbf = sbm("rbf", [1, TAU], BF16)
        rbc = sbm("rbc", [128, TAU], BF16)
        adump = sbm("adump", [128, TAU], BF16)
        psM = [[sm.enter_context(nc.psum_tensor(f"psM{s}{t}_{ta}", [128, 512], F32))
                for t in range(2)] for s in range(2)]
        psP = [sm.enter_context(nc.psum_tensor(f"psP{t}_{ta}", [128, 512], F32))
               for t in range(2)]
        psN = [sm.enter_context(nc.psum_tensor(f"psN{t}_{ta}", [1, 512], F32))
               for t in range(2)]

        ev_nrm0 = V_(lambda E: [E.memset(nrmk[:], 0.0),
                                E.memset(nrmv[:], 0.0)][-1], tag=f"n0{ta}")

        jorder = []
        for c in range(16):
            jorder += [16 + c, 32 + c]
        jorder += list(range(16))

        ev_mm = {}
        ev_free = {}
        ev_n_acc = None
        ev_perm_free = None
        ev_tab_free = [None, None]
        ev_tab = [None, None]
        ev_kfree = None
        ev_vfree = None
        ev_krot_free = None
        ev_vrot_free = None
        ev_t1_free = None
        k_src = {}
        k_ev = {}
        for ji, jc in enumerate(jorder):
            par = ji % 2
            ev_w = dma("gpsimd", "wi", lambda E, jc=jc, par=par: [
                E.dma_start(out=wjc[par][:], in_=w_in_t[jc])],
                after=(ev_mm.get(ji - 2),), tag=f"wi{ta}_{ji}")

            def fn_mm(E, par=par, wjc=wjc, z1=z1, psM=psM):
                last = None
                for tt in range(2):
                    for dc in range(DCN):
                        last = E.matmul(
                            psM[par][tt][:],
                            wjc[par][:, 128 * dc:128 * (dc + 1)],
                            z1[:, dc * TAU + tt * 512:dc * TAU + (tt + 1) * 512],
                            start=(dc == 0), stop=(dc == DCN - 1))
                return last
            ev_mm[ji] = T_(fn_mm, after=[ev_w, ev_free.get(ji - 2)] + bz,
                           tag=f"qkv{ta}_{ji}")

            if jc < 16:
                dc_q = jc

                def fn_qe(E, par=par, jc=jc, dc_q=dc_q, ta=ta):
                    last = None
                    for tt in range(2):
                        last = E.activation(
                            qa[ta][:, dc_q * TAU + tt * 512:
                                   dc_q * TAU + (tt + 1) * 512],
                            psM[par][tt][:], AF.Identity,
                            bias=b_in_sb[:, jc:jc + 1])
                    return last
                ev_free[ji] = S_(fn_qe, after=(ev_mm[ji],), tag=f"qe{ta}{jc}")
                continue

            is_v = jc >= 32
            c = jc - (32 if is_v else 16)
            dstb = vbuf if is_v else kbuf
            ev_dst_free = ev_vfree if is_v else ev_kfree

            def fn_kv(E, par=par, jc=jc, dstb=dstb, psM=psM):
                last = None
                for tt in range(2):
                    last = E.activation(dstb[:, tt * 512:(tt + 1) * 512],
                                        psM[par][tt][:], AF.Identity,
                                        bias=b_in_sb[:, jc:jc + 1])
                return last
            ev_ev = S_(fn_kv, after=(ev_mm[ji], ev_dst_free), tag=f"kv{ta}{jc}")

            def fn_sq(E, par=par, jc=jc, psM=psM, ksqb=ksqb):
                last = None
                for tt in range(2):
                    last = E.activation(ksqb[:, tt * 512:(tt + 1) * 512],
                                        psM[par][tt][:], AF.Square,
                                        bias=b_in_sb[:, jc:jc + 1])
                return last
            ev_sq = S_(fn_sq, after=(ev_mm[ji], ev_n_acc), tag=f"kq{ta}{jc}")
            ev_free[ji] = ev_sq

            def fn_nm(E, psN=psN, ksqb=ksqb):
                last = None
                for tt in range(2):
                    last = E.matmul(psN[tt][0:1, :], ones_col_bf[:],
                                    ksqb[:, tt * 512:(tt + 1) * 512],
                                    start=True, stop=True)
                return last
            ev_nm = T_(fn_nm, after=(ev_sq, ev_const, ev_n_acc),
                       tag=f"nm{ta}{jc}")
            nrm_dst = nrmv if is_v else nrmk

            def fn_na(E, nrm_dst=nrm_dst, psN=psN):
                for tt in range(2):
                    E.scalar_tensor_tensor(
                        nrm_dst[0:1, tt * 512:(tt + 1) * 512],
                        psN[tt][0:1, :], 1.0,
                        nrm_dst[0:1, tt * 512:(tt + 1) * 512],
                        OP.mult, OP.add)
                return E.drain()
            ev_n_acc = V_(fn_na, after=(ev_nm, ev_nrm0), tag=f"na{ta}{jc}")

            if c < ROTC and DOROT:
                tp = c % 2
                if not is_v:
                    ev_tab[tp] = dma("sync", "tb", lambda E, c=c, tp=tp, ta=ta: [
                        E.dma_start(out=tabc[tp][:],
                                    in_=cos_in[c][:, ta * TAU:(ta + 1) * TAU]),
                        E.dma_start(out=tabs[tp][:],
                                    in_=sin_in[c][:, ta * TAU:(ta + 1) * TAU]),
                    ], after=(ev_tab_free[tp],), n=2, tag=f"tb{ta}{c}")

                def fn_pm(E, dstb=dstb, psP=psP):
                    last = None
                    for tt in range(2):
                        last = E.matmul(psP[tt][:], perm[:],
                                        dstb[:, tt * 512:(tt + 1) * 512],
                                        start=True, stop=True)
                    return last
                ev_pm = T_(fn_pm, after=(ev_ev, ev_mx, ev_perm_free),
                           tag=f"pm{ta}{jc}")
                rotdst = vrot if is_v else krot
                rot_free = ev_vrot_free if is_v else ev_krot_free

                def fn_rot(E, dstb=dstb, rotdst=rotdst, tp=tp, tabc=tabc, tabs=tabs, psP=psP, t1b=t1b):
                    E.tensor_mul(t1b[:], dstb[:], tabc[tp][:])
                    for tt in range(2):
                        E.tensor_mul(rotdst[:, tt * 512:(tt + 1) * 512],
                                     psP[tt][:],
                                     tabs[tp][:, tt * 512:(tt + 1) * 512])
                    return E.tensor_add(rotdst[:], rotdst[:], t1b[:])
                ev_rot = V_(fn_rot, after=(ev_pm, ev_tab[tp], rot_free,
                                           ev_t1_free), tag=f"ro{ta}{jc}")
                ev_perm_free = ev_rot
                ev_t1_free = ev_rot
                if is_v:
                    ev_tab_free[tp] = ev_rot
                src, src_ev = rotdst, ev_rot
                if is_v:
                    ev_vfree = ev_rot
                else:
                    ev_kfree = ev_rot
            else:
                src, src_ev = dstb, ev_ev

            if not is_v:
                k_src[c] = src
                k_ev[c] = src_ev
            else:
                ev_p = V_(lambda E, c=c, a=k_src[c], b=src:
                          E.tensor_mul(pbuf[:, c * TAU:(c + 1) * TAU],
                                       a[:], b[:]),
                          after=(k_ev[c], src_ev), tag=f"p{ta}{c}")
                if c < ROTC:
                    ev_krot_free = ev_p
                    ev_vrot_free = ev_p
                else:
                    ev_kfree = ev_p
                    ev_vfree = ev_p
            scale_step()

        ev_prod = V_(lambda E: E.tensor_mul(rrow[:], nrmk[:], nrmv[:]),
                     after=(ev_n_acc,), tag=f"pr{ta}")
        ev_sqr = S_(lambda E: E.activation(rrow[:], rrow[:], AF.Sqrt),
                    after=(ev_prod,), tag=f"rs{ta}")

        def fn_rr(E, nrmk=nrmk, rrow=rrow, rbf=rbf):
            E.drain()
            E.reciprocal(nrmk[:], rrow[:])
            E.drain()
            E.tensor_copy(rbf[:], nrmk[:])
            return E.drain()
        ev_rr = V_(fn_rr, after=(ev_sqr,), tag=f"rr{ta}")

        def fn_rb(E, psP=psP, rbf=rbf):
            last = None
            for tt in range(2):
                last = E.matmul(psP[tt][:], ones_row_bf[:],
                                rbf[0:1, tt * 512:(tt + 1) * 512],
                                start=True, stop=True)
            return last
        ev_rb = T_(fn_rb, after=(ev_rr, ev_perm_free, ev_const), tag=f"rb{ta}")

        def fn_rbc(E, rbc=rbc, psP=psP):
            last = None
            for tt in range(2):
                last = E.activation(rbc[:, tt * 512:(tt + 1) * 512],
                                    psP[tt][:], AF.Identity)
            return last
        ev_rbc = S_(fn_rbc, after=(ev_rb,), tag=f"rc{ta}")

        ev_as = []
        for c in range(16):
            ev_as.append(V_(lambda E, c=c, adump=adump, pbuf=pbuf, rbc=rbc:
                E.scalar_tensor_tensor(
                    adump[:], pbuf[:, c * TAU:(c + 1) * TAU], 0.0, rbc[:],
                    OP.add, OP.mult, accum_out=a_tmp[:, c:c + 1]),
                after=(ev_rbc,), tag=f"as{ta}{c}"))
        def fn_aacc(E):
            E.drain()
            E.scalar_tensor_tensor(a_acc[:], a_tmp[:], 1.0, a_acc[:],
                                   OP.mult, OP.add)
            return E.drain()
        V_(fn_aacc, after=ev_as + [ev_const], tag=f"aa{ta}")

        ev_qa_done[ta] = pl.barrier(tag=f"Ab{ta}")
        if DBG and ta == 0:
            dma("sync", "mx", lambda E, z1=z1, pbuf=pbuf: [
                E.dma_start(out=dbg["d_z1"], in_=z1[:]),
                E.dma_start(out=dbg["d_pbuf"], in_=pbuf[:]),
                E.dma_start(out=dbg["d_qa0"], in_=qa[0][:]),
                E.dma_start(out=dbg["d_aacc"], in_=a_acc[:]),
                E.dma_start(out=dbg["d_stn"].rearrange("(k p c) -> k p c", p=128, c=16)[0], in_=stn["sx"][:]),
                E.dma_start(out=dbg["d_stn"].rearrange("(k p c) -> k p c", p=128, c=16)[1], in_=stn["sq"][:]),
                E.dma_start(out=dbg["d_stn"].rearrange("(k p c) -> k p c", p=128, c=16)[2], in_=stn["mu"][:]),
                E.dma_start(out=dbg["d_stn"].rearrange("(k p c) -> k p c", p=128, c=16)[3], in_=stn["veps"][:]),
                E.dma_start(out=dbg["d_stn"].rearrange("(k p c) -> k p c", p=128, c=16)[4], in_=stn["sqr"][:]),
                E.dma_start(out=dbg["d_stn"].rearrange("(k p c) -> k p c", p=128, c=16)[5], in_=stn["i0"][:]),
                E.dma_start(out=dbg["d_stn"].rearrange("(k p c) -> k p c", p=128, c=16)[6], in_=stn["inv"][:]),
                E.dma_start(out=dbg["d_stn"].rearrange("(k p c) -> k p c", p=128, c=16)[7], in_=stn["nmi"][:]),
            ], after=ev_qa_done[ta], n=12, tag="dbgA")
            pl.barrier(tag="dbgAb")
        sm.close()
        sa.close()

    sf = ExitStack()
    ps_s1 = sf.enter_context(nc.psum_tensor("ps_s1", [1, 16], F32))
    ps_sb2 = sf.enter_context(nc.psum_tensor("ps_sb2", [128, 16], F32))
    scale_flush()

    def fn_sfr(E):
        E.reduce_sum(scol[:, 0:1], sacc_u[:], axis=AX)
        E.reduce_sum(scol[:, 1:2], sacc_g[:], axis=AX)
        E.reduce_sum(scol[:, 2:3], sacc_d[:], axis=AX)
        return E.drain()
    ev_sfr = V_(fn_sfr, after=(sst["use"][0], sst["use"][1]), tag="sfr")

    def fn_sfm(E):
        last = None
        for j in range(3):
            last = E.matmul(ps_s1[0:1, j:j + 1], scol[:, j:j + 1],
                            ones_col_f[:], start=(j == 0), stop=(j == 2))
        return last
    ev_sfm = T_(fn_sfm, after=(ev_sfr, ev_const), tag="sfm")

    def fn_sfs(E):
        E.tensor_scalar(srow[0:1, 0:3], ps_s1[0:1, 0:3], 1.0 / NW, EPS_Q,
                        OP.mult, OP.add)
        E.drain()
        E.tensor_scalar(srow[0:1, 3:6], srow[0:1, 0:3], 0.5, None, OP.mult)
        E.drain()
        E.tensor_scalar(srow[0:1, 6:9], srow[0:1, 3:6], -1.0, None,
                        OP.mult)
        return E.drain()
    ev_sfs = V_(fn_sfs, after=(ev_sfm,), tag="sfs")

    def fn_sfb(E):
        E.matmul(ps_sb2[:, 0:3], ones_row_f[:], srow[0:1, 0:3],
                 start=True, stop=False)
        E.matmul(ps_sb2[:, 3:6], ones_row_f[:], srow[0:1, 3:6],
                 start=False, stop=False)
        return E.matmul(ps_sb2[:, 6:9], ones_row_f[:], srow[0:1, 6:9],
                        start=False, stop=True)
    ev_sfb = T_(fn_sfb, after=(ev_sfs,), tag="sfb")

    def fn_sfe(E):
        E.tensor_copy(s_scale[:, 0:3], ps_sb2[:, 0:3])
        E.tensor_copy(s_thr[:, 0:3], ps_sb2[:, 3:6])
        E.tensor_copy(s_nthr[:, 0:3], ps_sb2[:, 6:9])
        return E.drain()
    ev_scale_done = V_(fn_sfe, after=(ev_sfb,), tag="sfe")

    if PH < 2:
        ev_aprime = None
    ps_at = sf.enter_context(nc.psum_tensor("ps_at", [16, 128], F32))
    ps_atb = sf.enter_context(nc.psum_tensor("ps_atb", [128, 16], F32))

    if PH < 2:
        ev_t1 = None
    ev_t1 = T_(lambda E: E.transpose(ps_at[:], a_acc[:], ident[:]),
               after=ev_qa_done[1], tag="art") if PH >= 2 else None
    if PH >= 2:
        ev_v1 = V_(lambda E: E.tensor_copy(aflat[:], ps_at[:]), after=(ev_t1,),
                   tag="arv")
        ev_g1 = dma("gpsimd", "cc", lambda E: [
            E.dma_start(out=cc_in.ap(), in_=aflat[:])], after=(ev_v1,), tag="ar1")
        if PH >= 3:
            ev_cc = G_(lambda E: E.collective_compute(
                "AllReduce", OP.add, replica_groups=[list(range(NCORE))],
                ins=[cc_in.ap()], outs=[cc_out.ap()]), after=(ev_g1,), tag="arcc")
        else:
            ev_cc = dma("gpsimd", "cc", lambda E: [
                E.dma_start(out=cc_out.ap(), in_=cc_in.ap())], after=(ev_g1,),
                tag="arcc_bypass")
        ev_g2 = dma("gpsimd", "cc", lambda E: [
            E.dma_start(out=aflat[:], in_=cc_out.ap())], after=(ev_cc,), tag="ar2")
        ev_t2 = T_(lambda E: E.transpose(ps_atb[:], aflat[:],
                                         ident[0:16, 0:16]),
                   after=(ev_g2,), tag="art2")
        ev_aprime = V_(lambda E: E.tensor_scalar(
            a_prime[:], ps_atb[:], 0.5, 0.5, OP.mult, OP.add),
            after=(ev_t2,), tag="arp")

    if DBG and PH >= 2:
        dma("sync", "mx", lambda E: [
            E.dma_start(out=dbg["d_aprime"], in_=a_prime[:]),
            E.dma_start(out=dbg["d_srow"], in_=srow[0:1, :]),
            E.dma_start(out=dbg["d_sq3"].rearrange("(k p c) -> k p c", p=128, c=3)[0], in_=s_scale[:]),
            E.dma_start(out=dbg["d_sq3"].rearrange("(k p c) -> k p c", p=128, c=3)[1], in_=s_thr[:]),
            E.dma_start(out=dbg["d_sq3"].rearrange("(k p c) -> k p c", p=128, c=3)[2], in_=s_nthr[:]),
        ], after=(ev_aprime, ev_scale_done), n=5, tag="dbgAR")
    bar_ar = pl.barrier(tag="arb")
    sf.close()
    scale_es.close()
    down_acc = es.enter_context(
        nc.sbuf_tensor("down_acc", [128, DCN * TAU], F32, side="right"))

    ev_prev_store = [None, None]
    for tau in range(NTAU if PH >= 4 else 0):
        sc = ExitStack()

        def sbc(name, shape, dt):
            return sc.enter_context(nc.sbuf_tensor(f"{name}_c{tau}", shape, dt, side="right"))
        x2b = [sbc(f"x2b{i}", [128, TAU], F32) for i in range(2)]
        wob = [sbc(f"wob{i}", [128, D], BF16) for i in range(2)]
        y2b = [sbc(f"y2b{i}", [128, TAU], F32) for i in range(2)]
        dumpb = sbc("dumpb", [128, TAU], BF16)
        dumpq = sbc("dumpq", [128, TAU], BF16)
        mubc = sbc("mubc", [128, TAU], F32)
        invbc = sbc("invbc", [128, TAU], F32)
        tmphb = sbc("tmphb", [128, TAU], F32)
        st2 = {k: sbc("st2_" + k, [1, TAU], F32)
               for k in ("sx", "sq", "veps", "i0", "t1")}
        scps = ExitStack()
        psO = [[scps.enter_context(nc.psum_tensor(f"psO{s}{t}_c{tau}", [128, 512], F32))
                for t in range(2)] for s in range(2)]
        psL2x = [scps.enter_context(nc.psum_tensor(f"psL2x{t}_c{tau}", [1, 512], F32))
                 for t in range(2)]
        psL2q = [scps.enter_context(nc.psum_tensor(f"psL2q{t}_c{tau}", [1, 512], F32))
                 for t in range(2)]

        def fn_qs(E, tau=tau):
            last = None
            for dc in range(DCN):
                last = E.tensor_scalar(
                    qa[tau][:, dc * TAU:(dc + 1) * TAU],
                    qa[tau][:, dc * TAU:(dc + 1) * TAU],
                    a_prime[:, dc:dc + 1], None, OP.mult)
            return last
        ev_qs = V_(fn_qs, after=[ev_aprime] + ev_qa_done[tau], tag=f"qs{tau}")

        ev_mmO = {}
        pend_l2 = []
        ev_freeO = {}
        ev_h = {}
        ev_dump = {}
        ev_l2mm = None
        for jc in range(DCN):
            par = jc % 2
            ev_w = dma("gpsimd", "wo", lambda E, jc=jc, par=par: [
                E.dma_start(out=wob[par][:], in_=w_out_t[jc])],
                after=(ev_mmO.get(jc - 2),), tag=f"wo{tau}_{jc}")
            ev_x2 = dma("sync", "x2", lambda E, jc=jc, par=par, tau=tau: [
                E.dma_start(out=x2b[par][:],
                            in_=xT_in[jc][:, tau * TAU:(tau + 1) * TAU])],
                after=(ev_h.get(jc - 2),), tag=f"x2{tau}_{jc}")

            def fn_mmo(E, par=par, tau=tau):
                last = None
                for tt in range(2):
                    for dc in range(DCN):
                        last = E.matmul(
                            psO[par][tt][:],
                            wob[par][:, 128 * dc:128 * (dc + 1)],
                            qa[tau][:, dc * TAU + tt * 512:
                                     dc * TAU + (tt + 1) * 512],
                            start=(dc == 0), stop=(dc == DCN - 1))
                return last
            ev_mmO[jc] = T_(fn_mmo, after=(ev_w, ev_qs,
                                           ev_freeO.get(jc - 2)),
                            tag=f"op{tau}_{jc}")

            def fn_y2(E, par=par, jc=jc):
                last = None
                for tt in range(2):
                    last = E.activation(y2b[par][:, tt * 512:(tt + 1) * 512],
                                        psO[par][tt][:], AF.Identity,
                                        bias=b_out_sb[:, jc:jc + 1])
                return last
            ev_y2 = S_(fn_y2, after=(ev_mmO[jc], ev_h.get(jc - 2)),
                       tag=f"y2{tau}_{jc}")
            ev_freeO[jc] = ev_y2

            ev_h[jc] = V_(lambda E, par=par, jc=jc: E.scalar_tensor_tensor(
                down_acc[:, jc * TAU:(jc + 1) * TAU], x2b[par][:], 2.0,
                y2b[par][:], OP.mult, OP.add),
                after=(ev_x2, ev_y2, ev_prev_store[tau]), tag=f"h{tau}_{jc}")

            if pend_l2:
                fn_p, ev_d, jp = pend_l2.pop(0)
                ev_l2mm = T_(fn_p, after=(ev_d, ev_const), tag=f"lm{tau}_{jp}")

            def fn_dmp(E, jc=jc):
                E.activation(dumpb[:], down_acc[:, jc * TAU:(jc + 1) * TAU],
                             AF.Identity)
                return E.activation(dumpq[:],
                                    down_acc[:, jc * TAU:(jc + 1) * TAU],
                                    AF.Square)
            ev_dump[jc] = S_(fn_dmp, after=(ev_h[jc], ev_l2mm),
                             tag=f"dp{tau}_{jc}")

            def fn_l2(E, jc=jc):
                last = None
                for tt in range(2):
                    E.matmul(psL2x[tt][0:1, :], ones_col_bf[:],
                             dumpb[:, tt * 512:(tt + 1) * 512],
                             start=(jc == 0), stop=(jc == DCN - 1))
                    last = E.matmul(psL2q[tt][0:1, :], ones_col_bf[:],
                                    dumpq[:, tt * 512:(tt + 1) * 512],
                                    start=(jc == 0), stop=(jc == DCN - 1))
                return last
            pend_l2.append((fn_l2, ev_dump[jc], jc))
            scale_step()

        while pend_l2:
            fn_p, ev_d, jp = pend_l2.pop(0)
            ev_l2mm = T_(fn_p, after=(ev_d, ev_const), tag=f"lm{tau}_{jp}")

        def fn_l2e(E):
            for tt in range(2):
                E.tensor_copy(st2["sx"][0:1, tt * 512:(tt + 1) * 512],
                              psL2x[tt][0:1, :])
            for tt in range(2):
                E.tensor_copy(st2["sq"][0:1, tt * 512:(tt + 1) * 512],
                              psL2q[tt][0:1, :])
            E.drain()
            E.tensor_scalar(st2["sx"][:], st2["sx"][:], 1.0 / D, None, OP.mult)
            E.tensor_scalar(st2["sq"][:], st2["sq"][:], 1.0 / D, None, OP.mult)
            E.drain()
            E.tensor_mul(st2["veps"][:], st2["sx"][:], st2["sx"][:])
            E.drain()
            E.tensor_sub(st2["veps"][:], st2["sq"][:], st2["veps"][:])
            E.drain()
            E.tensor_scalar(st2["veps"][:], st2["veps"][:], EPS_LN,
                            None, OP.add)
            return E.drain()
        ev_l2e = V_(fn_l2e, after=(ev_l2mm,), tag=f"l2e{tau}")
        ev_l2s = S_(lambda E: E.activation(st2["sq"][:], st2["veps"][:],
                                           AF.Sqrt), after=(ev_l2e,),
                    tag=f"l2s{tau}")

        def fn_l2n(E):
            E.drain()
            E.reciprocal(st2["i0"][:], st2["sq"][:])
            E.drain()
            E.tensor_mul(st2["t1"][:], st2["i0"][:], st2["i0"][:])
            E.drain()
            E.tensor_mul(st2["t1"][:], st2["t1"][:], st2["veps"][:])
            E.drain()
            E.tensor_scalar(st2["t1"][:], st2["t1"][:], -0.5, 1.5, OP.mult,
                            OP.add)
            E.drain()
            E.scalar_tensor_tensor(st2["t1"][:], st2["i0"][:], 1.0,
                                   st2["t1"][:], OP.mult, OP.mult)
            return E.drain()
        ev_l2n = V_(fn_l2n, after=(ev_l2s,), tag=f"l2n{tau}")

        bar_c1 = pl.barrier(tag=f"cb{tau}")
        scps.close()
        scps2 = ExitStack()
        psB = [scps2.enter_context(nc.psum_tensor(f"psB{t}_c{tau}", [128, 512], F32))
               for t in range(2)]
        psB2 = [scps2.enter_context(nc.psum_tensor(f"psB2{t}_c{tau}", [128, 512], F32))
                for t in range(2)]

        def fn_bc(E):
            last = None
            for tt in range(2):
                E.matmul(psB[tt][:], ones_row_f[:],
                         st2["sx"][0:1, tt * 512:(tt + 1) * 512],
                         start=True, stop=True)
                last = E.matmul(psB2[tt][:], ones_row_f[:],
                                st2["t1"][0:1, tt * 512:(tt + 1) * 512],
                                start=True, stop=True)
            return last
        ev_bc = T_(fn_bc, after=[ev_l2n] + bar_c1, tag=f"bc{tau}")

        def fn_bce(E):
            for tt in range(2):
                E.activation(mubc[:, tt * 512:(tt + 1) * 512], psB[tt][:],
                             AF.Identity)
            last = None
            for tt in range(2):
                last = E.activation(invbc[:, tt * 512:(tt + 1) * 512],
                                    psB2[tt][:], AF.Identity)
            return last
        ev_bce = S_(fn_bce, after=(ev_bc,), tag=f"be{tau}")

        ev_apply = []
        for jc in range(DCN):
            def fn_ap(E, jc=jc):
                E.tensor_sub(tmphb[:], down_acc[:, jc * TAU:(jc + 1) * TAU],
                             mubc[:])
                return E.tensor_mul(down_acc[:, jc * TAU:(jc + 1) * TAU],
                                    tmphb[:], invbc[:])
            ev_apply.append(V_(fn_ap, after=(ev_bce,), tag=f"ap{tau}_{jc}"))

        bar_c2 = pl.barrier(tag=f"cb2{tau}")
        if DBG and tau == 0:
            dma("sync", "mx", lambda E: [
                E.dma_start(out=dbg["d_h1"], in_=down_acc[:])],
                after=bar_c2, tag="dbgC")
            bar_c2 = pl.barrier(tag="dbgCb")
        scps2.close()
        sc.close()
        qa_es[tau].close()

        if not DODOWN:
            continue
        sd_ = ExitStack()

        def sbd(name, shape, dt):
            return sd_.enter_context(nc.sbuf_tensor(f"{name}_d{tau}", shape, dt, side="right"))
        h1T = sbd("h1T", [128, DCN * TAU], BF16)
        hid = sbd("hid", [128, 8 * TAU], BF16)
        wuf = sbd("wuf", [128, D], F32)
        wgf = sbd("wgf", [128, D], F32)
        wdf = [sbd(f"wdf{i}", [128, 1024], F32) for i in range(2)]
        quq = [sbd(f"quq{i}", [128, D], BF16) for i in range(2)]
        qgq = [sbd(f"qgq{i}", [128, D], BF16) for i in range(2)]
        qdq = [sbd(f"qdq{i}", [128, 1024], BF16) for i in range(2)]
        mpos = sbd("mpos", [128, D], U8)
        mneg = sbd("mneg", [128, D], U8)
        silub = sbd("silub", [128, TAU], BF16)
        outb = sbd("outb", [128, D], F32)
        sdps = ExitStack()
        psU = [sdps.enter_context(nc.psum_tensor(f"psU{t}_d{tau}", [128, 512], F32))
               for t in range(2)]
        psG = [sdps.enter_context(nc.psum_tensor(f"psG{t}_d{tau}", [128, 512], F32))
               for t in range(2)]
        psDn = [sdps.enter_context(nc.psum_tensor(f"psD{t}_d{tau}", [128, 512], F32))
                for t in range(4)]

        def fn_h1t(E):
            last = None
            for dc in range(DCN):
                last = E.tensor_copy(h1T[:, dc * TAU:(dc + 1) * TAU],
                                     down_acc[:, dc * TAU:(dc + 1) * TAU])
            return last
        ev_h1t = V_(fn_h1t, after=ev_apply + bar_c2, tag=f"h1t{tau}")

        ev_wu_dma = None
        ev_q_ug = {}
        ev_mmU = {}
        ev_silu = {}
        ev_hid = {}
        ev_dn_ev = {}
        ev_dn_mm = {}

        def emit_ug_load_quant(ec):
            ev_d = dma("sync", "wu", lambda E, ec=ec: [
                E.dma_start(out=wuf[:], in_=wug_t[ec][:, 0:D]),
                E.dma_start(out=wgf[:], in_=wug_t[ec][:, D:2 * D])],
                after=(ev_q_ug.get(ec - 1), ev_scale_done), n=2,
                tag=f"wu{tau}_{ec}")

            def fn_qug(E, ec=ec):
                qp = ec % 2
                E.tensor_scalar(mpos[:], wuf[:], s_thr[:, 0:1], None,
                                OP.is_gt)
                E.tensor_scalar(mneg[:], wuf[:], s_nthr[:, 0:1], None,
                                OP.is_lt)
                E.tensor_sub(quq[qp][:], mpos[:], mneg[:])
                E.tensor_scalar(mpos[:], wgf[:], s_thr[:, 1:2], None,
                                OP.is_gt)
                E.tensor_scalar(mneg[:], wgf[:], s_nthr[:, 1:2], None,
                                OP.is_lt)
                return E.tensor_sub(qgq[qp][:], mpos[:], mneg[:])
            assert ec < 2 or (ec - 2) in ev_mmU
            ev_q_ug[ec] = V_(fn_qug, after=(ev_d, ev_scale_done,
                                            ev_mmU.get(ec - 2)),
                             tag=f"qq{tau}_{ec}")

        emit_ug_load_quant(0)
        for s in range(8):
            for ei in range(8):
                ec = 8 * s + ei
                if ec + 1 < ECN:
                    emit_ug_load_quant(ec + 1)

                def fn_mmu(E, ec=ec, tau=tau):
                    qp = ec % 2
                    last = None
                    for tt in range(2):
                        for dc in range(DCN):
                            E.matmul(psU[tt][:],
                                     quq[qp][:, 128 * dc:128 * (dc + 1)],
                                     h1T[:, dc * TAU + tt * 512:
                                         dc * TAU + (tt + 1) * 512],
                                     start=(dc == 0), stop=(dc == DCN - 1))
                        for dc in range(DCN):
                            last = E.matmul(
                                psG[tt][:],
                                qgq[qp][:, 128 * dc:128 * (dc + 1)],
                                h1T[:, dc * TAU + tt * 512:
                                    dc * TAU + (tt + 1) * 512],
                                start=(dc == 0), stop=(dc == DCN - 1))
                    return last
                ev_mmU[ec] = T_(fn_mmu, after=(ev_q_ug[ec], ev_h1t,
                                               ev_hid.get(ec - 1),
                                               ev_dn_mm.get(s - 1)),
                                tag=f"ug{tau}_{ec}")

                def fn_silu(E, ec=ec):
                    last = None
                    for tt in range(2):
                        last = E.activation(silub[:, tt * 512:(tt + 1) * 512],
                                            psU[tt][:], AF.Silu,
                                            scale=s_scale[:, 0:1])
                    return last
                ev_silu[ec] = S_(fn_silu, after=(ev_mmU[ec], ev_scale_done,
                                                 ev_dn_ev.get((s - 1, "sil"))),
                                 tag=f"si{tau}_{ec}")

                def fn_hid(E, ei=ei):
                    last = None
                    for tt in range(2):
                        last = E.scalar_tensor_tensor(
                            hid[:, ei * TAU + tt * 512:ei * TAU + (tt + 1) * 512],
                            psG[tt][:], s_scale[:, 1:2],
                            silub[:, tt * 512:(tt + 1) * 512],
                            OP.mult, OP.mult)
                    return last
                ev_hid[ec] = V_(fn_hid, after=(ev_silu[ec],), tag=f"hi{tau}_{ec}")

            def emit_wd_load_quant(s2, dc2):
                dp = dc2 % 2
                prev_rd = ev_q_ug.get(("d", s2, dc2 - 2))
                if prev_rd is None:
                    prev_rd = ev_q_ug.get(("d", s2 - 1, DCN + dc2 - 2))
                ev_d = dma("sync", "wd", lambda E, s2=s2, dc2=dc2, dp=dp: [
                    E.dma_start(out=wdf[dp][:], in_=wd_t[s2 * 16 + dc2])],
                    after=(prev_rd, ev_scale_done),
                    tag=f"wd{tau}_{s2}_{dc2}")

                def fn_qd(E, dp=dp):
                    E.tensor_scalar(mpos[:, 0:1024], wdf[dp][:],
                                    s_thr[:, 2:3], None, OP.is_gt)
                    E.tensor_scalar(mneg[:, 0:1024], wdf[dp][:],
                                    s_nthr[:, 2:3], None, OP.is_lt)
                    return E.tensor_sub(qdq[dp][:], mpos[:, 0:1024],
                                        mneg[:, 0:1024])
                ev_q_ug[("d", s2, dc2)] = V_(
                    fn_qd, after=(ev_d, ev_scale_done,
                                  ev_dn_mm.get((s2, dc2 - 2)),
                                  ev_dn_mm.get((s2 - 1, DCN - 2 + dc2))),
                    tag=f"qd{tau}_{s2}_{dc2}")

            if s == 0:
                emit_wd_load_quant(0, 0)
            ev_dn_ev_last = ev_dn_ev.get((s - 1, DCN - 1))
            for dc in range(DCN if DODOWN >= 2 else 0):
                dpar = dc % 2
                if dc + 1 < DCN:
                    emit_wd_load_quant(s, dc + 1)
                elif s + 1 < 8:
                    emit_wd_load_quant(s + 1, 0)

                def fn_dmm(E, s=s, dc=dc, dpar=dpar):
                    last = None
                    for tt in range(2):
                        for ei in range(8):
                            last = E.matmul(
                                psDn[2 * dpar + tt][:],
                                qdq[dpar][:, 128 * ei:128 * (ei + 1)],
                                hid[:, ei * TAU + tt * 512:
                                    ei * TAU + (tt + 1) * 512],
                                start=(ei == 0), stop=(ei == 7))
                    return last
                prev_same_par = ev_dn_ev.get((s, dc - 2))
                if dc < 2:
                    prev_same_par = ev_dn_ev.get((s - 1, DCN - 2 + dc))
                ev_dn_mm[(s, dc)] = T_(
                    fn_dmm, after=(ev_q_ug[("d", s, dc)], ev_hid[8 * s + 7],
                                   prev_same_par, ev_dn_ev_last,
                                   ev_mmU.get(8 * s + 7)),
                    tag=f"dn{tau}_{s}_{dc}")
                ev_dn_ev_last = None

                def fn_dev(E, dc=dc, dpar=dpar):
                    last = None
                    for tt in range(2):
                        sl = slice(dc * TAU + tt * 512,
                                   dc * TAU + (tt + 1) * 512)
                        E.tensor_scalar(silub[:, tt * 512:(tt + 1) * 512],
                                        psDn[2 * dpar + tt][:],
                                        s_scale[:, 2:3], None, OP.mult)
                        last = E.scalar_tensor_tensor(
                            down_acc[:, sl],
                            silub[:, tt * 512:(tt + 1) * 512], 1.0,
                            down_acc[:, sl], OP.mult, OP.add)
                    return last
                ev_dn_ev[(s, dc)] = V_(fn_dev, after=(ev_dn_mm[(s, dc)],),
                                       tag=f"de{tau}_{s}_{dc}")
            ev_dn_ev[(s, "sil")] = ev_dn_ev.get((s, DCN - 1))
            ev_dn_mm[s] = ev_dn_mm.get((s, DCN - 1))
            scale_step()

        bar_d1 = pl.barrier(tag=f"db{tau}")
        if DBG and tau == 0:
            dma("sync", "mx", lambda E, qdq=qdq, silub=silub: [
                E.dma_start(out=dbg["d_h2"], in_=down_acc[:]),
                E.dma_start(out=dbg["d_hid"], in_=hid[:]),
                E.dma_start(out=dbg["d_qd"], in_=qdq[1][:]),
                E.dma_start(out=dbg["d_dn"], in_=silub[:])],
                after=bar_d1, n=4, tag="dbgD")
            bar_d1 = pl.barrier(tag="dbgDb")
        sdps.close()
        sdps2 = ExitStack()
        psL3 = [sdps2.enter_context(nc.psum_tensor(f"psL3{i}_d{tau}", [128, 2048], F32))
                for i in range(2)]

        ev_l3 = {}
        ev_store = None
        ev_outb_free = None
        ev_gate = {}
        psx = sdps2.enter_context(nc.sbuf_tensor(f"l3px_d{tau}", [128, 8], F32,
                                                 side="right"))
        psq = sdps2.enter_context(nc.sbuf_tensor(f"l3pq_d{tau}", [128, 8], F32,
                                                 side="right"))
        for tc in range(8 if DODOWN >= 3 else 0):
            lpar = tc % 2
            ev_tb = {}
            for b in range(4):
                def fn_l3t(E, tc=tc, lpar=lpar, b=b, tau=tau):
                    last = None
                    for j in range(4):
                        dc = 4 * b + j
                        last = E.matmul(
                            psL3[lpar][:, 128 * dc:128 * (dc + 1)],
                            down_acc[:, dc * TAU + tc * 128:
                                     dc * TAU + (tc + 1) * 128],
                            ident[:], start=True, stop=True)
                    return last
                ev_tb[b] = T_(fn_l3t,
                              after=[ev_gate.get((tc, b - 2)),
                                     ev_gate.get((tc - 1, b + 2)),
                                     ev_l3.get(tc - 2), ev_mx] + bar_d1,
                              tag=f"l3t{tau}_{tc}_{b}")
                ev_rx = V_(lambda E, lpar=lpar, b=b: E.reduce_sum(
                    psx[:, b + 4 * lpar:b + 4 * lpar + 1],
                    psL3[lpar][:, 512 * b:512 * (b + 1)], axis=AX),
                    after=(ev_tb[b],), tag=f"l3x{tau}_{tc}_{b}")
                ev_rq = S_(lambda E, lpar=lpar, b=b: E.activation(
                    outb[:, 512 * b:512 * (b + 1)],
                    psL3[lpar][:, 512 * b:512 * (b + 1)], AF.Square,
                    accum_out=psq[:, b + 4 * lpar:b + 4 * lpar + 1]),
                    after=(ev_tb[b], ev_rx, ev_outb_free),
                    tag=f"l3q{tau}_{tc}_{b}")
                ev_gate[(tc, b)] = ev_rq

            def fn_l3v(E, tc=tc, lpar=lpar):
                E.drain()
                E.reduce_sum(stn["sx"][:, tc:tc + 1],
                             psx[:, 4 * lpar:4 * lpar + 4], axis=AX)
                E.reduce_sum(stn["sq"][:, tc:tc + 1],
                             psq[:, 4 * lpar:4 * lpar + 4], axis=AX)
                E.drain()
                E.tensor_scalar(stn["mu"][:, tc:tc + 1], stn["sx"][:, tc:tc + 1],
                                1.0 / D, None, OP.mult)
                E.tensor_scalar(stn["ex2"][:, tc:tc + 1], stn["sq"][:, tc:tc + 1],
                                1.0 / D, None, OP.mult)
                E.drain()
                E.tensor_mul(stn["veps"][:, tc:tc + 1], stn["mu"][:, tc:tc + 1],
                             stn["mu"][:, tc:tc + 1])
                E.drain()
                E.tensor_sub(stn["veps"][:, tc:tc + 1], stn["ex2"][:, tc:tc + 1],
                             stn["veps"][:, tc:tc + 1])
                E.drain()
                E.tensor_scalar(stn["veps"][:, tc:tc + 1],
                                stn["veps"][:, tc:tc + 1], EPS_LN,
                                None, OP.add)
                return E.drain()
            ev_l3v = V_(fn_l3v, after=[ev_gate[(tc, b)] for b in range(4)],
                        tag=f"l3v{tau}_{tc}")
            ev_l3s = S_(lambda E, tc=tc: E.activation(
                stn["sqr"][:, tc:tc + 1], stn["veps"][:, tc:tc + 1], AF.Sqrt),
                after=(ev_l3v,), tag=f"l3s{tau}_{tc}")

            def fn_l3n(E, tc=tc):
                E.drain()
                E.reciprocal(stn["i0"][:, tc:tc + 1], stn["sqr"][:, tc:tc + 1])
                E.drain()
                E.tensor_mul(stn["t1"][:, tc:tc + 1], stn["i0"][:, tc:tc + 1],
                             stn["i0"][:, tc:tc + 1])
                E.drain()
                E.tensor_mul(stn["t1"][:, tc:tc + 1], stn["t1"][:, tc:tc + 1],
                             stn["veps"][:, tc:tc + 1])
                E.drain()
                E.tensor_scalar(stn["t1"][:, tc:tc + 1], stn["t1"][:, tc:tc + 1],
                                -0.5, 1.5, OP.mult, OP.add)
                E.drain()
                E.tensor_mul(stn["inv"][:, tc:tc + 1], stn["i0"][:, tc:tc + 1],
                             stn["t1"][:, tc:tc + 1])
                E.drain()
                E.tensor_mul(stn["nmi"][:, tc:tc + 1], stn["mu"][:, tc:tc + 1],
                             stn["inv"][:, tc:tc + 1])
                E.drain()
                E.tensor_scalar(stn["nmi"][:, tc:tc + 1],
                                stn["nmi"][:, tc:tc + 1], -1.0, None,
                                OP.mult)
                return E.drain()
            ev_l3n = V_(fn_l3n, after=(ev_l3s,), tag=f"l3n{tau}_{tc}")

            ev_ap3 = S_(lambda E, lpar=lpar, tc=tc: E.activation(
                outb[:], psL3[lpar][:], AF.Identity,
                scale=stn["inv"][:, tc:tc + 1], bias=stn["nmi"][:, tc:tc + 1]),
                after=(ev_l3n,), tag=f"l3a{tau}_{tc}")
            ev_l3[tc] = ev_ap3
            gi = tau * 8 + tc
            ev_store = dma("scalar", "ou", lambda E, gi=gi: [
                E.dma_start(out=y_out[gi], in_=outb[:])],
                after=(ev_ap3,), tag=f"st{tau}_{tc}")
            ev_outb_free = ev_store
        ev_prev_store[(tau + 1) % 2] = ev_store

        bar_d2 = pl.barrier(tag=f"db2{tau}")
        sdps2.close()
        sd_.close()

    def fn_fin(E):
        return E.nop()
    n_ou_0 = pl.counts[SD["ou"][0]]
    n_ou_1 = pl.counts[SD["ou"][1]]
    if n_ou_0 or n_ou_1:
        pl.op("sync", fn_fin,
              after=((SD["ou"][0], n_ou_0), (SD["ou"][1], n_ou_1)), tag="fin")
    else:
        pl.barrier(tag="fin")

    if PH < 4:
        qa_es[0].close()
        qa_es[1].close()

    pl.check()

    semh = {}
    for name in pl.sems:
        semh[name] = es.enter_context(nc.semaphore(name))
    block = es.enter_context(nc.Block())

    def make_runner(eng):
        def run(E):
            for waits, fn, sem, amt, tag, is_dma in pl.ops[eng]:
                for s, v in waits:
                    E.wait_ge(semh[s], v)
                ret = fn(E)
                if sem is None:
                    continue
                if is_dma:
                    assert isinstance(ret, list) and ret, tag
                    for inst in ret:
                        inst.then_inc(semh[sem], 16)
                else:
                    if ret is None:
                        ret = E.nop()
                    ret.then_inc(semh[sem], amt)
        return run

    block.tensor(make_runner("tensor"))
    block.vector(make_runner("vector"))
    block.scalar(make_runner("scalar"))
    block.sync(make_runner("sync"))
    block.gpsimd(make_runner("gpsimd"))

    es.close()
    return nc


def _prep_host(inputs):
    X = np.asarray(inputs["L0_attn_X"], dtype=np.float32).reshape(S, D)
    W_in = np.ascontiguousarray(np.asarray(inputs["L0_attn_W_in"], np.float32))
    b_in = np.asarray(inputs["L0_attn_b_in"], np.float32)
    W_out = np.ascontiguousarray(np.asarray(inputs["L0_attn_W_out"], np.float32))
    b_out = np.asarray(inputs["L0_attn_b_out"], np.float32)
    gam = np.asarray(inputs["L0_attn_gamma"], np.float32)
    bet = np.asarray(inputs["L0_attn_beta"], np.float32)
    W_up = np.asarray(inputs["L0_ffn_W_up"], np.float32)
    W_gate = np.asarray(inputs["L0_ffn_W_gate"], np.float32)
    W_down = np.asarray(inputs["L0_ffn_W_down"], np.float32)

    w_in_t = np.ascontiguousarray(
        W_in.reshape(DCN, 128, JCN, 128).transpose(2, 1, 0, 3)).ravel()
    w_out_t = np.ascontiguousarray(
        W_out.reshape(DCN, 128, DCN, 128).transpose(2, 1, 0, 3)).ravel()
    tu = W_up.reshape(ECN, 128, DCN, 128).transpose(0, 3, 2, 1)
    tg = W_gate.reshape(ECN, 128, DCN, 128).transpose(0, 3, 2, 1)
    wug = np.ascontiguousarray(
        np.stack([tu, tg], axis=1).transpose(0, 2, 1, 3, 4)).ravel()
    wd = np.ascontiguousarray(
        W_down.reshape(DCN, 128, 8, 8, 128).transpose(2, 0, 4, 3, 1)).ravel()

    gamma_t = np.ascontiguousarray(gam.reshape(DCN, 128).T).ravel()
    beta_t = np.ascontiguousarray(bet.reshape(DCN, 128).T).ravel()
    b_in_t = np.ascontiguousarray(b_in.reshape(JCN, 128).T).ravel()
    b_out_t = np.ascontiguousarray(b_out.reshape(DCN, 128).T).ravel()
    ident = np.eye(128, dtype=np.float32).ravel()
    permm = np.zeros((128, 128), dtype=np.float64)
    for i in range(64):
        permm[2 * i + 1, 2 * i] = -1.0
        permm[2 * i, 2 * i + 1] = 1.0
    permm = permm.astype(ml_dtypes.bfloat16).ravel()

    inv_freq = 1.0 / (10000.0 ** (np.arange(0, 1024, 2, dtype=np.float64) / 1024.0))
    per_core = []
    for c0 in range(NCORE):
        pos = np.arange(c0 * T, (c0 + 1) * T, dtype=np.float64)
        ang = pos[None, :] * inv_freq[:, None]
        cosf = np.cos(ang)
        sinf = np.sin(ang)
        cos_full = np.repeat(cosf, 2, axis=0)
        sin_full = np.repeat(sinf, 2, axis=0)
        cos_t = np.ascontiguousarray(
            cos_full.reshape(ROTC, 128, T)).astype(ml_dtypes.bfloat16).ravel()
        sin_t = np.ascontiguousarray(
            sin_full.reshape(ROTC, 128, T)).astype(ml_dtypes.bfloat16).ravel()
        xs = np.ascontiguousarray(X[c0 * T:(c0 + 1) * T, :])
        xT = np.ascontiguousarray(xs.T)
        per_core.append({
            "x": xs.ravel(), "xT": xT.ravel(),
            "w_in_t": w_in_t, "w_out_t": w_out_t, "wug_t": wug, "wd_t": wd,
            "cos_t": cos_t, "sin_t": sin_t,
            "gamma_t": gamma_t, "beta_t": beta_t, "b_in_t": b_in_t,
            "b_out_t": b_out_t, "ident": ident, "perm": permm,
        })
    return per_core


def kernel(**inputs):
    in_maps = _prep_host(inputs)
    nc = build_nc()
    res = run_bass_kernel_spmd(nc, in_maps, core_ids=list(range(NCORE)))
    out = np.concatenate(
        [res.results[c]["y"].reshape(T, D) for c in range(NCORE)], axis=0)
    return out.reshape(1, S, D).astype(np.float32)


# version 10
# speedup vs baseline: 1.0035x; 1.0022x over previous
from contextlib import ExitStack

import numpy as np
import ml_dtypes

import concourse.bass as bass
import concourse.mybir as mybir
from concourse.bass_utils import run_bass_kernel_spmd

F32 = mybir.dt.float32
BF16 = mybir.dt.bfloat16
U8 = mybir.dt.uint8
AF = mybir.ActivationFunctionType
OP = mybir.AluOpType
AX = mybir.AxisListType.X

NCORE = 8
D = 2048
S = 16384
T = S // NCORE
DCN = 16
JCN = 48
EH = 8192
ECN = 64
TAU = 1024
NTAU = T // TAU
ROTC = 8
EPS_LN = 1e-5
EPS_Q = 1e-8
NW = float(EH * D)


class Plan:
    ENGINES = ("tensor", "vector", "scalar", "sync", "gpsimd")

    def __init__(self):
        self.ops = {e: [] for e in self.ENGINES}
        self.counts = {}
        self.sems = []
        self.last_ev = {}

    def sem(self, name):
        self.sems.append(name)
        self.counts[name] = 0
        return name

    def op(self, eng, fn, after=(), sig=True, tag=""):
        waits = {}
        for ev in after:
            if ev is None:
                continue
            s, v = ev
            if s == "eng_" + eng:
                continue
            waits[s] = max(waits.get(s, 0), v)
        sem, amt, ev = None, 0, None
        if sig:
            sem = "eng_" + eng
            self.counts[sem] += 1
            amt = 1
            ev = (sem, self.counts[sem])
            self.last_ev[eng] = ev
        self.ops[eng].append((sorted(waits.items()), fn, sem, amt, tag, False))
        return ev

    def dma(self, eng, sem, fn, after=(), n=1, tag=""):
        waits = {}
        for ev in after:
            if ev is None:
                continue
            s, v = ev
            waits[s] = max(waits.get(s, 0), v)
        self.counts[sem] += 16 * n
        ev = (sem, self.counts[sem])
        self.ops[eng].append((sorted(waits.items()), fn, sem, 16 * n, tag, True))
        self.last_ev["dma_" + sem] = ev
        return ev

    def barrier(self, extra=(), tag="bar"):
        evs = [e for e in self.last_ev.values() if e is not None]
        evs += [e for e in extra if e is not None]
        out = []
        for eng in self.ENGINES:
            out.append(self.op(eng, lambda E: E.nop(), after=evs, tag=tag))
        return out

    def check(self):
        pos = {e: 0 for e in self.ENGINES}
        cnt = {s: 0 for s in self.sems}
        progress = True
        while progress:
            progress = False
            for e in self.ENGINES:
                while pos[e] < len(self.ops[e]):
                    waits, fn, sem, amt, tag, is_dma = self.ops[e][pos[e]]
                    if all(cnt[s] >= v for s, v in waits):
                        pos[e] += 1
                        progress = True
                        if sem is not None:
                            cnt[sem] += amt
                    else:
                        break
        stuck = [(e, self.ops[e][pos[e]][4], self.ops[e][pos[e]][0])
                 for e in self.ENGINES if pos[e] < len(self.ops[e])]
        if stuck:
            raise RuntimeError(f"plan deadlock: {stuck[:4]}")


def build_nc():
    import os
    PH = int(os.environ.get("KMK_PHASES", "4"))
    DODOWN = int(os.environ.get("KMK_D", "3"))
    L3 = int(os.environ.get("KMK_L3", "4"))
    DBG = int(os.environ.get("KMK_DBG", "0"))
    DOSCALE = int(os.environ.get("KMK_SCALE", "1"))
    DOROT = int(os.environ.get("KMK_ROT", "1"))
    nc = bass.Bass("TRN2", target_bir_lowering=False)
    pl = Plan()
    es = ExitStack()

    def din(name, n, dt):
        return nc.dram_tensor(name, [n], dt, kind="ExternalInput").ap()

    x_in = din("x", T * D, F32).rearrange("(n p d) -> n p d", p=128, d=D)
    xT_in = din("xT", D * T, F32).rearrange("(n p t) -> n p t", p=128, t=T)
    w_in_t = din("w_in_t", JCN * 128 * D, F32).rearrange(
        "(j p q) -> j p q", p=128, q=D)
    w_out_t = din("w_out_t", DCN * 128 * D, F32).rearrange(
        "(j p q) -> j p q", p=128, q=D)
    wug_t = din("wug_t", ECN * 128 * 2 * D, F32).rearrange(
        "(e p q) -> e p q", p=128, q=2 * D)
    wd_t = din("wd_t", EH * D, F32).rearrange(
        "(n p q) -> n p q", p=128, q=1024)
    cos_in = din("cos_t", ROTC * 128 * T, BF16).rearrange(
        "(c p t) -> c p t", p=128, t=T)
    sin_in = din("sin_t", ROTC * 128 * T, BF16).rearrange(
        "(c p t) -> c p t", p=128, t=T)
    gamma_in = din("gamma_t", 128 * DCN, F32).rearrange("(p c) -> p c", c=DCN)
    beta_in = din("beta_t", 128 * DCN, F32).rearrange("(p c) -> p c", c=DCN)
    b_in_in = din("b_in_t", 128 * JCN, F32).rearrange("(p c) -> p c", c=JCN)
    b_out_in = din("b_out_t", 128 * DCN, F32).rearrange("(p c) -> p c", c=DCN)
    ident_in = din("ident", 128 * 128, F32).rearrange("(p q) -> p q", q=128)
    perm_in = din("perm", 128 * 128, BF16).rearrange("(p q) -> p q", q=128)

    y_out = nc.dram_tensor("y", [T * D], F32, kind="ExternalOutput").ap() \
        .rearrange("(n p d) -> n p d", p=128, d=D)
    cc_in = nc.dram_tensor("cc_in", [16, 128], F32)
    cc_out = nc.dram_tensor("cc_out", [16, 128], F32)
    dbg = {}
    if DBG:
        for nm, n, dt in [("d_z1", 128 * DCN * TAU, BF16),
                          ("d_qa0", 128 * DCN * TAU, BF16),
                          ("d_pbuf", 128 * DCN * TAU, BF16),
                          ("d_aacc", 128 * DCN, F32),
                          ("d_aprime", 128 * DCN, F32),
                          ("d_h1", 128 * DCN * TAU, F32),
                          ("d_h2", 128 * DCN * TAU, F32),
                          ("d_srow", 16, F32),
                          ("d_hid", 128 * 8 * TAU, BF16), ("d_stn", 128 * 16 * 8, F32), ("d_sq3", 128 * 9, F32), ("d_qd", 128 * 1024, BF16), ("d_dn", 128 * 1024, BF16)]:
            dbg[nm] = nc.dram_tensor(nm, [n], dt, kind="ExternalOutput").ap()

    def sb(name, shape, dt):
        return es.enter_context(nc.sbuf_tensor(name, shape, dt))

    ident = sb("c_ident", [128, 128], F32)
    perm = sb("c_perm", [128, 128], BF16)
    ones_col_bf = sb("c_ones_cb", [128, 1], BF16)
    ones_row_bf = sb("c_ones_rb", [1, 128], BF16)
    ones_row_f = sb("c_ones_rf", [1, 128], F32)
    ones_col_f = sb("c_ones_cf", [128, 1], F32)
    gamma = sb("c_gamma", [128, DCN], F32)
    beta = sb("c_beta", [128, DCN], F32)
    b_in_sb = sb("c_bin", [128, JCN], F32)
    b_out_sb = sb("c_bout", [128, DCN], F32)
    a_acc = sb("a_acc", [128, DCN], F32)
    a_tmp = sb("a_tmp", [128, DCN], F32)
    a_prime = sb("a_prime", [128, DCN], F32)
    aflat = sb("aflat", [16, 128], F32)

    stn = {k: sb("st_" + k, [128, 16], F32) for k in
           ("sx", "sq", "mu", "ex2", "veps", "sqr", "i0", "t1", "inv", "nmi")}
    srow = sb("srow", [1, 16], F32)

    sacc_u = sb("sacc_u", [128, ECN], F32)
    sacc_g = sb("sacc_g", [128, ECN], F32)
    sacc_d = sb("sacc_d", [128, 64], F32)
    scol = sb("scol", [128, 3], F32)
    s_thr = sb("s_thr", [128, 3], F32)
    s_nthr = sb("s_nthr", [128, 3], F32)
    s_scale = sb("s_scale", [128, 3], F32)

    qa_es = [ExitStack() for _ in range(NTAU)]
    qa = [None, None]
    qa[1] = qa_es[1].enter_context(
        nc.sbuf_tensor("qa1", [128, DCN * TAU], BF16))
    qa[0] = qa_es[0].enter_context(
        nc.sbuf_tensor("qa0", [128, DCN * TAU], BF16))
    down_acc = None

    for e in Plan.ENGINES:
        pl.sem("eng_" + e)
    SD = {}
    for name, k in [("mx", 1), ("x", 2), ("tb", 2), ("wi", 2), ("sc", 2),
                    ("x2", 2), ("wo", 2), ("wu", 2), ("wd", 2), ("ou", 2),
                    ("cc", 1)]:
        SD[name] = [pl.sem(f"dma_{name}{i}") for i in range(k)]
    dstream_idx = {k: 0 for k in SD}

    def dma(eng, stream, fn, after=(), n=1, tag=""):
        i = dstream_idx[stream]
        dstream_idx[stream] += 1
        sem = SD[stream][i % len(SD[stream])]
        return pl.dma(eng, sem, fn, after=after, n=n, tag=tag)

    def T_(fn, after=(), tag="t"):
        return pl.op("tensor", fn, after, tag=tag)

    def V_(fn, after=(), tag="v"):
        return pl.op("vector", fn, after, tag=tag)

    def S_(fn, after=(), tag="s"):
        return pl.op("scalar", fn, after, tag=tag)

    def G_(fn, after=(), tag="g"):
        return pl.op("gpsimd", fn, after, tag=tag)

    ev_mx = dma("sync", "mx", lambda E: [
        E.dma_start(out=ident[:], in_=ident_in),
        E.dma_start(out=perm[:], in_=perm_in),
        E.dma_start(out=gamma[:], in_=gamma_in),
        E.dma_start(out=beta[:], in_=beta_in),
        E.dma_start(out=b_in_sb[:], in_=b_in_in),
        E.dma_start(out=b_out_sb[:], in_=b_out_in),
    ], n=6, tag="mx")

    def fn_consts(E):
        E.memset(ones_col_bf[:], 1.0)
        E.memset(ones_row_bf[:], 1.0)
        E.memset(ones_row_f[:], 1.0)
        E.memset(ones_col_f[:], 1.0)
        E.memset(a_acc[:], 0.0)
        return E.memset(sacc_d[:], 0.0)
    ev_const = V_(fn_consts, tag="consts")

    scale_es = ExitStack()
    scaleb = [scale_es.enter_context(
        nc.sbuf_tensor(f"scaleb{i}", [128, 2048], F32, side="right")) for i in range(2)]
    scale_jobs = ([("u", ec) for ec in range(ECN)]
                  + [("g", ec) for ec in range(ECN)]
                  + [("d", n2) for n2 in range(64)])
    if not DOSCALE:
        scale_jobs = scale_jobs[:2]
    sst = {"iss": 0, "con": 0, "ld": [None, None], "use": [None, None]}

    def scale_issue():
        i = sst["iss"]
        if i >= len(scale_jobs):
            return
        sst["iss"] = i + 1
        kind, idx = scale_jobs[i]
        buf = scaleb[i % 2]
        if kind == "u":
            ev = dma("scalar", "sc", lambda E, idx=idx, buf=buf: [
                E.dma_start(out=buf[:], in_=wug_t[idx][:, 0:2048])],
                after=(sst["use"][i % 2],), tag=f"scl{i}")
        elif kind == "g":
            ev = dma("scalar", "sc", lambda E, idx=idx, buf=buf: [
                E.dma_start(out=buf[:], in_=wug_t[idx][:, 2048:4096])],
                after=(sst["use"][i % 2],), tag=f"scl{i}")
        else:
            ev = dma("scalar", "sc", lambda E, idx=idx, buf=buf: [
                E.dma_start(out=buf[:, 1024 * j:1024 * (j + 1)],
                            in_=wd_t[2 * idx + j]) for j in range(2)],
                after=(sst["use"][i % 2],), n=2, tag=f"scl{i}")
        sst["ld"][i % 2] = ev

    def scale_consume():
        i = sst["con"]
        if i >= len(scale_jobs):
            return
        sst["con"] = i + 1
        kind, idx = scale_jobs[i]
        buf = scaleb[i % 2]
        ev_ld = sst["ld"][i % 2]
        dst = {"u": sacc_u, "g": sacc_g, "d": sacc_d}[kind]
        sst["use"][i % 2] = V_(lambda E, buf=buf, c=idx, dst=dst:
                               E.reduce_sum(dst[:, c:c + 1], buf[:],
                                            axis=AX,
                                            apply_absolute_value=True),
                               after=(ev_ld,), tag=f"sv{i}")

    def scale_step():
        while sst["iss"] - sst["con"] < 2 and sst["iss"] < len(scale_jobs):
            scale_issue()
        scale_consume()

    def scale_flush():
        while sst["con"] < len(scale_jobs):
            scale_step()

    ev_qa_done = [None, None]

    for ta in range(NTAU):
        sa = ExitStack()
        z1 = sa.enter_context(nc.sbuf_tensor(f"z1_{ta}", [128, DCN * TAU], BF16, side="right"))
        pbuf = sa.enter_context(nc.sbuf_tensor(f"pbuf_{ta}", [128, DCN * TAU], BF16, side="right"))

        sp = ExitStack()
        xb = [sp.enter_context(nc.sbuf_tensor(f"xb{i}_{ta}", [128, D], F32, side="right"))
              for i in range(2)]
        z0b = [sp.enter_context(nc.sbuf_tensor(f"z0b{i}_{ta}", [128, D], F32, side="right"))
               for i in range(2)]
        psA = [sp.enter_context(nc.psum_tensor(f"psA{i}_{ta}", [128, 256], F32))
               for i in range(2)]

        ev_z0 = [None] * 8
        prev_x_user = [ev_mx, ev_mx]
        prev_z0_user = [None, None]
        ev_evac = {}
        for g in range(4):
            tcs = [2 * g, 2 * g + 1]
            ev_x = {}
            for tc in tcs:
                gi = ta * 8 + tc
                ev_x[tc] = dma("sync", "x", lambda E, gi=gi, b=xb[tc % 2]: [
                    E.dma_start(out=b[:], in_=x_in[gi])],
                    after=(prev_x_user[tc % 2],), tag=f"x{ta}_{tc}")
            ev_st = []
            for tc in tcs:
                e1 = S_(lambda E, tc=tc, z0b=z0b, xb=xb: E.activation(
                    z0b[tc % 2][:], xb[tc % 2][:], AF.Square,
                    accum_out=stn["sq"][:, tc:tc + 1]),
                    after=(ev_x[tc], prev_z0_user[tc % 2]), tag=f"l1q{ta}{tc}")
                e2 = V_(lambda E, tc=tc, xb=xb: E.reduce_sum(
                    stn["sx"][:, tc:tc + 1], xb[tc % 2][:], axis=AX),
                    after=(ev_x[tc],), tag=f"l1x{ta}{tc}")
                ev_st += [e1, e2]
                scale_step()
            c0, c1 = 2 * g, 2 * g + 2

            def fn_v1(E, c0=c0, c1=c1):
                E.tensor_scalar(stn["mu"][:, c0:c1], stn["sx"][:, c0:c1],
                                1.0 / D, None, OP.mult)
                E.tensor_scalar(stn["ex2"][:, c0:c1], stn["sq"][:, c0:c1],
                                1.0 / D, None, OP.mult)
                E.drain()
                E.tensor_mul(stn["veps"][:, c0:c1], stn["mu"][:, c0:c1],
                             stn["mu"][:, c0:c1])
                E.drain()
                E.tensor_sub(stn["veps"][:, c0:c1], stn["ex2"][:, c0:c1],
                             stn["veps"][:, c0:c1])
                E.drain()
                E.tensor_scalar(stn["veps"][:, c0:c1],
                                stn["veps"][:, c0:c1], EPS_LN, None,
                                OP.add)
                return E.drain()
            ev_v1 = V_(fn_v1, after=ev_st, tag=f"l1v{ta}{g}")
            ev_s1 = S_(lambda E, c0=c0, c1=c1: E.activation(
                stn["sqr"][:, c0:c1], stn["veps"][:, c0:c1], AF.Sqrt),
                after=(ev_v1,), tag=f"l1s{ta}{g}")

            def fn_v2(E, c0=c0, c1=c1):
                E.drain()
                E.reciprocal(stn["i0"][:, c0:c1], stn["sqr"][:, c0:c1])
                E.drain()
                E.tensor_mul(stn["t1"][:, c0:c1], stn["i0"][:, c0:c1],
                             stn["i0"][:, c0:c1])
                E.drain()
                E.tensor_mul(stn["t1"][:, c0:c1], stn["t1"][:, c0:c1],
                             stn["veps"][:, c0:c1])
                E.drain()
                E.tensor_scalar(stn["t1"][:, c0:c1], stn["t1"][:, c0:c1],
                                -0.5, 1.5, OP.mult, OP.add)
                E.drain()
                E.tensor_mul(stn["inv"][:, c0:c1], stn["i0"][:, c0:c1],
                             stn["t1"][:, c0:c1])
                E.drain()
                E.tensor_mul(stn["nmi"][:, c0:c1], stn["mu"][:, c0:c1],
                             stn["inv"][:, c0:c1])
                E.drain()
                E.tensor_scalar(stn["nmi"][:, c0:c1],
                                stn["nmi"][:, c0:c1], -1.0, None,
                                OP.mult)
                return E.drain()
            ev_v2 = V_(fn_v2, after=(ev_s1,), tag=f"l1w{ta}{g}")
            for tc in tcs:
                ev_z0[tc] = S_(lambda E, tc=tc, z0b=z0b, xb=xb: E.activation(
                    z0b[tc % 2][:], xb[tc % 2][:], AF.Identity,
                    scale=stn["inv"][:, tc:tc + 1],
                    bias=stn["nmi"][:, tc:tc + 1]),
                    after=(ev_v2,), tag=f"z0{ta}{tc}")
                prev_x_user[tc % 2] = ev_z0[tc]
            ev_tT = {}
            for dc in range(DCN):
                par = dc % 2

                def fn_tr(E, dc=dc, par=par, tcs=tcs, psA=psA, z0b=z0b):
                    last = None
                    for i, tc in enumerate(tcs):
                        last = E.transpose(
                            psA[par][:, 128 * i:128 * (i + 1)],
                            z0b[tc % 2][:, 128 * dc:128 * (dc + 1)], ident[:])
                    return last
                ev_tT[dc] = T_(fn_tr, after=(ev_z0[tcs[0]], ev_z0[tcs[1]],
                                             ev_mx, ev_evac.get((g - 1, dc)),
                                             ev_evac.get((g, dc - 2))),
                               tag=f"tr{ta}{g}_{dc}")
                col0 = dc * TAU + g * 256
                ev_evac[(g, dc)] = S_(lambda E, par=par, dc=dc, col0=col0,
                                      z1=z1, psA=psA:
                                      E.activation(z1[:, col0:col0 + 256],
                                                   psA[par][:], AF.Identity,
                                                   scale=gamma[:, dc:dc + 1],
                                                   bias=beta[:, dc:dc + 1]),
                                      after=(ev_tT[dc],), tag=f"ze{ta}{g}_{dc}")
            prev_z0_user[0] = ev_tT[DCN - 1]
            prev_z0_user[1] = ev_tT[DCN - 1]
            scale_step()
        bz = pl.barrier(tag=f"z1b{ta}")
        sp.close()

        sm = ExitStack()

        def sbm(name, shape, dt):
            return sm.enter_context(nc.sbuf_tensor(f"{name}_{ta}", shape, dt, side="right"))
        wjc = [sbm(f"wjc{i}", [128, D], BF16) for i in range(2)]
        tabc = [sbm(f"tabc{i}", [128, TAU], BF16) for i in range(2)]
        tabs = [sbm(f"tabs{i}", [128, TAU], BF16) for i in range(2)]
        kbuf = sbm("kbuf", [128, TAU], BF16)
        krot = sbm("krot", [128, TAU], BF16)
        vbuf = sbm("vbuf", [128, TAU], BF16)
        vrot = sbm("vrot", [128, TAU], BF16)
        t1b = sbm("t1b", [128, TAU], BF16)
        ksqb = sbm("ksqb", [128, TAU], BF16)
        nrmk = sbm("nrmk", [1, TAU], F32)
        nrmv = sbm("nrmv", [1, TAU], F32)
        rrow = sbm("rrow", [1, TAU], F32)
        rbf = sbm("rbf", [1, TAU], BF16)
        rbc = sbm("rbc", [128, TAU], BF16)
        adump = sbm("adump", [128, TAU], BF16)
        psM = [[sm.enter_context(nc.psum_tensor(f"psM{s}{t}_{ta}", [128, 512], F32))
                for t in range(2)] for s in range(2)]
        psP = [sm.enter_context(nc.psum_tensor(f"psP{t}_{ta}", [128, 512], F32))
               for t in range(2)]
        psN = [sm.enter_context(nc.psum_tensor(f"psN{t}_{ta}", [1, 512], F32))
               for t in range(2)]

        ev_nrm0 = V_(lambda E: [E.memset(nrmk[:], 0.0),
                                E.memset(nrmv[:], 0.0)][-1], tag=f"n0{ta}")

        jorder = []
        for c in range(16):
            jorder += [16 + c, 32 + c]
        jorder += list(range(16))

        ev_mm = {}
        ev_free = {}
        ev_n_acc = None
        ev_perm_free = None
        ev_tab_free = [None, None]
        ev_tab = [None, None]
        ev_kfree = None
        ev_vfree = None
        ev_krot_free = None
        ev_vrot_free = None
        ev_t1_free = None
        k_src = {}
        k_ev = {}
        for ji, jc in enumerate(jorder):
            par = ji % 2
            ev_w = dma("gpsimd", "wi", lambda E, jc=jc, par=par: [
                E.dma_start(out=wjc[par][:], in_=w_in_t[jc])],
                after=(ev_mm.get(ji - 2),), tag=f"wi{ta}_{ji}")

            def fn_mm(E, par=par, wjc=wjc, z1=z1, psM=psM):
                last = None
                for tt in range(2):
                    for dc in range(DCN):
                        last = E.matmul(
                            psM[par][tt][:],
                            wjc[par][:, 128 * dc:128 * (dc + 1)],
                            z1[:, dc * TAU + tt * 512:dc * TAU + (tt + 1) * 512],
                            start=(dc == 0), stop=(dc == DCN - 1))
                return last
            ev_mm[ji] = T_(fn_mm, after=[ev_w, ev_free.get(ji - 2)] + bz,
                           tag=f"qkv{ta}_{ji}")

            if jc < 16:
                dc_q = jc

                def fn_qe(E, par=par, jc=jc, dc_q=dc_q, ta=ta):
                    last = None
                    for tt in range(2):
                        last = E.activation(
                            qa[ta][:, dc_q * TAU + tt * 512:
                                   dc_q * TAU + (tt + 1) * 512],
                            psM[par][tt][:], AF.Identity,
                            bias=b_in_sb[:, jc:jc + 1])
                    return last
                ev_free[ji] = S_(fn_qe, after=(ev_mm[ji],), tag=f"qe{ta}{jc}")
                continue

            is_v = jc >= 32
            c = jc - (32 if is_v else 16)
            dstb = vbuf if is_v else kbuf
            ev_dst_free = ev_vfree if is_v else ev_kfree

            def fn_kv(E, par=par, jc=jc, dstb=dstb, psM=psM):
                last = None
                for tt in range(2):
                    last = E.activation(dstb[:, tt * 512:(tt + 1) * 512],
                                        psM[par][tt][:], AF.Identity,
                                        bias=b_in_sb[:, jc:jc + 1])
                return last
            ev_ev = S_(fn_kv, after=(ev_mm[ji], ev_dst_free), tag=f"kv{ta}{jc}")

            def fn_sq(E, par=par, jc=jc, psM=psM, ksqb=ksqb):
                last = None
                for tt in range(2):
                    last = E.activation(ksqb[:, tt * 512:(tt + 1) * 512],
                                        psM[par][tt][:], AF.Square,
                                        bias=b_in_sb[:, jc:jc + 1])
                return last
            ev_sq = S_(fn_sq, after=(ev_mm[ji], ev_n_acc), tag=f"kq{ta}{jc}")
            ev_free[ji] = ev_sq

            def fn_nm(E, psN=psN, ksqb=ksqb):
                last = None
                for tt in range(2):
                    last = E.matmul(psN[tt][0:1, :], ones_col_bf[:],
                                    ksqb[:, tt * 512:(tt + 1) * 512],
                                    start=True, stop=True)
                return last
            ev_nm = T_(fn_nm, after=(ev_sq, ev_const, ev_n_acc),
                       tag=f"nm{ta}{jc}")
            nrm_dst = nrmv if is_v else nrmk

            def fn_na(E, nrm_dst=nrm_dst, psN=psN):
                for tt in range(2):
                    E.scalar_tensor_tensor(
                        nrm_dst[0:1, tt * 512:(tt + 1) * 512],
                        psN[tt][0:1, :], 1.0,
                        nrm_dst[0:1, tt * 512:(tt + 1) * 512],
                        OP.mult, OP.add)
                return E.drain()
            ev_n_acc = V_(fn_na, after=(ev_nm, ev_nrm0), tag=f"na{ta}{jc}")

            if c < ROTC and DOROT:
                tp = c % 2
                if not is_v:
                    ev_tab[tp] = dma("sync", "tb", lambda E, c=c, tp=tp, ta=ta: [
                        E.dma_start(out=tabc[tp][:],
                                    in_=cos_in[c][:, ta * TAU:(ta + 1) * TAU]),
                        E.dma_start(out=tabs[tp][:],
                                    in_=sin_in[c][:, ta * TAU:(ta + 1) * TAU]),
                    ], after=(ev_tab_free[tp],), n=2, tag=f"tb{ta}{c}")

                def fn_pm(E, dstb=dstb, psP=psP):
                    last = None
                    for tt in range(2):
                        last = E.matmul(psP[tt][:], perm[:],
                                        dstb[:, tt * 512:(tt + 1) * 512],
                                        start=True, stop=True)
                    return last
                ev_pm = T_(fn_pm, after=(ev_ev, ev_mx, ev_perm_free),
                           tag=f"pm{ta}{jc}")
                rotdst = vrot if is_v else krot
                rot_free = ev_vrot_free if is_v else ev_krot_free

                def fn_rot(E, dstb=dstb, rotdst=rotdst, tp=tp, tabc=tabc, tabs=tabs, psP=psP, t1b=t1b):
                    E.tensor_mul(t1b[:], dstb[:], tabc[tp][:])
                    for tt in range(2):
                        E.tensor_mul(rotdst[:, tt * 512:(tt + 1) * 512],
                                     psP[tt][:],
                                     tabs[tp][:, tt * 512:(tt + 1) * 512])
                    return E.tensor_add(rotdst[:], rotdst[:], t1b[:])
                ev_rot = V_(fn_rot, after=(ev_pm, ev_tab[tp], rot_free,
                                           ev_t1_free), tag=f"ro{ta}{jc}")
                ev_perm_free = ev_rot
                ev_t1_free = ev_rot
                if is_v:
                    ev_tab_free[tp] = ev_rot
                src, src_ev = rotdst, ev_rot
                if is_v:
                    ev_vfree = ev_rot
                else:
                    ev_kfree = ev_rot
            else:
                src, src_ev = dstb, ev_ev

            if not is_v:
                k_src[c] = src
                k_ev[c] = src_ev
            else:
                ev_p = V_(lambda E, c=c, a=k_src[c], b=src:
                          E.tensor_mul(pbuf[:, c * TAU:(c + 1) * TAU],
                                       a[:], b[:]),
                          after=(k_ev[c], src_ev), tag=f"p{ta}{c}")
                if c < ROTC:
                    ev_krot_free = ev_p
                    ev_vrot_free = ev_p
                else:
                    ev_kfree = ev_p
                    ev_vfree = ev_p
            scale_step()

        ev_prod = V_(lambda E: E.tensor_mul(rrow[:], nrmk[:], nrmv[:]),
                     after=(ev_n_acc,), tag=f"pr{ta}")
        ev_sqr = S_(lambda E: E.activation(rrow[:], rrow[:], AF.Sqrt),
                    after=(ev_prod,), tag=f"rs{ta}")

        def fn_rr(E, nrmk=nrmk, rrow=rrow, rbf=rbf):
            E.drain()
            E.reciprocal(nrmk[:], rrow[:])
            E.drain()
            E.tensor_copy(rbf[:], nrmk[:])
            return E.drain()
        ev_rr = V_(fn_rr, after=(ev_sqr,), tag=f"rr{ta}")

        def fn_rb(E, psP=psP, rbf=rbf):
            last = None
            for tt in range(2):
                last = E.matmul(psP[tt][:], ones_row_bf[:],
                                rbf[0:1, tt * 512:(tt + 1) * 512],
                                start=True, stop=True)
            return last
        ev_rb = T_(fn_rb, after=(ev_rr, ev_perm_free, ev_const), tag=f"rb{ta}")

        def fn_rbc(E, rbc=rbc, psP=psP):
            last = None
            for tt in range(2):
                last = E.activation(rbc[:, tt * 512:(tt + 1) * 512],
                                    psP[tt][:], AF.Identity)
            return last
        ev_rbc = S_(fn_rbc, after=(ev_rb,), tag=f"rc{ta}")

        ev_as = []
        for c in range(16):
            ev_as.append(V_(lambda E, c=c, adump=adump, pbuf=pbuf, rbc=rbc:
                E.scalar_tensor_tensor(
                    adump[:], pbuf[:, c * TAU:(c + 1) * TAU], 0.0, rbc[:],
                    OP.add, OP.mult, accum_out=a_tmp[:, c:c + 1]),
                after=(ev_rbc,), tag=f"as{ta}{c}"))
        def fn_aacc(E):
            E.drain()
            E.scalar_tensor_tensor(a_acc[:], a_tmp[:], 1.0, a_acc[:],
                                   OP.mult, OP.add)
            return E.drain()
        V_(fn_aacc, after=ev_as + [ev_const], tag=f"aa{ta}")

        ev_qa_done[ta] = pl.barrier(tag=f"Ab{ta}")
        if DBG and ta == 0:
            dma("sync", "mx", lambda E, z1=z1, pbuf=pbuf: [
                E.dma_start(out=dbg["d_z1"], in_=z1[:]),
                E.dma_start(out=dbg["d_pbuf"], in_=pbuf[:]),
                E.dma_start(out=dbg["d_qa0"], in_=qa[0][:]),
                E.dma_start(out=dbg["d_aacc"], in_=a_acc[:]),
                E.dma_start(out=dbg["d_stn"].rearrange("(k p c) -> k p c", p=128, c=16)[0], in_=stn["sx"][:]),
                E.dma_start(out=dbg["d_stn"].rearrange("(k p c) -> k p c", p=128, c=16)[1], in_=stn["sq"][:]),
                E.dma_start(out=dbg["d_stn"].rearrange("(k p c) -> k p c", p=128, c=16)[2], in_=stn["mu"][:]),
                E.dma_start(out=dbg["d_stn"].rearrange("(k p c) -> k p c", p=128, c=16)[3], in_=stn["veps"][:]),
                E.dma_start(out=dbg["d_stn"].rearrange("(k p c) -> k p c", p=128, c=16)[4], in_=stn["sqr"][:]),
                E.dma_start(out=dbg["d_stn"].rearrange("(k p c) -> k p c", p=128, c=16)[5], in_=stn["i0"][:]),
                E.dma_start(out=dbg["d_stn"].rearrange("(k p c) -> k p c", p=128, c=16)[6], in_=stn["inv"][:]),
                E.dma_start(out=dbg["d_stn"].rearrange("(k p c) -> k p c", p=128, c=16)[7], in_=stn["nmi"][:]),
            ], after=ev_qa_done[ta], n=12, tag="dbgA")
            pl.barrier(tag="dbgAb")
        sm.close()
        sa.close()

    sf = ExitStack()
    ps_s1 = sf.enter_context(nc.psum_tensor("ps_s1", [1, 16], F32))
    ps_sb2 = sf.enter_context(nc.psum_tensor("ps_sb2", [128, 16], F32))
    scale_flush()

    def fn_sfr(E):
        E.reduce_sum(scol[:, 0:1], sacc_u[:], axis=AX)
        E.reduce_sum(scol[:, 1:2], sacc_g[:], axis=AX)
        E.reduce_sum(scol[:, 2:3], sacc_d[:], axis=AX)
        return E.drain()
    ev_sfr = V_(fn_sfr, after=(sst["use"][0], sst["use"][1]), tag="sfr")

    def fn_sfm(E):
        last = None
        for j in range(3):
            last = E.matmul(ps_s1[0:1, j:j + 1], scol[:, j:j + 1],
                            ones_col_f[:], start=(j == 0), stop=(j == 2))
        return last
    ev_sfm = T_(fn_sfm, after=(ev_sfr, ev_const), tag="sfm")

    def fn_sfs(E):
        E.tensor_scalar(srow[0:1, 0:3], ps_s1[0:1, 0:3], 1.0 / NW, EPS_Q,
                        OP.mult, OP.add)
        E.drain()
        E.tensor_scalar(srow[0:1, 3:6], srow[0:1, 0:3], 0.5, None, OP.mult)
        E.drain()
        E.tensor_scalar(srow[0:1, 6:9], srow[0:1, 3:6], -1.0, None,
                        OP.mult)
        return E.drain()
    ev_sfs = V_(fn_sfs, after=(ev_sfm,), tag="sfs")

    def fn_sfb(E):
        E.matmul(ps_sb2[:, 0:3], ones_row_f[:], srow[0:1, 0:3],
                 start=True, stop=False)
        E.matmul(ps_sb2[:, 3:6], ones_row_f[:], srow[0:1, 3:6],
                 start=False, stop=False)
        return E.matmul(ps_sb2[:, 6:9], ones_row_f[:], srow[0:1, 6:9],
                        start=False, stop=True)
    ev_sfb = T_(fn_sfb, after=(ev_sfs,), tag="sfb")

    def fn_sfe(E):
        E.tensor_copy(s_scale[:, 0:3], ps_sb2[:, 0:3])
        E.tensor_copy(s_thr[:, 0:3], ps_sb2[:, 3:6])
        E.tensor_copy(s_nthr[:, 0:3], ps_sb2[:, 6:9])
        return E.drain()
    ev_scale_done = V_(fn_sfe, after=(ev_sfb,), tag="sfe")

    if PH < 2:
        ev_aprime = None
    ps_at = sf.enter_context(nc.psum_tensor("ps_at", [16, 128], F32))
    ps_atb = sf.enter_context(nc.psum_tensor("ps_atb", [128, 16], F32))

    if PH < 2:
        ev_t1 = None
    ev_t1 = T_(lambda E: E.transpose(ps_at[:], a_acc[:], ident[:]),
               after=ev_qa_done[1], tag="art") if PH >= 2 else None
    if PH >= 2:
        ev_v1 = V_(lambda E: E.tensor_copy(aflat[:], ps_at[:]), after=(ev_t1,),
                   tag="arv")
        ev_g1 = dma("gpsimd", "cc", lambda E: [
            E.dma_start(out=cc_in.ap(), in_=aflat[:])], after=(ev_v1,), tag="ar1")
        if PH >= 3:
            ev_cc = G_(lambda E: E.collective_compute(
                "AllReduce", OP.add, replica_groups=[list(range(NCORE))],
                ins=[cc_in.ap()], outs=[cc_out.ap()]), after=(ev_g1,), tag="arcc")
        else:
            ev_cc = dma("gpsimd", "cc", lambda E: [
                E.dma_start(out=cc_out.ap(), in_=cc_in.ap())], after=(ev_g1,),
                tag="arcc_bypass")
        ev_g2 = dma("gpsimd", "cc", lambda E: [
            E.dma_start(out=aflat[:], in_=cc_out.ap())], after=(ev_cc,), tag="ar2")
        ev_t2 = T_(lambda E: E.transpose(ps_atb[:], aflat[:],
                                         ident[0:16, 0:16]),
                   after=(ev_g2,), tag="art2")
        ev_aprime = V_(lambda E: E.tensor_scalar(
            a_prime[:], ps_atb[:], 0.5, 0.5, OP.mult, OP.add),
            after=(ev_t2,), tag="arp")

    if DBG and PH >= 2:
        dma("sync", "mx", lambda E: [
            E.dma_start(out=dbg["d_aprime"], in_=a_prime[:]),
            E.dma_start(out=dbg["d_srow"], in_=srow[0:1, :]),
            E.dma_start(out=dbg["d_sq3"].rearrange("(k p c) -> k p c", p=128, c=3)[0], in_=s_scale[:]),
            E.dma_start(out=dbg["d_sq3"].rearrange("(k p c) -> k p c", p=128, c=3)[1], in_=s_thr[:]),
            E.dma_start(out=dbg["d_sq3"].rearrange("(k p c) -> k p c", p=128, c=3)[2], in_=s_nthr[:]),
        ], after=(ev_aprime, ev_scale_done), n=5, tag="dbgAR")
    bar_ar = pl.barrier(tag="arb")
    sf.close()
    scale_es.close()
    down_acc = es.enter_context(
        nc.sbuf_tensor("down_acc", [128, DCN * TAU], F32, side="right"))

    ev_prev_store = [None, None]
    for tau in range(NTAU if PH >= 4 else 0):
        sc = ExitStack()

        def sbc(name, shape, dt):
            return sc.enter_context(nc.sbuf_tensor(f"{name}_c{tau}", shape, dt, side="right"))
        x2b = [sbc(f"x2b{i}", [128, TAU], F32) for i in range(2)]
        wob = [sbc(f"wob{i}", [128, D], BF16) for i in range(2)]
        y2b = [sbc(f"y2b{i}", [128, TAU], F32) for i in range(2)]
        dumpb = sbc("dumpb", [128, TAU], BF16)
        dumpq = sbc("dumpq", [128, TAU], BF16)
        mubc = sbc("mubc", [128, TAU], F32)
        invbc = sbc("invbc", [128, TAU], F32)
        tmphb = sbc("tmphb", [128, TAU], F32)
        st2 = {k: sbc("st2_" + k, [1, TAU], F32)
               for k in ("sx", "sq", "veps", "i0", "t1")}
        scps = ExitStack()
        psO = [[scps.enter_context(nc.psum_tensor(f"psO{s}{t}_c{tau}", [128, 512], F32))
                for t in range(2)] for s in range(2)]
        psL2x = [scps.enter_context(nc.psum_tensor(f"psL2x{t}_c{tau}", [1, 512], F32))
                 for t in range(2)]
        psL2q = [scps.enter_context(nc.psum_tensor(f"psL2q{t}_c{tau}", [1, 512], F32))
                 for t in range(2)]

        def fn_qs(E, tau=tau):
            last = None
            for dc in range(DCN):
                last = E.tensor_scalar(
                    qa[tau][:, dc * TAU:(dc + 1) * TAU],
                    qa[tau][:, dc * TAU:(dc + 1) * TAU],
                    a_prime[:, dc:dc + 1], None, OP.mult)
            return last
        ev_qs = V_(fn_qs, after=[ev_aprime] + ev_qa_done[tau], tag=f"qs{tau}")

        ev_mmO = {}
        pend_l2 = []
        ev_freeO = {}
        ev_h = {}
        ev_dump = {}
        ev_l2mm = None
        for jc in range(DCN):
            par = jc % 2
            ev_w = dma("gpsimd", "wo", lambda E, jc=jc, par=par: [
                E.dma_start(out=wob[par][:], in_=w_out_t[jc])],
                after=(ev_mmO.get(jc - 2),), tag=f"wo{tau}_{jc}")
            ev_x2 = dma("sync", "x2", lambda E, jc=jc, par=par, tau=tau: [
                E.dma_start(out=x2b[par][:],
                            in_=xT_in[jc][:, tau * TAU:(tau + 1) * TAU])],
                after=(ev_h.get(jc - 2),), tag=f"x2{tau}_{jc}")

            def fn_mmo(E, par=par, tau=tau):
                last = None
                for tt in range(2):
                    for dc in range(DCN):
                        last = E.matmul(
                            psO[par][tt][:],
                            wob[par][:, 128 * dc:128 * (dc + 1)],
                            qa[tau][:, dc * TAU + tt * 512:
                                     dc * TAU + (tt + 1) * 512],
                            start=(dc == 0), stop=(dc == DCN - 1))
                return last
            ev_mmO[jc] = T_(fn_mmo, after=(ev_w, ev_qs,
                                           ev_freeO.get(jc - 2)),
                            tag=f"op{tau}_{jc}")

            def fn_y2(E, par=par, jc=jc):
                last = None
                for tt in range(2):
                    last = E.activation(y2b[par][:, tt * 512:(tt + 1) * 512],
                                        psO[par][tt][:], AF.Identity,
                                        bias=b_out_sb[:, jc:jc + 1])
                return last
            ev_y2 = S_(fn_y2, after=(ev_mmO[jc], ev_h.get(jc - 2)),
                       tag=f"y2{tau}_{jc}")
            ev_freeO[jc] = ev_y2

            ev_h[jc] = V_(lambda E, par=par, jc=jc: E.scalar_tensor_tensor(
                down_acc[:, jc * TAU:(jc + 1) * TAU], x2b[par][:], 2.0,
                y2b[par][:], OP.mult, OP.add),
                after=(ev_x2, ev_y2, ev_prev_store[tau]), tag=f"h{tau}_{jc}")

            if pend_l2:
                fn_p, ev_d, jp = pend_l2.pop(0)
                ev_l2mm = T_(fn_p, after=(ev_d, ev_const), tag=f"lm{tau}_{jp}")

            def fn_dmp(E, jc=jc):
                E.activation(dumpb[:], down_acc[:, jc * TAU:(jc + 1) * TAU],
                             AF.Identity)
                return E.activation(dumpq[:],
                                    down_acc[:, jc * TAU:(jc + 1) * TAU],
                                    AF.Square)
            ev_dump[jc] = S_(fn_dmp, after=(ev_h[jc], ev_l2mm),
                             tag=f"dp{tau}_{jc}")

            def fn_l2(E, jc=jc):
                last = None
                for tt in range(2):
                    E.matmul(psL2x[tt][0:1, :], ones_col_bf[:],
                             dumpb[:, tt * 512:(tt + 1) * 512],
                             start=(jc == 0), stop=(jc == DCN - 1))
                    last = E.matmul(psL2q[tt][0:1, :], ones_col_bf[:],
                                    dumpq[:, tt * 512:(tt + 1) * 512],
                                    start=(jc == 0), stop=(jc == DCN - 1))
                return last
            pend_l2.append((fn_l2, ev_dump[jc], jc))
            scale_step()

        while pend_l2:
            fn_p, ev_d, jp = pend_l2.pop(0)
            ev_l2mm = T_(fn_p, after=(ev_d, ev_const), tag=f"lm{tau}_{jp}")

        def fn_l2e(E):
            for tt in range(2):
                E.tensor_copy(st2["sx"][0:1, tt * 512:(tt + 1) * 512],
                              psL2x[tt][0:1, :])
            for tt in range(2):
                E.tensor_copy(st2["sq"][0:1, tt * 512:(tt + 1) * 512],
                              psL2q[tt][0:1, :])
            E.drain()
            E.tensor_scalar(st2["sx"][:], st2["sx"][:], 1.0 / D, None, OP.mult)
            E.tensor_scalar(st2["sq"][:], st2["sq"][:], 1.0 / D, None, OP.mult)
            E.drain()
            E.tensor_mul(st2["veps"][:], st2["sx"][:], st2["sx"][:])
            E.drain()
            E.tensor_sub(st2["veps"][:], st2["sq"][:], st2["veps"][:])
            E.drain()
            E.tensor_scalar(st2["veps"][:], st2["veps"][:], EPS_LN,
                            None, OP.add)
            return E.drain()
        ev_l2e = V_(fn_l2e, after=(ev_l2mm,), tag=f"l2e{tau}")
        ev_l2s = S_(lambda E: E.activation(st2["sq"][:], st2["veps"][:],
                                           AF.Sqrt), after=(ev_l2e,),
                    tag=f"l2s{tau}")

        def fn_l2n(E):
            E.drain()
            E.reciprocal(st2["i0"][:], st2["sq"][:])
            E.drain()
            E.tensor_mul(st2["t1"][:], st2["i0"][:], st2["i0"][:])
            E.drain()
            E.tensor_mul(st2["t1"][:], st2["t1"][:], st2["veps"][:])
            E.drain()
            E.tensor_scalar(st2["t1"][:], st2["t1"][:], -0.5, 1.5, OP.mult,
                            OP.add)
            E.drain()
            E.scalar_tensor_tensor(st2["t1"][:], st2["i0"][:], 1.0,
                                   st2["t1"][:], OP.mult, OP.mult)
            return E.drain()
        ev_l2n = V_(fn_l2n, after=(ev_l2s,), tag=f"l2n{tau}")

        bar_c1 = pl.barrier(tag=f"cb{tau}")
        scps.close()
        scps2 = ExitStack()
        psB = [scps2.enter_context(nc.psum_tensor(f"psB{t}_c{tau}", [128, 512], F32))
               for t in range(2)]
        psB2 = [scps2.enter_context(nc.psum_tensor(f"psB2{t}_c{tau}", [128, 512], F32))
                for t in range(2)]

        def fn_bc(E):
            last = None
            for tt in range(2):
                E.matmul(psB[tt][:], ones_row_f[:],
                         st2["sx"][0:1, tt * 512:(tt + 1) * 512],
                         start=True, stop=True)
                last = E.matmul(psB2[tt][:], ones_row_f[:],
                                st2["t1"][0:1, tt * 512:(tt + 1) * 512],
                                start=True, stop=True)
            return last
        ev_bc = T_(fn_bc, after=[ev_l2n] + bar_c1, tag=f"bc{tau}")

        def fn_bce(E):
            for tt in range(2):
                E.activation(mubc[:, tt * 512:(tt + 1) * 512], psB[tt][:],
                             AF.Identity)
            last = None
            for tt in range(2):
                last = E.activation(invbc[:, tt * 512:(tt + 1) * 512],
                                    psB2[tt][:], AF.Identity)
            return last
        ev_bce = S_(fn_bce, after=(ev_bc,), tag=f"be{tau}")

        ev_apply = []
        for jc in range(DCN):
            def fn_ap(E, jc=jc):
                E.tensor_sub(tmphb[:], down_acc[:, jc * TAU:(jc + 1) * TAU],
                             mubc[:])
                return E.tensor_mul(down_acc[:, jc * TAU:(jc + 1) * TAU],
                                    tmphb[:], invbc[:])
            ev_apply.append(V_(fn_ap, after=(ev_bce,), tag=f"ap{tau}_{jc}"))

        bar_c2 = pl.barrier(tag=f"cb2{tau}")
        if DBG and tau == 0:
            dma("sync", "mx", lambda E: [
                E.dma_start(out=dbg["d_h1"], in_=down_acc[:])],
                after=bar_c2, tag="dbgC")
            bar_c2 = pl.barrier(tag="dbgCb")
        scps2.close()
        sc.close()
        qa_es[tau].close()

        if not DODOWN:
            continue
        sd_ = ExitStack()

        def sbd(name, shape, dt):
            return sd_.enter_context(nc.sbuf_tensor(f"{name}_d{tau}", shape, dt, side="right"))
        h1T = sbd("h1T", [128, DCN * TAU], BF16)
        hid = sbd("hid", [128, 8 * TAU], BF16)
        wuf = sbd("wuf", [128, D], F32)
        wgf = sbd("wgf", [128, D], F32)
        wdf = [sbd(f"wdf{i}", [128, 1024], F32) for i in range(2)]
        quq = [sbd(f"quq{i}", [128, D], BF16) for i in range(2)]
        qgq = [sbd(f"qgq{i}", [128, D], BF16) for i in range(2)]
        qdq = [sbd(f"qdq{i}", [128, 1024], BF16) for i in range(2)]
        mpos = sbd("mpos", [128, D], U8)
        mneg = sbd("mneg", [128, D], U8)
        silub = sbd("silub", [128, TAU], BF16)
        outb = sbd("outb", [128, D], F32)
        sdps = ExitStack()
        psU = [sdps.enter_context(nc.psum_tensor(f"psU{t}_d{tau}", [128, 512], F32))
               for t in range(2)]
        psG = [sdps.enter_context(nc.psum_tensor(f"psG{t}_d{tau}", [128, 512], F32))
               for t in range(2)]
        psDn = [sdps.enter_context(nc.psum_tensor(f"psD{t}_d{tau}", [128, 512], F32))
                for t in range(4)]

        def fn_h1t(E):
            last = None
            for dc in range(DCN):
                last = E.tensor_copy(h1T[:, dc * TAU:(dc + 1) * TAU],
                                     down_acc[:, dc * TAU:(dc + 1) * TAU])
            return last
        ev_h1t = V_(fn_h1t, after=ev_apply + bar_c2, tag=f"h1t{tau}")

        ev_wu_dma = None
        ev_q_ug = {}
        ev_mmU = {}
        ev_silu = {}
        ev_hid = {}
        ev_dn_ev = {}
        ev_dn_mm = {}

        def emit_ug_load_quant(ec):
            ev_d = dma("sync", "wu", lambda E, ec=ec: [
                E.dma_start(out=wuf[:], in_=wug_t[ec][:, 0:D]),
                E.dma_start(out=wgf[:], in_=wug_t[ec][:, D:2 * D])],
                after=(ev_q_ug.get(ec - 1), ev_scale_done), n=2,
                tag=f"wu{tau}_{ec}")

            def fn_qug(E, ec=ec):
                qp = ec % 2
                E.tensor_scalar(mpos[:], wuf[:], s_thr[:, 0:1], None,
                                OP.is_gt)
                E.tensor_scalar(mneg[:], wuf[:], s_nthr[:, 0:1], None,
                                OP.is_lt)
                E.tensor_sub(quq[qp][:], mpos[:], mneg[:])
                E.tensor_scalar(mpos[:], wgf[:], s_thr[:, 1:2], None,
                                OP.is_gt)
                E.tensor_scalar(mneg[:], wgf[:], s_nthr[:, 1:2], None,
                                OP.is_lt)
                return E.tensor_sub(qgq[qp][:], mpos[:], mneg[:])
            assert ec < 2 or (ec - 2) in ev_mmU
            ev_q_ug[ec] = V_(fn_qug, after=(ev_d, ev_scale_done,
                                            ev_mmU.get(ec - 2)),
                             tag=f"qq{tau}_{ec}")

        emit_ug_load_quant(0)
        for s in range(8):
            for ei in range(8):
                ec = 8 * s + ei
                if ec + 1 < ECN:
                    emit_ug_load_quant(ec + 1)

                def fn_mmu(E, ec=ec, tau=tau):
                    qp = ec % 2
                    last = None
                    for tt in range(2):
                        for dc in range(DCN):
                            E.matmul(psU[tt][:],
                                     quq[qp][:, 128 * dc:128 * (dc + 1)],
                                     h1T[:, dc * TAU + tt * 512:
                                         dc * TAU + (tt + 1) * 512],
                                     start=(dc == 0), stop=(dc == DCN - 1))
                        for dc in range(DCN):
                            last = E.matmul(
                                psG[tt][:],
                                qgq[qp][:, 128 * dc:128 * (dc + 1)],
                                h1T[:, dc * TAU + tt * 512:
                                    dc * TAU + (tt + 1) * 512],
                                start=(dc == 0), stop=(dc == DCN - 1))
                    return last
                ev_mmU[ec] = T_(fn_mmu, after=(ev_q_ug[ec], ev_h1t,
                                               ev_hid.get(ec - 1),
                                               ev_dn_mm.get(s - 1)),
                                tag=f"ug{tau}_{ec}")

                def fn_silu(E, ec=ec):
                    last = None
                    for tt in range(2):
                        last = E.activation(silub[:, tt * 512:(tt + 1) * 512],
                                            psU[tt][:], AF.Silu,
                                            scale=s_scale[:, 0:1])
                    return last
                ev_silu[ec] = S_(fn_silu, after=(ev_mmU[ec], ev_scale_done,
                                                 ev_dn_ev.get((s - 1, "sil"))),
                                 tag=f"si{tau}_{ec}")

                def fn_hid(E, ei=ei):
                    last = None
                    for tt in range(2):
                        last = E.scalar_tensor_tensor(
                            hid[:, ei * TAU + tt * 512:ei * TAU + (tt + 1) * 512],
                            psG[tt][:], s_scale[:, 1:2],
                            silub[:, tt * 512:(tt + 1) * 512],
                            OP.mult, OP.mult)
                    return last
                ev_hid[ec] = V_(fn_hid, after=(ev_silu[ec],), tag=f"hi{tau}_{ec}")

            def emit_wd_load_quant(s2, dc2):
                dp = dc2 % 2
                prev_rd = ev_q_ug.get(("d", s2, dc2 - 2))
                if prev_rd is None:
                    prev_rd = ev_q_ug.get(("d", s2 - 1, DCN + dc2 - 2))
                ev_d = dma("sync", "wd", lambda E, s2=s2, dc2=dc2, dp=dp: [
                    E.dma_start(out=wdf[dp][:], in_=wd_t[s2 * 16 + dc2])],
                    after=(prev_rd, ev_scale_done),
                    tag=f"wd{tau}_{s2}_{dc2}")

                def fn_qd(E, dp=dp):
                    E.tensor_scalar(mpos[:, 0:1024], wdf[dp][:],
                                    s_thr[:, 2:3], None, OP.is_gt)
                    E.tensor_scalar(mneg[:, 0:1024], wdf[dp][:],
                                    s_nthr[:, 2:3], None, OP.is_lt)
                    return E.tensor_sub(qdq[dp][:], mpos[:, 0:1024],
                                        mneg[:, 0:1024])
                ev_q_ug[("d", s2, dc2)] = V_(
                    fn_qd, after=(ev_d, ev_scale_done,
                                  ev_dn_mm.get((s2, dc2 - 2)),
                                  ev_dn_mm.get((s2 - 1, DCN - 2 + dc2))),
                    tag=f"qd{tau}_{s2}_{dc2}")

            if s == 0:
                emit_wd_load_quant(0, 0)
            ev_dn_ev_last = ev_dn_ev.get((s - 1, DCN - 1))
            for dc in range(DCN if DODOWN >= 2 else 0):
                dpar = dc % 2
                if dc + 1 < DCN:
                    emit_wd_load_quant(s, dc + 1)
                elif s + 1 < 8:
                    emit_wd_load_quant(s + 1, 0)

                def fn_dmm(E, s=s, dc=dc, dpar=dpar):
                    last = None
                    for tt in range(2):
                        for ei in range(8):
                            last = E.matmul(
                                psDn[2 * dpar + tt][:],
                                qdq[dpar][:, 128 * ei:128 * (ei + 1)],
                                hid[:, ei * TAU + tt * 512:
                                    ei * TAU + (tt + 1) * 512],
                                start=(ei == 0), stop=(ei == 7))
                    return last
                prev_same_par = ev_dn_ev.get((s, dc - 2))
                if dc < 2:
                    prev_same_par = ev_dn_ev.get((s - 1, DCN - 2 + dc))
                ev_dn_mm[(s, dc)] = T_(
                    fn_dmm, after=(ev_q_ug[("d", s, dc)], ev_hid[8 * s + 7],
                                   prev_same_par, ev_dn_ev_last,
                                   ev_mmU.get(8 * s + 7)),
                    tag=f"dn{tau}_{s}_{dc}")
                ev_dn_ev_last = None

                def fn_dev(E, dc=dc, dpar=dpar):
                    last = None
                    for tt in range(2):
                        sl = slice(dc * TAU + tt * 512,
                                   dc * TAU + (tt + 1) * 512)
                        E.tensor_scalar(silub[:, tt * 512:(tt + 1) * 512],
                                        psDn[2 * dpar + tt][:],
                                        s_scale[:, 2:3], None, OP.mult)
                        last = E.scalar_tensor_tensor(
                            down_acc[:, sl],
                            silub[:, tt * 512:(tt + 1) * 512], 1.0,
                            down_acc[:, sl], OP.mult, OP.add)
                    return last
                ev_dn_ev[(s, dc)] = V_(fn_dev, after=(ev_dn_mm[(s, dc)],),
                                       tag=f"de{tau}_{s}_{dc}")
            ev_dn_ev[(s, "sil")] = ev_dn_ev.get((s, DCN - 1))
            ev_dn_mm[s] = ev_dn_mm.get((s, DCN - 1))
            scale_step()

        bar_d1 = pl.barrier(tag=f"db{tau}")
        if DBG and tau == 0:
            dma("sync", "mx", lambda E, qdq=qdq, silub=silub: [
                E.dma_start(out=dbg["d_h2"], in_=down_acc[:]),
                E.dma_start(out=dbg["d_hid"], in_=hid[:]),
                E.dma_start(out=dbg["d_qd"], in_=qdq[1][:]),
                E.dma_start(out=dbg["d_dn"], in_=silub[:])],
                after=bar_d1, n=4, tag="dbgD")
            bar_d1 = pl.barrier(tag="dbgDb")
        sdps.close()
        sdps2 = ExitStack()
        psL3 = [sdps2.enter_context(nc.psum_tensor(f"psL3{i}_d{tau}", [128, 2048], F32))
                for i in range(2)]

        ev_l3 = {}
        ev_store = None
        ev_outb_free = None
        ev_gate = {}
        psx = sdps2.enter_context(nc.sbuf_tensor(f"l3px_d{tau}", [128, 8], F32,
                                                 side="right"))
        psq = sdps2.enter_context(nc.sbuf_tensor(f"l3pq_d{tau}", [128, 8], F32,
                                                 side="right"))
        for tc in range(8 if DODOWN >= 3 else 0):
            lpar = tc % 2
            ev_tb = {}
            for b in range(4):
                def fn_l3t(E, tc=tc, lpar=lpar, b=b, tau=tau):
                    last = None
                    for j in range(4):
                        dc = 4 * b + j
                        last = E.matmul(
                            psL3[lpar][:, 128 * dc:128 * (dc + 1)],
                            down_acc[:, dc * TAU + tc * 128:
                                     dc * TAU + (tc + 1) * 128],
                            ident[:], start=True, stop=True)
                    return last
                ev_tb[b] = T_(fn_l3t,
                              after=[ev_gate.get((tc, b - 2)),
                                     ev_gate.get((tc - 1, b + 2)),
                                     ev_l3.get(tc - 2), ev_mx] + bar_d1,
                              tag=f"l3t{tau}_{tc}_{b}")
                ev_rx = V_(lambda E, lpar=lpar, b=b: E.reduce_sum(
                    psx[:, b + 4 * lpar:b + 4 * lpar + 1],
                    psL3[lpar][:, 512 * b:512 * (b + 1)], axis=AX),
                    after=(ev_tb[b],), tag=f"l3x{tau}_{tc}_{b}")
                ev_rq = S_(lambda E, lpar=lpar, b=b: E.activation(
                    outb[:, 512 * b:512 * (b + 1)],
                    psL3[lpar][:, 512 * b:512 * (b + 1)], AF.Square,
                    accum_out=psq[:, b + 4 * lpar:b + 4 * lpar + 1]),
                    after=(ev_tb[b], ev_rx, ev_outb_free),
                    tag=f"l3q{tau}_{tc}_{b}")
                ev_gate[(tc, b)] = ev_rq

            def fn_l3v(E, tc=tc, lpar=lpar):
                E.drain()
                E.reduce_sum(stn["sx"][:, tc:tc + 1],
                             psx[:, 4 * lpar:4 * lpar + 4], axis=AX)
                E.reduce_sum(stn["sq"][:, tc:tc + 1],
                             psq[:, 4 * lpar:4 * lpar + 4], axis=AX)
                E.drain()
                E.tensor_scalar(stn["mu"][:, tc:tc + 1], stn["sx"][:, tc:tc + 1],
                                1.0 / D, None, OP.mult)
                E.tensor_scalar(stn["ex2"][:, tc:tc + 1], stn["sq"][:, tc:tc + 1],
                                1.0 / D, None, OP.mult)
                E.drain()
                E.tensor_mul(stn["veps"][:, tc:tc + 1], stn["mu"][:, tc:tc + 1],
                             stn["mu"][:, tc:tc + 1])
                E.drain()
                E.tensor_sub(stn["veps"][:, tc:tc + 1], stn["ex2"][:, tc:tc + 1],
                             stn["veps"][:, tc:tc + 1])
                E.drain()
                E.tensor_scalar(stn["veps"][:, tc:tc + 1],
                                stn["veps"][:, tc:tc + 1], EPS_LN,
                                None, OP.add)
                return E.drain()
            ev_l3v = V_(fn_l3v, after=[ev_gate[(tc, b)] for b in range(4)],
                        tag=f"l3v{tau}_{tc}")
            ev_l3s = S_(lambda E, tc=tc: E.activation(
                stn["sqr"][:, tc:tc + 1], stn["veps"][:, tc:tc + 1], AF.Sqrt),
                after=(ev_l3v,), tag=f"l3s{tau}_{tc}")

            def fn_l3n(E, tc=tc):
                E.drain()
                E.reciprocal(stn["i0"][:, tc:tc + 1], stn["sqr"][:, tc:tc + 1])
                E.drain()
                E.tensor_mul(stn["t1"][:, tc:tc + 1], stn["i0"][:, tc:tc + 1],
                             stn["i0"][:, tc:tc + 1])
                E.drain()
                E.tensor_mul(stn["t1"][:, tc:tc + 1], stn["t1"][:, tc:tc + 1],
                             stn["veps"][:, tc:tc + 1])
                E.drain()
                E.tensor_scalar(stn["t1"][:, tc:tc + 1], stn["t1"][:, tc:tc + 1],
                                -0.5, 1.5, OP.mult, OP.add)
                E.drain()
                E.tensor_mul(stn["inv"][:, tc:tc + 1], stn["i0"][:, tc:tc + 1],
                             stn["t1"][:, tc:tc + 1])
                E.drain()
                E.tensor_mul(stn["nmi"][:, tc:tc + 1], stn["mu"][:, tc:tc + 1],
                             stn["inv"][:, tc:tc + 1])
                E.drain()
                E.tensor_scalar(stn["nmi"][:, tc:tc + 1],
                                stn["nmi"][:, tc:tc + 1], -1.0, None,
                                OP.mult)
                return E.drain()
            ev_l3n = V_(fn_l3n, after=(ev_l3s,), tag=f"l3n{tau}_{tc}")

            ev_ap3 = S_(lambda E, lpar=lpar, tc=tc: E.activation(
                outb[:], psL3[lpar][:], AF.Identity,
                scale=stn["inv"][:, tc:tc + 1], bias=stn["nmi"][:, tc:tc + 1]),
                after=(ev_l3n,), tag=f"l3a{tau}_{tc}")
            ev_l3[tc] = ev_ap3
            gi = tau * 8 + tc
            ev_store = dma("scalar", "ou", lambda E, gi=gi: [
                E.dma_start(out=y_out[gi], in_=outb[:])],
                after=(ev_ap3,), tag=f"st{tau}_{tc}")
            ev_outb_free = ev_store
        ev_prev_store[(tau + 1) % 2] = ev_store

        bar_d2 = pl.barrier(tag=f"db2{tau}")
        sdps2.close()
        sd_.close()

    def fn_fin(E):
        return E.nop()
    n_ou_0 = pl.counts[SD["ou"][0]]
    n_ou_1 = pl.counts[SD["ou"][1]]
    if n_ou_0 or n_ou_1:
        pl.op("sync", fn_fin,
              after=((SD["ou"][0], n_ou_0), (SD["ou"][1], n_ou_1)), tag="fin")
    else:
        pl.barrier(tag="fin")

    if PH < 4:
        qa_es[0].close()
        qa_es[1].close()

    pl.check()

    semh = {}
    for name in pl.sems:
        semh[name] = es.enter_context(nc.semaphore(name))
    block = es.enter_context(nc.Block())

    def make_runner(eng):
        def run(E):
            for waits, fn, sem, amt, tag, is_dma in pl.ops[eng]:
                for s, v in waits:
                    E.wait_ge(semh[s], v)
                ret = fn(E)
                if sem is None:
                    continue
                if is_dma:
                    assert isinstance(ret, list) and ret, tag
                    for inst in ret:
                        inst.then_inc(semh[sem], 16)
                else:
                    if ret is None:
                        ret = E.nop()
                    ret.then_inc(semh[sem], amt)
        return run

    block.tensor(make_runner("tensor"))
    block.vector(make_runner("vector"))
    block.scalar(make_runner("scalar"))
    block.sync(make_runner("sync"))
    block.gpsimd(make_runner("gpsimd"))

    es.close()
    return nc


def _prep_host(inputs):
    X = np.asarray(inputs["L0_attn_X"], dtype=np.float32).reshape(S, D)
    W_in = np.ascontiguousarray(np.asarray(inputs["L0_attn_W_in"], np.float32))
    b_in = np.asarray(inputs["L0_attn_b_in"], np.float32)
    W_out = np.ascontiguousarray(np.asarray(inputs["L0_attn_W_out"], np.float32))
    b_out = np.asarray(inputs["L0_attn_b_out"], np.float32)
    gam = np.asarray(inputs["L0_attn_gamma"], np.float32)
    bet = np.asarray(inputs["L0_attn_beta"], np.float32)
    W_up = np.asarray(inputs["L0_ffn_W_up"], np.float32)
    W_gate = np.asarray(inputs["L0_ffn_W_gate"], np.float32)
    W_down = np.asarray(inputs["L0_ffn_W_down"], np.float32)

    w_in_t = np.ascontiguousarray(
        W_in.reshape(DCN, 128, JCN, 128).transpose(2, 1, 0, 3)).ravel()
    w_out_t = np.ascontiguousarray(
        W_out.reshape(DCN, 128, DCN, 128).transpose(2, 1, 0, 3)).ravel()
    tu = W_up.reshape(ECN, 128, DCN, 128).transpose(0, 3, 2, 1)
    tg = W_gate.reshape(ECN, 128, DCN, 128).transpose(0, 3, 2, 1)
    wug = np.ascontiguousarray(
        np.stack([tu, tg], axis=1).transpose(0, 2, 1, 3, 4)).ravel()
    wd = np.ascontiguousarray(
        W_down.reshape(DCN, 128, 8, 8, 128).transpose(2, 0, 4, 3, 1)).ravel()

    gamma_t = np.ascontiguousarray(gam.reshape(DCN, 128).T).ravel()
    beta_t = np.ascontiguousarray(bet.reshape(DCN, 128).T).ravel()
    b_in_t = np.ascontiguousarray(b_in.reshape(JCN, 128).T).ravel()
    b_out_t = np.ascontiguousarray(b_out.reshape(DCN, 128).T).ravel()
    ident = np.eye(128, dtype=np.float32).ravel()
    permm = np.zeros((128, 128), dtype=np.float64)
    for i in range(64):
        permm[2 * i + 1, 2 * i] = -1.0
        permm[2 * i, 2 * i + 1] = 1.0
    permm = permm.astype(ml_dtypes.bfloat16).ravel()

    inv_freq = 1.0 / (10000.0 ** (np.arange(0, 1024, 2, dtype=np.float64) / 1024.0))
    per_core = []
    for c0 in range(NCORE):
        pos = np.arange(c0 * T, (c0 + 1) * T, dtype=np.float64)
        ang = pos[None, :] * inv_freq[:, None]
        cosf = np.cos(ang)
        sinf = np.sin(ang)
        cos_full = np.repeat(cosf, 2, axis=0)
        sin_full = np.repeat(sinf, 2, axis=0)
        cos_t = np.ascontiguousarray(
            cos_full.reshape(ROTC, 128, T)).astype(ml_dtypes.bfloat16).ravel()
        sin_t = np.ascontiguousarray(
            sin_full.reshape(ROTC, 128, T)).astype(ml_dtypes.bfloat16).ravel()
        xs = np.ascontiguousarray(X[c0 * T:(c0 + 1) * T, :])
        xT = np.ascontiguousarray(xs.T)
        per_core.append({
            "x": xs.ravel(), "xT": xT.ravel(),
            "w_in_t": w_in_t, "w_out_t": w_out_t, "wug_t": wug, "wd_t": wd,
            "cos_t": cos_t, "sin_t": sin_t,
            "gamma_t": gamma_t, "beta_t": beta_t, "b_in_t": b_in_t,
            "b_out_t": b_out_t, "ident": ident, "perm": permm,
        })
    return per_core


def kernel(**inputs):
    in_maps = _prep_host(inputs)
    nc = build_nc()
    res = run_bass_kernel_spmd(nc, in_maps, core_ids=list(range(NCORE)))
    out = np.concatenate(
        [res.results[c]["y"].reshape(T, D) for c in range(NCORE)], axis=0)
    return out.reshape(1, S, D).astype(np.float32)


# version 11 (final)
# speedup vs baseline: 1.0113x; 1.0077x over previous
from contextlib import ExitStack

import numpy as np
import ml_dtypes

import concourse.bass as bass
import concourse.mybir as mybir
from concourse.bass_utils import run_bass_kernel_spmd

F32 = mybir.dt.float32
BF16 = mybir.dt.bfloat16
U8 = mybir.dt.uint8
AF = mybir.ActivationFunctionType
OP = mybir.AluOpType
AX = mybir.AxisListType.X

NCORE = 8
D = 2048
S = 16384
T = S // NCORE
DCN = 16
JCN = 48
EH = 8192
ECN = 64
TAU = 1024
NTAU = T // TAU
ROTC = 8
EPS_LN = 1e-5
EPS_Q = 1e-8
NW = float(EH * D)


class Plan:
    ENGINES = ("tensor", "vector", "scalar", "sync", "gpsimd")

    def __init__(self):
        self.ops = {e: [] for e in self.ENGINES}
        self.counts = {}
        self.sems = []
        self.last_ev = {}

    def sem(self, name):
        self.sems.append(name)
        self.counts[name] = 0
        return name

    def op(self, eng, fn, after=(), sig=True, tag=""):
        waits = {}
        for ev in after:
            if ev is None:
                continue
            s, v = ev
            waits[s] = max(waits.get(s, 0), v)
        sem, amt, ev = None, 0, None
        if sig:
            sem = "eng_" + eng
            self.counts[sem] += 1
            amt = 1
            ev = (sem, self.counts[sem])
            self.last_ev[eng] = ev
        self.ops[eng].append((sorted(waits.items()), fn, sem, amt, tag, False))
        return ev

    def dma(self, eng, sem, fn, after=(), n=1, tag=""):
        waits = {}
        for ev in after:
            if ev is None:
                continue
            s, v = ev
            waits[s] = max(waits.get(s, 0), v)
        self.counts[sem] += 16 * n
        ev = (sem, self.counts[sem])
        self.ops[eng].append((sorted(waits.items()), fn, sem, 16 * n, tag, True))
        self.last_ev["dma_" + sem] = ev
        return ev

    def barrier(self, extra=(), tag="bar"):
        evs = [e for e in self.last_ev.values() if e is not None]
        evs += [e for e in extra if e is not None]
        out = []
        for eng in self.ENGINES:
            out.append(self.op(eng, lambda E: E.nop(), after=evs, tag=tag))
        return out

    def check(self):
        pos = {e: 0 for e in self.ENGINES}
        cnt = {s: 0 for s in self.sems}
        progress = True
        while progress:
            progress = False
            for e in self.ENGINES:
                while pos[e] < len(self.ops[e]):
                    waits, fn, sem, amt, tag, is_dma = self.ops[e][pos[e]]
                    if all(cnt[s] >= v for s, v in waits):
                        pos[e] += 1
                        progress = True
                        if sem is not None:
                            cnt[sem] += amt
                    else:
                        break
        stuck = [(e, self.ops[e][pos[e]][4], self.ops[e][pos[e]][0])
                 for e in self.ENGINES if pos[e] < len(self.ops[e])]
        if stuck:
            raise RuntimeError(f"plan deadlock: {stuck[:4]}")


def build_nc():
    import os
    PH = int(os.environ.get("KMK_PHASES", "4"))
    DODOWN = int(os.environ.get("KMK_D", "3"))
    L3 = int(os.environ.get("KMK_L3", "4"))
    DBG = int(os.environ.get("KMK_DBG", "0"))
    DOSCALE = int(os.environ.get("KMK_SCALE", "1"))
    DOROT = int(os.environ.get("KMK_ROT", "1"))
    nc = bass.Bass("TRN2", target_bir_lowering=False)
    pl = Plan()
    es = ExitStack()

    def din(name, n, dt):
        return nc.dram_tensor(name, [n], dt, kind="ExternalInput").ap()

    x_in = din("x", T * D, F32).rearrange("(n p d) -> n p d", p=128, d=D)
    xT_in = din("xT", D * T, F32).rearrange("(n p t) -> n p t", p=128, t=T)
    w_in_t = din("w_in_t", JCN * 128 * D, F32).rearrange(
        "(j p q) -> j p q", p=128, q=D)
    w_out_t = din("w_out_t", DCN * 128 * D, F32).rearrange(
        "(j p q) -> j p q", p=128, q=D)
    wug_t = din("wug_t", ECN * 128 * 2 * D, F32).rearrange(
        "(e p q) -> e p q", p=128, q=2 * D)
    wd_t = din("wd_t", EH * D, F32).rearrange(
        "(n p q) -> n p q", p=128, q=1024)
    cos_in = din("cos_t", ROTC * 128 * T, BF16).rearrange(
        "(c p t) -> c p t", p=128, t=T)
    sin_in = din("sin_t", ROTC * 128 * T, BF16).rearrange(
        "(c p t) -> c p t", p=128, t=T)
    gamma_in = din("gamma_t", 128 * DCN, F32).rearrange("(p c) -> p c", c=DCN)
    beta_in = din("beta_t", 128 * DCN, F32).rearrange("(p c) -> p c", c=DCN)
    b_in_in = din("b_in_t", 128 * JCN, F32).rearrange("(p c) -> p c", c=JCN)
    b_out_in = din("b_out_t", 128 * DCN, F32).rearrange("(p c) -> p c", c=DCN)
    ident_in = din("ident", 128 * 128, F32).rearrange("(p q) -> p q", q=128)
    perm_in = din("perm", 128 * 128, BF16).rearrange("(p q) -> p q", q=128)

    y_out = nc.dram_tensor("y", [T * D], F32, kind="ExternalOutput").ap() \
        .rearrange("(n p d) -> n p d", p=128, d=D)
    cc_in = nc.dram_tensor("cc_in", [16, 128], F32)
    cc_out = nc.dram_tensor("cc_out", [16, 128], F32)
    dbg = {}
    if DBG:
        for nm, n, dt in [("d_z1", 128 * DCN * TAU, BF16),
                          ("d_qa0", 128 * DCN * TAU, BF16),
                          ("d_pbuf", 128 * DCN * TAU, BF16),
                          ("d_aacc", 128 * DCN, F32),
                          ("d_aprime", 128 * DCN, F32),
                          ("d_h1", 128 * DCN * TAU, F32),
                          ("d_h2", 128 * DCN * TAU, F32),
                          ("d_srow", 16, F32),
                          ("d_hid", 128 * 8 * TAU, BF16), ("d_stn", 128 * 16 * 8, F32), ("d_sq3", 128 * 9, F32), ("d_qd", 128 * 1024, BF16), ("d_dn", 128 * 1024, BF16)]:
            dbg[nm] = nc.dram_tensor(nm, [n], dt, kind="ExternalOutput").ap()

    def sb(name, shape, dt):
        return es.enter_context(nc.sbuf_tensor(name, shape, dt))

    ident = sb("c_ident", [128, 128], F32)
    perm = sb("c_perm", [128, 128], BF16)
    ones_col_bf = sb("c_ones_cb", [128, 1], BF16)
    ones_row_bf = sb("c_ones_rb", [1, 128], BF16)
    ones_row_f = sb("c_ones_rf", [1, 128], F32)
    ones_col_f = sb("c_ones_cf", [128, 1], F32)
    gamma = sb("c_gamma", [128, DCN], F32)
    beta = sb("c_beta", [128, DCN], F32)
    b_in_sb = sb("c_bin", [128, JCN], F32)
    b_out_sb = sb("c_bout", [128, DCN], F32)
    a_acc = sb("a_acc", [128, DCN], F32)
    a_tmp = sb("a_tmp", [128, DCN], F32)
    a_prime = sb("a_prime", [128, DCN], F32)
    aflat = sb("aflat", [16, 128], F32)

    stn = {k: sb("st_" + k, [128, 16], F32) for k in
           ("sx", "sq", "mu", "ex2", "veps", "sqr", "i0", "t1", "inv", "nmi")}
    srow = sb("srow", [1, 16], F32)

    sacc_u = sb("sacc_u", [128, ECN], F32)
    sacc_g = sb("sacc_g", [128, ECN], F32)
    sacc_d = sb("sacc_d", [128, 64], F32)
    scol = sb("scol", [128, 3], F32)
    s_thr = sb("s_thr", [128, 3], F32)
    s_nthr = sb("s_nthr", [128, 3], F32)
    s_scale = sb("s_scale", [128, 3], F32)

    qa_es = [ExitStack() for _ in range(NTAU)]
    qa = [None, None]
    qa[1] = qa_es[1].enter_context(
        nc.sbuf_tensor("qa1", [128, DCN * TAU], BF16))
    qa[0] = qa_es[0].enter_context(
        nc.sbuf_tensor("qa0", [128, DCN * TAU], BF16))
    down_acc = None

    for e in Plan.ENGINES:
        pl.sem("eng_" + e)
    SD = {}
    for name, k in [("mx", 1), ("x", 2), ("tb", 2), ("wi", 2), ("sc", 2),
                    ("x2", 2), ("wo", 2), ("wu", 2), ("wd", 2), ("ou", 2),
                    ("cc", 1)]:
        SD[name] = [pl.sem(f"dma_{name}{i}") for i in range(k)]
    dstream_idx = {k: 0 for k in SD}

    def dma(eng, stream, fn, after=(), n=1, tag=""):
        i = dstream_idx[stream]
        dstream_idx[stream] += 1
        sem = SD[stream][i % len(SD[stream])]
        return pl.dma(eng, sem, fn, after=after, n=n, tag=tag)

    def T_(fn, after=(), tag="t"):
        return pl.op("tensor", fn, after, tag=tag)

    def V_(fn, after=(), tag="v"):
        return pl.op("vector", fn, after, tag=tag)

    def S_(fn, after=(), tag="s"):
        return pl.op("scalar", fn, after, tag=tag)

    def G_(fn, after=(), tag="g"):
        return pl.op("gpsimd", fn, after, tag=tag)

    ev_mx = dma("sync", "mx", lambda E: [
        E.dma_start(out=ident[:], in_=ident_in),
        E.dma_start(out=perm[:], in_=perm_in),
        E.dma_start(out=gamma[:], in_=gamma_in),
        E.dma_start(out=beta[:], in_=beta_in),
        E.dma_start(out=b_in_sb[:], in_=b_in_in),
        E.dma_start(out=b_out_sb[:], in_=b_out_in),
    ], n=6, tag="mx")

    def fn_consts(E):
        E.memset(ones_col_bf[:], 1.0)
        E.memset(ones_row_bf[:], 1.0)
        E.memset(ones_row_f[:], 1.0)
        E.memset(ones_col_f[:], 1.0)
        E.memset(a_acc[:], 0.0)
        return E.memset(sacc_d[:], 0.0)
    ev_const = V_(fn_consts, tag="consts")

    scale_es = ExitStack()
    scaleb = [scale_es.enter_context(
        nc.sbuf_tensor(f"scaleb{i}", [128, 2048], F32, side="right")) for i in range(2)]
    scale_jobs = ([("u", ec) for ec in range(ECN)]
                  + [("g", ec) for ec in range(ECN)]
                  + [("d", n2) for n2 in range(64)])
    if not DOSCALE:
        scale_jobs = scale_jobs[:2]
    sst = {"iss": 0, "con": 0, "ld": [None, None], "use": [None, None]}

    def scale_issue():
        i = sst["iss"]
        if i >= len(scale_jobs):
            return
        sst["iss"] = i + 1
        kind, idx = scale_jobs[i]
        buf = scaleb[i % 2]
        if kind == "u":
            ev = dma("scalar", "sc", lambda E, idx=idx, buf=buf: [
                E.dma_start(out=buf[:], in_=wug_t[idx][:, 0:2048])],
                after=(sst["use"][i % 2],), tag=f"scl{i}")
        elif kind == "g":
            ev = dma("scalar", "sc", lambda E, idx=idx, buf=buf: [
                E.dma_start(out=buf[:], in_=wug_t[idx][:, 2048:4096])],
                after=(sst["use"][i % 2],), tag=f"scl{i}")
        else:
            ev = dma("scalar", "sc", lambda E, idx=idx, buf=buf: [
                E.dma_start(out=buf[:, 1024 * j:1024 * (j + 1)],
                            in_=wd_t[2 * idx + j]) for j in range(2)],
                after=(sst["use"][i % 2],), n=2, tag=f"scl{i}")
        sst["ld"][i % 2] = ev

    def scale_consume():
        i = sst["con"]
        if i >= len(scale_jobs):
            return
        sst["con"] = i + 1
        kind, idx = scale_jobs[i]
        buf = scaleb[i % 2]
        ev_ld = sst["ld"][i % 2]
        dst = {"u": sacc_u, "g": sacc_g, "d": sacc_d}[kind]
        sst["use"][i % 2] = V_(lambda E, buf=buf, c=idx, dst=dst:
                               E.reduce_sum(dst[:, c:c + 1], buf[:],
                                            axis=AX,
                                            apply_absolute_value=True),
                               after=(ev_ld,), tag=f"sv{i}")

    def scale_step():
        while sst["iss"] - sst["con"] < 2 and sst["iss"] < len(scale_jobs):
            scale_issue()
        scale_consume()

    def scale_flush():
        while sst["con"] < len(scale_jobs):
            scale_step()

    ev_qa_done = [None, None]

    for ta in range(NTAU):
        sa = ExitStack()
        z1 = sa.enter_context(nc.sbuf_tensor(f"z1_{ta}", [128, DCN * TAU], BF16, side="right"))
        pbuf = sa.enter_context(nc.sbuf_tensor(f"pbuf_{ta}", [128, DCN * TAU], BF16, side="right"))

        sp = ExitStack()
        xb = [sp.enter_context(nc.sbuf_tensor(f"xb{i}_{ta}", [128, D], F32, side="right"))
              for i in range(2)]
        z0b = [sp.enter_context(nc.sbuf_tensor(f"z0b{i}_{ta}", [128, D], F32, side="right"))
               for i in range(2)]
        psA = [sp.enter_context(nc.psum_tensor(f"psA{i}_{ta}", [128, 256], F32))
               for i in range(2)]

        ev_z0 = [None] * 8
        prev_x_user = [ev_mx, ev_mx]
        prev_z0_user = [None, None]
        ev_evac = {}
        for g in range(4):
            tcs = [2 * g, 2 * g + 1]
            ev_x = {}
            for tc in tcs:
                gi = ta * 8 + tc
                ev_x[tc] = dma("sync", "x", lambda E, gi=gi, b=xb[tc % 2]: [
                    E.dma_start(out=b[:], in_=x_in[gi])],
                    after=(prev_x_user[tc % 2],), tag=f"x{ta}_{tc}")
            ev_st = []
            for tc in tcs:
                e1 = S_(lambda E, tc=tc, z0b=z0b, xb=xb: E.activation(
                    z0b[tc % 2][:], xb[tc % 2][:], AF.Square,
                    accum_out=stn["sq"][:, tc:tc + 1]),
                    after=(ev_x[tc], prev_z0_user[tc % 2]), tag=f"l1q{ta}{tc}")
                e2 = V_(lambda E, tc=tc, xb=xb: E.reduce_sum(
                    stn["sx"][:, tc:tc + 1], xb[tc % 2][:], axis=AX),
                    after=(ev_x[tc],), tag=f"l1x{ta}{tc}")
                ev_st += [e1, e2]
                scale_step()
            c0, c1 = 2 * g, 2 * g + 2

            def fn_v1(E, c0=c0, c1=c1):
                E.tensor_scalar(stn["mu"][:, c0:c1], stn["sx"][:, c0:c1],
                                1.0 / D, None, OP.mult)
                E.tensor_scalar(stn["ex2"][:, c0:c1], stn["sq"][:, c0:c1],
                                1.0 / D, None, OP.mult)
                E.drain()
                E.tensor_mul(stn["veps"][:, c0:c1], stn["mu"][:, c0:c1],
                             stn["mu"][:, c0:c1])
                E.drain()
                E.tensor_sub(stn["veps"][:, c0:c1], stn["ex2"][:, c0:c1],
                             stn["veps"][:, c0:c1])
                E.drain()
                E.tensor_scalar(stn["veps"][:, c0:c1],
                                stn["veps"][:, c0:c1], EPS_LN, None,
                                OP.add)
                return E.drain()
            ev_v1 = V_(fn_v1, after=ev_st, tag=f"l1v{ta}{g}")
            ev_s1 = S_(lambda E, c0=c0, c1=c1: E.activation(
                stn["sqr"][:, c0:c1], stn["veps"][:, c0:c1], AF.Sqrt),
                after=(ev_v1,), tag=f"l1s{ta}{g}")

            def fn_v2(E, c0=c0, c1=c1):
                E.drain()
                E.reciprocal(stn["i0"][:, c0:c1], stn["sqr"][:, c0:c1])
                E.drain()
                E.tensor_mul(stn["t1"][:, c0:c1], stn["i0"][:, c0:c1],
                             stn["i0"][:, c0:c1])
                E.drain()
                E.tensor_mul(stn["t1"][:, c0:c1], stn["t1"][:, c0:c1],
                             stn["veps"][:, c0:c1])
                E.drain()
                E.tensor_scalar(stn["t1"][:, c0:c1], stn["t1"][:, c0:c1],
                                -0.5, 1.5, OP.mult, OP.add)
                E.drain()
                E.tensor_mul(stn["inv"][:, c0:c1], stn["i0"][:, c0:c1],
                             stn["t1"][:, c0:c1])
                E.drain()
                E.tensor_mul(stn["nmi"][:, c0:c1], stn["mu"][:, c0:c1],
                             stn["inv"][:, c0:c1])
                E.drain()
                E.tensor_scalar(stn["nmi"][:, c0:c1],
                                stn["nmi"][:, c0:c1], -1.0, None,
                                OP.mult)
                return E.drain()
            ev_v2 = V_(fn_v2, after=(ev_s1,), tag=f"l1w{ta}{g}")
            for tc in tcs:
                ev_z0[tc] = S_(lambda E, tc=tc, z0b=z0b, xb=xb: E.activation(
                    z0b[tc % 2][:], xb[tc % 2][:], AF.Identity,
                    scale=stn["inv"][:, tc:tc + 1],
                    bias=stn["nmi"][:, tc:tc + 1]),
                    after=(ev_v2,), tag=f"z0{ta}{tc}")
                prev_x_user[tc % 2] = ev_z0[tc]
            ev_tT = {}
            for dc in range(DCN):
                par = dc % 2

                def fn_tr(E, dc=dc, par=par, tcs=tcs, psA=psA, z0b=z0b):
                    last = None
                    for i, tc in enumerate(tcs):
                        last = E.transpose(
                            psA[par][:, 128 * i:128 * (i + 1)],
                            z0b[tc % 2][:, 128 * dc:128 * (dc + 1)], ident[:])
                    return last
                ev_tT[dc] = T_(fn_tr, after=(ev_z0[tcs[0]], ev_z0[tcs[1]],
                                             ev_mx, ev_evac.get((g - 1, dc)),
                                             ev_evac.get((g, dc - 2))),
                               tag=f"tr{ta}{g}_{dc}")
                col0 = dc * TAU + g * 256
                ev_evac[(g, dc)] = S_(lambda E, par=par, dc=dc, col0=col0,
                                      z1=z1, psA=psA:
                                      E.activation(z1[:, col0:col0 + 256],
                                                   psA[par][:], AF.Identity,
                                                   scale=gamma[:, dc:dc + 1],
                                                   bias=beta[:, dc:dc + 1]),
                                      after=(ev_tT[dc],), tag=f"ze{ta}{g}_{dc}")
            prev_z0_user[0] = ev_tT[DCN - 1]
            prev_z0_user[1] = ev_tT[DCN - 1]
            scale_step()
        bz = pl.barrier(tag=f"z1b{ta}")
        sp.close()

        sm = ExitStack()

        def sbm(name, shape, dt):
            return sm.enter_context(nc.sbuf_tensor(f"{name}_{ta}", shape, dt, side="right"))
        wjc = [sbm(f"wjc{i}", [128, D], BF16) for i in range(2)]
        tabc = [sbm(f"tabc{i}", [128, TAU], BF16) for i in range(2)]
        tabs = [sbm(f"tabs{i}", [128, TAU], BF16) for i in range(2)]
        kbuf = sbm("kbuf", [128, TAU], BF16)
        krot = sbm("krot", [128, TAU], BF16)
        vbuf = sbm("vbuf", [128, TAU], BF16)
        vrot = sbm("vrot", [128, TAU], BF16)
        t1b = sbm("t1b", [128, TAU], BF16)
        ksqb = sbm("ksqb", [128, TAU], BF16)
        nrmk = sbm("nrmk", [1, TAU], F32)
        nrmv = sbm("nrmv", [1, TAU], F32)
        rrow = sbm("rrow", [1, TAU], F32)
        rbf = sbm("rbf", [1, TAU], BF16)
        rbc = sbm("rbc", [128, TAU], BF16)
        adump = sbm("adump", [128, TAU], BF16)
        psM = [[sm.enter_context(nc.psum_tensor(f"psM{s}{t}_{ta}", [128, 512], F32))
                for t in range(2)] for s in range(2)]
        psP = [sm.enter_context(nc.psum_tensor(f"psP{t}_{ta}", [128, 512], F32))
               for t in range(2)]
        psN = [sm.enter_context(nc.psum_tensor(f"psN{t}_{ta}", [1, 512], F32))
               for t in range(2)]

        ev_nrm0 = V_(lambda E: [E.memset(nrmk[:], 0.0),
                                E.memset(nrmv[:], 0.0)][-1], tag=f"n0{ta}")

        jorder = []
        for c in range(16):
            jorder += [16 + c, 32 + c]
        jorder += list(range(16))

        ev_mm = {}
        ev_free = {}
        ev_n_acc = None
        ev_perm_free = None
        ev_tab_free = [None, None]
        ev_tab = [None, None]
        ev_kfree = None
        ev_vfree = None
        ev_krot_free = None
        ev_vrot_free = None
        ev_t1_free = None
        k_src = {}
        k_ev = {}
        for ji, jc in enumerate(jorder):
            par = ji % 2
            ev_w = dma("gpsimd", "wi", lambda E, jc=jc, par=par: [
                E.dma_start(out=wjc[par][:], in_=w_in_t[jc])],
                after=(ev_mm.get(ji - 2),), tag=f"wi{ta}_{ji}")

            def fn_mm(E, par=par, wjc=wjc, z1=z1, psM=psM):
                last = None
                for tt in range(2):
                    for dc in range(DCN):
                        last = E.matmul(
                            psM[par][tt][:],
                            wjc[par][:, 128 * dc:128 * (dc + 1)],
                            z1[:, dc * TAU + tt * 512:dc * TAU + (tt + 1) * 512],
                            start=(dc == 0), stop=(dc == DCN - 1))
                return last
            ev_mm[ji] = T_(fn_mm, after=[ev_w, ev_free.get(ji - 2)] + bz,
                           tag=f"qkv{ta}_{ji}")

            if jc < 16:
                dc_q = jc

                def fn_qe(E, par=par, jc=jc, dc_q=dc_q, ta=ta):
                    last = None
                    for tt in range(2):
                        last = E.activation(
                            qa[ta][:, dc_q * TAU + tt * 512:
                                   dc_q * TAU + (tt + 1) * 512],
                            psM[par][tt][:], AF.Identity,
                            bias=b_in_sb[:, jc:jc + 1])
                    return last
                ev_free[ji] = S_(fn_qe, after=(ev_mm[ji],), tag=f"qe{ta}{jc}")
                continue

            is_v = jc >= 32
            c = jc - (32 if is_v else 16)
            dstb = vbuf if is_v else kbuf
            ev_dst_free = ev_vfree if is_v else ev_kfree

            def fn_kv(E, par=par, jc=jc, dstb=dstb, psM=psM):
                last = None
                for tt in range(2):
                    last = E.activation(dstb[:, tt * 512:(tt + 1) * 512],
                                        psM[par][tt][:], AF.Identity,
                                        bias=b_in_sb[:, jc:jc + 1])
                return last
            ev_ev = S_(fn_kv, after=(ev_mm[ji], ev_dst_free), tag=f"kv{ta}{jc}")

            def fn_sq(E, par=par, jc=jc, psM=psM, ksqb=ksqb):
                last = None
                for tt in range(2):
                    last = E.activation(ksqb[:, tt * 512:(tt + 1) * 512],
                                        psM[par][tt][:], AF.Square,
                                        bias=b_in_sb[:, jc:jc + 1])
                return last
            ev_sq = S_(fn_sq, after=(ev_mm[ji], ev_n_acc), tag=f"kq{ta}{jc}")
            ev_free[ji] = ev_sq

            def fn_nm(E, psN=psN, ksqb=ksqb):
                last = None
                for tt in range(2):
                    last = E.matmul(psN[tt][0:1, :], ones_col_bf[:],
                                    ksqb[:, tt * 512:(tt + 1) * 512],
                                    start=True, stop=True)
                return last
            ev_nm = T_(fn_nm, after=(ev_sq, ev_const, ev_n_acc),
                       tag=f"nm{ta}{jc}")
            nrm_dst = nrmv if is_v else nrmk

            def fn_na(E, nrm_dst=nrm_dst, psN=psN):
                for tt in range(2):
                    E.scalar_tensor_tensor(
                        nrm_dst[0:1, tt * 512:(tt + 1) * 512],
                        psN[tt][0:1, :], 1.0,
                        nrm_dst[0:1, tt * 512:(tt + 1) * 512],
                        OP.mult, OP.add)
                return E.drain()
            ev_n_acc = V_(fn_na, after=(ev_nm, ev_nrm0), tag=f"na{ta}{jc}")

            if c < ROTC and DOROT:
                tp = c % 2
                if not is_v:
                    ev_tab[tp] = dma("sync", "tb", lambda E, c=c, tp=tp, ta=ta: [
                        E.dma_start(out=tabc[tp][:],
                                    in_=cos_in[c][:, ta * TAU:(ta + 1) * TAU]),
                        E.dma_start(out=tabs[tp][:],
                                    in_=sin_in[c][:, ta * TAU:(ta + 1) * TAU]),
                    ], after=(ev_tab_free[tp],), n=2, tag=f"tb{ta}{c}")

                def fn_pm(E, dstb=dstb, psP=psP):
                    last = None
                    for tt in range(2):
                        last = E.matmul(psP[tt][:], perm[:],
                                        dstb[:, tt * 512:(tt + 1) * 512],
                                        start=True, stop=True)
                    return last
                ev_pm = T_(fn_pm, after=(ev_ev, ev_mx, ev_perm_free),
                           tag=f"pm{ta}{jc}")
                rotdst = vrot if is_v else krot
                rot_free = ev_vrot_free if is_v else ev_krot_free

                def fn_rot(E, dstb=dstb, rotdst=rotdst, tp=tp, tabc=tabc, tabs=tabs, psP=psP, t1b=t1b):
                    E.tensor_mul(t1b[:], dstb[:], tabc[tp][:])
                    for tt in range(2):
                        E.tensor_mul(rotdst[:, tt * 512:(tt + 1) * 512],
                                     psP[tt][:],
                                     tabs[tp][:, tt * 512:(tt + 1) * 512])
                    return E.tensor_add(rotdst[:], rotdst[:], t1b[:])
                ev_rot = V_(fn_rot, after=(ev_pm, ev_tab[tp], rot_free,
                                           ev_t1_free), tag=f"ro{ta}{jc}")
                ev_perm_free = ev_rot
                ev_t1_free = ev_rot
                if is_v:
                    ev_tab_free[tp] = ev_rot
                src, src_ev = rotdst, ev_rot
                if is_v:
                    ev_vfree = ev_rot
                else:
                    ev_kfree = ev_rot
            else:
                src, src_ev = dstb, ev_ev

            if not is_v:
                k_src[c] = src
                k_ev[c] = src_ev
            else:
                ev_p = V_(lambda E, c=c, a=k_src[c], b=src:
                          E.tensor_mul(pbuf[:, c * TAU:(c + 1) * TAU],
                                       a[:], b[:]),
                          after=(k_ev[c], src_ev), tag=f"p{ta}{c}")
                if c < ROTC:
                    ev_krot_free = ev_p
                    ev_vrot_free = ev_p
                else:
                    ev_kfree = ev_p
                    ev_vfree = ev_p
            scale_step()

        ev_prod = V_(lambda E: E.tensor_mul(rrow[:], nrmk[:], nrmv[:]),
                     after=(ev_n_acc,), tag=f"pr{ta}")
        ev_sqr = S_(lambda E: E.activation(rrow[:], rrow[:], AF.Sqrt),
                    after=(ev_prod,), tag=f"rs{ta}")

        def fn_rr(E, nrmk=nrmk, rrow=rrow, rbf=rbf):
            E.drain()
            E.reciprocal(nrmk[:], rrow[:])
            E.drain()
            E.tensor_copy(rbf[:], nrmk[:])
            return E.drain()
        ev_rr = V_(fn_rr, after=(ev_sqr,), tag=f"rr{ta}")

        def fn_rb(E, psP=psP, rbf=rbf):
            last = None
            for tt in range(2):
                last = E.matmul(psP[tt][:], ones_row_bf[:],
                                rbf[0:1, tt * 512:(tt + 1) * 512],
                                start=True, stop=True)
            return last
        ev_rb = T_(fn_rb, after=(ev_rr, ev_perm_free, ev_const), tag=f"rb{ta}")

        def fn_rbc(E, rbc=rbc, psP=psP):
            last = None
            for tt in range(2):
                last = E.activation(rbc[:, tt * 512:(tt + 1) * 512],
                                    psP[tt][:], AF.Identity)
            return last
        ev_rbc = S_(fn_rbc, after=(ev_rb,), tag=f"rc{ta}")

        ev_as = []
        for c in range(16):
            ev_as.append(V_(lambda E, c=c, adump=adump, pbuf=pbuf, rbc=rbc:
                E.scalar_tensor_tensor(
                    adump[:], pbuf[:, c * TAU:(c + 1) * TAU], 0.0, rbc[:],
                    OP.add, OP.mult, accum_out=a_tmp[:, c:c + 1]),
                after=(ev_rbc,), tag=f"as{ta}{c}"))
        def fn_aacc(E):
            E.drain()
            E.scalar_tensor_tensor(a_acc[:], a_tmp[:], 1.0, a_acc[:],
                                   OP.mult, OP.add)
            return E.drain()
        V_(fn_aacc, after=ev_as + [ev_const], tag=f"aa{ta}")

        ev_qa_done[ta] = pl.barrier(tag=f"Ab{ta}")
        if DBG and ta == 0:
            dma("sync", "mx", lambda E, z1=z1, pbuf=pbuf: [
                E.dma_start(out=dbg["d_z1"], in_=z1[:]),
                E.dma_start(out=dbg["d_pbuf"], in_=pbuf[:]),
                E.dma_start(out=dbg["d_qa0"], in_=qa[0][:]),
                E.dma_start(out=dbg["d_aacc"], in_=a_acc[:]),
                E.dma_start(out=dbg["d_stn"].rearrange("(k p c) -> k p c", p=128, c=16)[0], in_=stn["sx"][:]),
                E.dma_start(out=dbg["d_stn"].rearrange("(k p c) -> k p c", p=128, c=16)[1], in_=stn["sq"][:]),
                E.dma_start(out=dbg["d_stn"].rearrange("(k p c) -> k p c", p=128, c=16)[2], in_=stn["mu"][:]),
                E.dma_start(out=dbg["d_stn"].rearrange("(k p c) -> k p c", p=128, c=16)[3], in_=stn["veps"][:]),
                E.dma_start(out=dbg["d_stn"].rearrange("(k p c) -> k p c", p=128, c=16)[4], in_=stn["sqr"][:]),
                E.dma_start(out=dbg["d_stn"].rearrange("(k p c) -> k p c", p=128, c=16)[5], in_=stn["i0"][:]),
                E.dma_start(out=dbg["d_stn"].rearrange("(k p c) -> k p c", p=128, c=16)[6], in_=stn["inv"][:]),
                E.dma_start(out=dbg["d_stn"].rearrange("(k p c) -> k p c", p=128, c=16)[7], in_=stn["nmi"][:]),
            ], after=ev_qa_done[ta], n=12, tag="dbgA")
            pl.barrier(tag="dbgAb")
        sm.close()
        sa.close()

    sf = ExitStack()
    ps_s1 = sf.enter_context(nc.psum_tensor("ps_s1", [1, 16], F32))
    ps_sb2 = sf.enter_context(nc.psum_tensor("ps_sb2", [128, 16], F32))
    scale_flush()

    def fn_sfr(E):
        E.reduce_sum(scol[:, 0:1], sacc_u[:], axis=AX)
        E.reduce_sum(scol[:, 1:2], sacc_g[:], axis=AX)
        E.reduce_sum(scol[:, 2:3], sacc_d[:], axis=AX)
        return E.drain()
    ev_sfr = V_(fn_sfr, after=(sst["use"][0], sst["use"][1]), tag="sfr")

    def fn_sfm(E):
        last = None
        for j in range(3):
            last = E.matmul(ps_s1[0:1, j:j + 1], scol[:, j:j + 1],
                            ones_col_f[:], start=(j == 0), stop=(j == 2))
        return last
    ev_sfm = T_(fn_sfm, after=(ev_sfr, ev_const), tag="sfm")

    def fn_sfs(E):
        E.tensor_scalar(srow[0:1, 0:3], ps_s1[0:1, 0:3], 1.0 / NW, EPS_Q,
                        OP.mult, OP.add)
        E.drain()
        E.tensor_scalar(srow[0:1, 3:6], srow[0:1, 0:3], 0.5, None, OP.mult)
        E.drain()
        E.tensor_scalar(srow[0:1, 6:9], srow[0:1, 3:6], -1.0, None,
                        OP.mult)
        return E.drain()
    ev_sfs = V_(fn_sfs, after=(ev_sfm,), tag="sfs")

    def fn_sfb(E):
        E.matmul(ps_sb2[:, 0:3], ones_row_f[:], srow[0:1, 0:3],
                 start=True, stop=False)
        E.matmul(ps_sb2[:, 3:6], ones_row_f[:], srow[0:1, 3:6],
                 start=False, stop=False)
        return E.matmul(ps_sb2[:, 6:9], ones_row_f[:], srow[0:1, 6:9],
                        start=False, stop=True)
    ev_sfb = T_(fn_sfb, after=(ev_sfs,), tag="sfb")

    def fn_sfe(E):
        E.tensor_copy(s_scale[:, 0:3], ps_sb2[:, 0:3])
        E.tensor_copy(s_thr[:, 0:3], ps_sb2[:, 3:6])
        E.tensor_copy(s_nthr[:, 0:3], ps_sb2[:, 6:9])
        return E.drain()
    ev_scale_done = V_(fn_sfe, after=(ev_sfb,), tag="sfe")

    if PH < 2:
        ev_aprime = None
    ps_at = sf.enter_context(nc.psum_tensor("ps_at", [16, 128], F32))
    ps_atb = sf.enter_context(nc.psum_tensor("ps_atb", [128, 16], F32))

    if PH < 2:
        ev_t1 = None
    ev_t1 = T_(lambda E: E.transpose(ps_at[:], a_acc[:], ident[:]),
               after=ev_qa_done[1], tag="art") if PH >= 2 else None
    if PH >= 2:
        ev_v1 = V_(lambda E: E.tensor_copy(aflat[:], ps_at[:]), after=(ev_t1,),
                   tag="arv")
        ev_g1 = dma("gpsimd", "cc", lambda E: [
            E.dma_start(out=cc_in.ap(), in_=aflat[:])], after=(ev_v1,), tag="ar1")
        if PH >= 3:
            ev_cc = G_(lambda E: E.collective_compute(
                "AllReduce", OP.add, replica_groups=[list(range(NCORE))],
                ins=[cc_in.ap()], outs=[cc_out.ap()]), after=(ev_g1,), tag="arcc")
        else:
            ev_cc = dma("gpsimd", "cc", lambda E: [
                E.dma_start(out=cc_out.ap(), in_=cc_in.ap())], after=(ev_g1,),
                tag="arcc_bypass")
        ev_g2 = dma("gpsimd", "cc", lambda E: [
            E.dma_start(out=aflat[:], in_=cc_out.ap())], after=(ev_cc,), tag="ar2")
        ev_t2 = T_(lambda E: E.transpose(ps_atb[:], aflat[:],
                                         ident[0:16, 0:16]),
                   after=(ev_g2,), tag="art2")
        ev_aprime = V_(lambda E: E.tensor_scalar(
            a_prime[:], ps_atb[:], 0.5, 0.5, OP.mult, OP.add),
            after=(ev_t2,), tag="arp")

    if DBG and PH >= 2:
        dma("sync", "mx", lambda E: [
            E.dma_start(out=dbg["d_aprime"], in_=a_prime[:]),
            E.dma_start(out=dbg["d_srow"], in_=srow[0:1, :]),
            E.dma_start(out=dbg["d_sq3"].rearrange("(k p c) -> k p c", p=128, c=3)[0], in_=s_scale[:]),
            E.dma_start(out=dbg["d_sq3"].rearrange("(k p c) -> k p c", p=128, c=3)[1], in_=s_thr[:]),
            E.dma_start(out=dbg["d_sq3"].rearrange("(k p c) -> k p c", p=128, c=3)[2], in_=s_nthr[:]),
        ], after=(ev_aprime, ev_scale_done), n=5, tag="dbgAR")
    bar_ar = pl.barrier(tag="arb")
    sf.close()
    scale_es.close()
    down_acc = es.enter_context(
        nc.sbuf_tensor("down_acc", [128, DCN * TAU], F32, side="right"))

    ev_prev_store = [None, None]
    for tau in range(NTAU if PH >= 4 else 0):
        sc = ExitStack()

        def sbc(name, shape, dt):
            return sc.enter_context(nc.sbuf_tensor(f"{name}_c{tau}", shape, dt, side="right"))
        x2b = [sbc(f"x2b{i}", [128, TAU], F32) for i in range(2)]
        wob = [sbc(f"wob{i}", [128, D], BF16) for i in range(2)]
        y2b = [sbc(f"y2b{i}", [128, TAU], F32) for i in range(2)]
        dumpb = sbc("dumpb", [128, TAU], BF16)
        dumpq = sbc("dumpq", [128, TAU], BF16)
        mubc = sbc("mubc", [128, TAU], F32)
        invbc = sbc("invbc", [128, TAU], F32)
        tmphb = sbc("tmphb", [128, TAU], F32)
        st2 = {k: sbc("st2_" + k, [1, TAU], F32)
               for k in ("sx", "sq", "veps", "i0", "t1")}
        scps = ExitStack()
        psO = [[scps.enter_context(nc.psum_tensor(f"psO{s}{t}_c{tau}", [128, 512], F32))
                for t in range(2)] for s in range(2)]
        psL2x = [scps.enter_context(nc.psum_tensor(f"psL2x{t}_c{tau}", [1, 512], F32))
                 for t in range(2)]
        psL2q = [scps.enter_context(nc.psum_tensor(f"psL2q{t}_c{tau}", [1, 512], F32))
                 for t in range(2)]

        def fn_qs(E, tau=tau):
            last = None
            for dc in range(DCN):
                last = E.tensor_scalar(
                    qa[tau][:, dc * TAU:(dc + 1) * TAU],
                    qa[tau][:, dc * TAU:(dc + 1) * TAU],
                    a_prime[:, dc:dc + 1], None, OP.mult)
            return last
        ev_qs = V_(fn_qs, after=[ev_aprime] + ev_qa_done[tau], tag=f"qs{tau}")

        ev_mmO = {}
        pend_l2 = []
        ev_freeO = {}
        ev_h = {}
        ev_dump = {}
        ev_l2mm = None
        for jc in range(DCN):
            par = jc % 2
            ev_w = dma("gpsimd", "wo", lambda E, jc=jc, par=par: [
                E.dma_start(out=wob[par][:], in_=w_out_t[jc])],
                after=(ev_mmO.get(jc - 2),), tag=f"wo{tau}_{jc}")
            ev_x2 = dma("sync", "x2", lambda E, jc=jc, par=par, tau=tau: [
                E.dma_start(out=x2b[par][:],
                            in_=xT_in[jc][:, tau * TAU:(tau + 1) * TAU])],
                after=(ev_h.get(jc - 2),), tag=f"x2{tau}_{jc}")

            def fn_mmo(E, par=par, tau=tau):
                last = None
                for tt in range(2):
                    for dc in range(DCN):
                        last = E.matmul(
                            psO[par][tt][:],
                            wob[par][:, 128 * dc:128 * (dc + 1)],
                            qa[tau][:, dc * TAU + tt * 512:
                                     dc * TAU + (tt + 1) * 512],
                            start=(dc == 0), stop=(dc == DCN - 1))
                return last
            ev_mmO[jc] = T_(fn_mmo, after=(ev_w, ev_qs,
                                           ev_freeO.get(jc - 2)),
                            tag=f"op{tau}_{jc}")

            def fn_y2(E, par=par, jc=jc):
                last = None
                for tt in range(2):
                    last = E.activation(y2b[par][:, tt * 512:(tt + 1) * 512],
                                        psO[par][tt][:], AF.Identity,
                                        bias=b_out_sb[:, jc:jc + 1])
                return last
            ev_y2 = S_(fn_y2, after=(ev_mmO[jc], ev_h.get(jc - 2)),
                       tag=f"y2{tau}_{jc}")
            ev_freeO[jc] = ev_y2

            ev_h[jc] = V_(lambda E, par=par, jc=jc: E.scalar_tensor_tensor(
                down_acc[:, jc * TAU:(jc + 1) * TAU], x2b[par][:], 2.0,
                y2b[par][:], OP.mult, OP.add),
                after=(ev_x2, ev_y2, ev_prev_store[tau]), tag=f"h{tau}_{jc}")

            if pend_l2:
                fn_p, ev_d, jp = pend_l2.pop(0)
                ev_l2mm = T_(fn_p, after=(ev_d, ev_const), tag=f"lm{tau}_{jp}")

            def fn_dmp(E, jc=jc):
                E.activation(dumpb[:], down_acc[:, jc * TAU:(jc + 1) * TAU],
                             AF.Identity)
                return E.activation(dumpq[:],
                                    down_acc[:, jc * TAU:(jc + 1) * TAU],
                                    AF.Square)
            ev_dump[jc] = S_(fn_dmp, after=(ev_h[jc], ev_l2mm),
                             tag=f"dp{tau}_{jc}")

            def fn_l2(E, jc=jc):
                last = None
                for tt in range(2):
                    E.matmul(psL2x[tt][0:1, :], ones_col_bf[:],
                             dumpb[:, tt * 512:(tt + 1) * 512],
                             start=(jc == 0), stop=(jc == DCN - 1))
                    last = E.matmul(psL2q[tt][0:1, :], ones_col_bf[:],
                                    dumpq[:, tt * 512:(tt + 1) * 512],
                                    start=(jc == 0), stop=(jc == DCN - 1))
                return last
            pend_l2.append((fn_l2, ev_dump[jc], jc))
            scale_step()

        while pend_l2:
            fn_p, ev_d, jp = pend_l2.pop(0)
            ev_l2mm = T_(fn_p, after=(ev_d, ev_const), tag=f"lm{tau}_{jp}")

        def fn_l2e(E):
            for tt in range(2):
                E.tensor_copy(st2["sx"][0:1, tt * 512:(tt + 1) * 512],
                              psL2x[tt][0:1, :])
            for tt in range(2):
                E.tensor_copy(st2["sq"][0:1, tt * 512:(tt + 1) * 512],
                              psL2q[tt][0:1, :])
            E.drain()
            E.tensor_scalar(st2["sx"][:], st2["sx"][:], 1.0 / D, None, OP.mult)
            E.tensor_scalar(st2["sq"][:], st2["sq"][:], 1.0 / D, None, OP.mult)
            E.drain()
            E.tensor_mul(st2["veps"][:], st2["sx"][:], st2["sx"][:])
            E.drain()
            E.tensor_sub(st2["veps"][:], st2["sq"][:], st2["veps"][:])
            E.drain()
            E.tensor_scalar(st2["veps"][:], st2["veps"][:], EPS_LN,
                            None, OP.add)
            return E.drain()
        ev_l2e = V_(fn_l2e, after=(ev_l2mm,), tag=f"l2e{tau}")
        ev_l2s = S_(lambda E: E.activation(st2["sq"][:], st2["veps"][:],
                                           AF.Sqrt), after=(ev_l2e,),
                    tag=f"l2s{tau}")

        def fn_l2n(E):
            E.drain()
            E.reciprocal(st2["i0"][:], st2["sq"][:])
            E.drain()
            E.tensor_mul(st2["t1"][:], st2["i0"][:], st2["i0"][:])
            E.drain()
            E.tensor_mul(st2["t1"][:], st2["t1"][:], st2["veps"][:])
            E.drain()
            E.tensor_scalar(st2["t1"][:], st2["t1"][:], -0.5, 1.5, OP.mult,
                            OP.add)
            E.drain()
            E.scalar_tensor_tensor(st2["t1"][:], st2["i0"][:], 1.0,
                                   st2["t1"][:], OP.mult, OP.mult)
            return E.drain()
        ev_l2n = V_(fn_l2n, after=(ev_l2s,), tag=f"l2n{tau}")

        bar_c1 = pl.barrier(tag=f"cb{tau}")
        scps.close()
        scps2 = ExitStack()
        psB = [scps2.enter_context(nc.psum_tensor(f"psB{t}_c{tau}", [128, 512], F32))
               for t in range(2)]
        psB2 = [scps2.enter_context(nc.psum_tensor(f"psB2{t}_c{tau}", [128, 512], F32))
                for t in range(2)]

        def fn_bc(E):
            last = None
            for tt in range(2):
                E.matmul(psB[tt][:], ones_row_f[:],
                         st2["sx"][0:1, tt * 512:(tt + 1) * 512],
                         start=True, stop=True)
                last = E.matmul(psB2[tt][:], ones_row_f[:],
                                st2["t1"][0:1, tt * 512:(tt + 1) * 512],
                                start=True, stop=True)
            return last
        ev_bc = T_(fn_bc, after=[ev_l2n] + bar_c1, tag=f"bc{tau}")

        def fn_bce(E):
            for tt in range(2):
                E.activation(mubc[:, tt * 512:(tt + 1) * 512], psB[tt][:],
                             AF.Identity)
            last = None
            for tt in range(2):
                last = E.activation(invbc[:, tt * 512:(tt + 1) * 512],
                                    psB2[tt][:], AF.Identity)
            return last
        ev_bce = S_(fn_bce, after=(ev_bc,), tag=f"be{tau}")

        ev_apply = []
        for jc in range(DCN):
            def fn_ap(E, jc=jc):
                E.tensor_sub(tmphb[:], down_acc[:, jc * TAU:(jc + 1) * TAU],
                             mubc[:])
                return E.tensor_mul(down_acc[:, jc * TAU:(jc + 1) * TAU],
                                    tmphb[:], invbc[:])
            ev_apply.append(V_(fn_ap, after=(ev_bce,), tag=f"ap{tau}_{jc}"))

        bar_c2 = pl.barrier(tag=f"cb2{tau}")
        if DBG and tau == 0:
            dma("sync", "mx", lambda E: [
                E.dma_start(out=dbg["d_h1"], in_=down_acc[:])],
                after=bar_c2, tag="dbgC")
            bar_c2 = pl.barrier(tag="dbgCb")
        scps2.close()
        sc.close()
        qa_es[tau].close()

        if not DODOWN:
            continue
        sd_ = ExitStack()

        def sbd(name, shape, dt):
            return sd_.enter_context(nc.sbuf_tensor(f"{name}_d{tau}", shape, dt, side="right"))
        h1T = sbd("h1T", [128, DCN * TAU], BF16)
        hid = sbd("hid", [128, 8 * TAU], BF16)
        wuf = sbd("wuf", [128, D], F32)
        wgf = sbd("wgf", [128, D], F32)
        wdf = [sbd(f"wdf{i}", [128, 1024], F32) for i in range(2)]
        quq = [sbd(f"quq{i}", [128, D], BF16) for i in range(2)]
        qgq = [sbd(f"qgq{i}", [128, D], BF16) for i in range(2)]
        qdq = [sbd(f"qdq{i}", [128, 1024], BF16) for i in range(2)]
        mpos = sbd("mpos", [128, D], U8)
        mneg = sbd("mneg", [128, D], U8)
        silub = sbd("silub", [128, TAU], BF16)
        outb = sbd("outb", [128, D], F32)
        sdps = ExitStack()
        psU = [sdps.enter_context(nc.psum_tensor(f"psU{t}_d{tau}", [128, 512], F32))
               for t in range(2)]
        psG = [sdps.enter_context(nc.psum_tensor(f"psG{t}_d{tau}", [128, 512], F32))
               for t in range(2)]
        psDn = [sdps.enter_context(nc.psum_tensor(f"psD{t}_d{tau}", [128, 512], F32))
                for t in range(4)]

        def fn_h1t(E):
            last = None
            for dc in range(DCN):
                last = E.tensor_copy(h1T[:, dc * TAU:(dc + 1) * TAU],
                                     down_acc[:, dc * TAU:(dc + 1) * TAU])
            return last
        ev_h1t = V_(fn_h1t, after=ev_apply + bar_c2, tag=f"h1t{tau}")

        ev_wu_dma = None
        ev_q_ug = {}
        ev_mmU = {}
        ev_silu = {}
        ev_hid = {}
        ev_dn_ev = {}
        ev_dn_mm = {}

        def emit_ug_load_quant(ec):
            ev_d = dma("sync", "wu", lambda E, ec=ec: [
                E.dma_start(out=wuf[:], in_=wug_t[ec][:, 0:D]),
                E.dma_start(out=wgf[:], in_=wug_t[ec][:, D:2 * D])],
                after=(ev_q_ug.get(ec - 1), ev_scale_done), n=2,
                tag=f"wu{tau}_{ec}")

            def fn_qug(E, ec=ec):
                qp = ec % 2
                E.tensor_scalar(mpos[:], wuf[:], s_thr[:, 0:1], None,
                                OP.is_gt)
                E.tensor_scalar(mneg[:], wuf[:], s_nthr[:, 0:1], None,
                                OP.is_lt)
                E.tensor_sub(quq[qp][:], mpos[:], mneg[:])
                E.tensor_scalar(mpos[:], wgf[:], s_thr[:, 1:2], None,
                                OP.is_gt)
                E.tensor_scalar(mneg[:], wgf[:], s_nthr[:, 1:2], None,
                                OP.is_lt)
                return E.tensor_sub(qgq[qp][:], mpos[:], mneg[:])
            assert ec < 2 or (ec - 2) in ev_mmU
            ev_q_ug[ec] = V_(fn_qug, after=(ev_d, ev_scale_done,
                                            ev_mmU.get(ec - 2)),
                             tag=f"qq{tau}_{ec}")

        emit_ug_load_quant(0)
        for s in range(8):
            for ei in range(8):
                ec = 8 * s + ei
                if ec + 1 < ECN:
                    emit_ug_load_quant(ec + 1)

                def fn_mmu(E, ec=ec, tau=tau):
                    qp = ec % 2
                    last = None
                    for tt in range(2):
                        for dc in range(DCN):
                            E.matmul(psU[tt][:],
                                     quq[qp][:, 128 * dc:128 * (dc + 1)],
                                     h1T[:, dc * TAU + tt * 512:
                                         dc * TAU + (tt + 1) * 512],
                                     start=(dc == 0), stop=(dc == DCN - 1))
                        for dc in range(DCN):
                            last = E.matmul(
                                psG[tt][:],
                                qgq[qp][:, 128 * dc:128 * (dc + 1)],
                                h1T[:, dc * TAU + tt * 512:
                                    dc * TAU + (tt + 1) * 512],
                                start=(dc == 0), stop=(dc == DCN - 1))
                    return last
                ev_mmU[ec] = T_(fn_mmu, after=(ev_q_ug[ec], ev_h1t,
                                               ev_hid.get(ec - 1),
                                               ev_dn_mm.get(s - 1)),
                                tag=f"ug{tau}_{ec}")

                def fn_silu(E, ec=ec):
                    last = None
                    for tt in range(2):
                        last = E.activation(silub[:, tt * 512:(tt + 1) * 512],
                                            psU[tt][:], AF.Silu,
                                            scale=s_scale[:, 0:1])
                    return last
                ev_silu[ec] = S_(fn_silu, after=(ev_mmU[ec], ev_scale_done,
                                                 ev_dn_ev.get((s - 1, "sil"))),
                                 tag=f"si{tau}_{ec}")

                def fn_hid(E, ei=ei):
                    last = None
                    for tt in range(2):
                        last = E.scalar_tensor_tensor(
                            hid[:, ei * TAU + tt * 512:ei * TAU + (tt + 1) * 512],
                            psG[tt][:], s_scale[:, 1:2],
                            silub[:, tt * 512:(tt + 1) * 512],
                            OP.mult, OP.mult)
                    return last
                ev_hid[ec] = V_(fn_hid, after=(ev_silu[ec],), tag=f"hi{tau}_{ec}")

            def emit_wd_load_quant(s2, dc2):
                dp = dc2 % 2
                prev_rd = ev_q_ug.get(("d", s2, dc2 - 2))
                if prev_rd is None:
                    prev_rd = ev_q_ug.get(("d", s2 - 1, DCN + dc2 - 2))
                ev_d = dma("sync", "wd", lambda E, s2=s2, dc2=dc2, dp=dp: [
                    E.dma_start(out=wdf[dp][:], in_=wd_t[s2 * 16 + dc2])],
                    after=(prev_rd, ev_scale_done),
                    tag=f"wd{tau}_{s2}_{dc2}")

                def fn_qd(E, dp=dp):
                    E.tensor_scalar(mpos[:, 0:1024], wdf[dp][:],
                                    s_thr[:, 2:3], None, OP.is_gt)
                    E.tensor_scalar(mneg[:, 0:1024], wdf[dp][:],
                                    s_nthr[:, 2:3], None, OP.is_lt)
                    return E.tensor_sub(qdq[dp][:], mpos[:, 0:1024],
                                        mneg[:, 0:1024])
                ev_q_ug[("d", s2, dc2)] = V_(
                    fn_qd, after=(ev_d, ev_scale_done,
                                  ev_dn_mm.get((s2, dc2 - 2)),
                                  ev_dn_mm.get((s2 - 1, DCN - 2 + dc2))),
                    tag=f"qd{tau}_{s2}_{dc2}")

            if s == 0:
                emit_wd_load_quant(0, 0)
            ev_dn_ev_last = ev_dn_ev.get((s - 1, DCN - 1))
            for dc in range(DCN if DODOWN >= 2 else 0):
                dpar = dc % 2
                if dc + 1 < DCN:
                    emit_wd_load_quant(s, dc + 1)
                elif s + 1 < 8:
                    emit_wd_load_quant(s + 1, 0)

                def fn_dmm(E, s=s, dc=dc, dpar=dpar):
                    last = None
                    for tt in range(2):
                        for ei in range(8):
                            last = E.matmul(
                                psDn[2 * dpar + tt][:],
                                qdq[dpar][:, 128 * ei:128 * (ei + 1)],
                                hid[:, ei * TAU + tt * 512:
                                    ei * TAU + (tt + 1) * 512],
                                start=(ei == 0), stop=(ei == 7))
                    return last
                prev_same_par = ev_dn_ev.get((s, dc - 2))
                if dc < 2:
                    prev_same_par = ev_dn_ev.get((s - 1, DCN - 2 + dc))
                ev_dn_mm[(s, dc)] = T_(
                    fn_dmm, after=(ev_q_ug[("d", s, dc)], ev_hid[8 * s + 7],
                                   prev_same_par, ev_dn_ev_last,
                                   ev_mmU.get(8 * s + 7)),
                    tag=f"dn{tau}_{s}_{dc}")
                ev_dn_ev_last = None

                def fn_dev(E, dc=dc, dpar=dpar):
                    last = None
                    for tt in range(2):
                        sl = slice(dc * TAU + tt * 512,
                                   dc * TAU + (tt + 1) * 512)
                        E.tensor_scalar(silub[:, tt * 512:(tt + 1) * 512],
                                        psDn[2 * dpar + tt][:],
                                        s_scale[:, 2:3], None, OP.mult)
                        last = E.scalar_tensor_tensor(
                            down_acc[:, sl],
                            silub[:, tt * 512:(tt + 1) * 512], 1.0,
                            down_acc[:, sl], OP.mult, OP.add)
                    return last
                ev_dn_ev[(s, dc)] = V_(fn_dev, after=(ev_dn_mm[(s, dc)],),
                                       tag=f"de{tau}_{s}_{dc}")
            ev_dn_ev[(s, "sil")] = ev_dn_ev.get((s, DCN - 1))
            ev_dn_mm[s] = ev_dn_mm.get((s, DCN - 1))
            scale_step()

        bar_d1 = pl.barrier(tag=f"db{tau}")
        if DBG and tau == 0:
            dma("sync", "mx", lambda E, qdq=qdq, silub=silub: [
                E.dma_start(out=dbg["d_h2"], in_=down_acc[:]),
                E.dma_start(out=dbg["d_hid"], in_=hid[:]),
                E.dma_start(out=dbg["d_qd"], in_=qdq[1][:]),
                E.dma_start(out=dbg["d_dn"], in_=silub[:])],
                after=bar_d1, n=4, tag="dbgD")
            bar_d1 = pl.barrier(tag="dbgDb")
        sdps.close()
        sdps2 = ExitStack()
        psL3 = [sdps2.enter_context(nc.psum_tensor(f"psL3{i}_d{tau}", [128, 2048], F32))
                for i in range(2)]

        ev_l3 = {}
        ev_store = None
        ev_outb_free = None
        ev_gate = {}
        psx = sdps2.enter_context(nc.sbuf_tensor(f"l3px_d{tau}", [128, 8], F32,
                                                 side="right"))
        psq = sdps2.enter_context(nc.sbuf_tensor(f"l3pq_d{tau}", [128, 8], F32,
                                                 side="right"))
        for tc in range(8 if DODOWN >= 3 else 0):
            lpar = tc % 2
            ev_tb = {}
            for b in range(4):
                def fn_l3t(E, tc=tc, lpar=lpar, b=b, tau=tau):
                    last = None
                    for j in range(4):
                        dc = 4 * b + j
                        last = E.matmul(
                            psL3[lpar][:, 128 * dc:128 * (dc + 1)],
                            down_acc[:, dc * TAU + tc * 128:
                                     dc * TAU + (tc + 1) * 128],
                            ident[:], start=True, stop=True)
                    return last
                ev_tb[b] = T_(fn_l3t,
                              after=[ev_gate.get((tc, b - 2)),
                                     ev_gate.get((tc - 1, b + 2)),
                                     ev_l3.get(tc - 2), ev_mx] + bar_d1,
                              tag=f"l3t{tau}_{tc}_{b}")
                ev_rx = V_(lambda E, lpar=lpar, b=b: E.reduce_sum(
                    psx[:, b + 4 * lpar:b + 4 * lpar + 1],
                    psL3[lpar][:, 512 * b:512 * (b + 1)], axis=AX),
                    after=(ev_tb[b],), tag=f"l3x{tau}_{tc}_{b}")
                ev_rq = S_(lambda E, lpar=lpar, b=b: E.activation(
                    outb[:, 512 * b:512 * (b + 1)],
                    psL3[lpar][:, 512 * b:512 * (b + 1)], AF.Square,
                    accum_out=psq[:, b + 4 * lpar:b + 4 * lpar + 1]),
                    after=(ev_tb[b], ev_rx, ev_outb_free),
                    tag=f"l3q{tau}_{tc}_{b}")
                ev_gate[(tc, b)] = ev_rq

            def fn_l3v(E, tc=tc, lpar=lpar):
                E.drain()
                E.reduce_sum(stn["sx"][:, tc:tc + 1],
                             psx[:, 4 * lpar:4 * lpar + 4], axis=AX)
                E.reduce_sum(stn["sq"][:, tc:tc + 1],
                             psq[:, 4 * lpar:4 * lpar + 4], axis=AX)
                E.drain()
                E.tensor_scalar(stn["mu"][:, tc:tc + 1], stn["sx"][:, tc:tc + 1],
                                1.0 / D, None, OP.mult)
                E.tensor_scalar(stn["ex2"][:, tc:tc + 1], stn["sq"][:, tc:tc + 1],
                                1.0 / D, None, OP.mult)
                E.drain()
                E.tensor_mul(stn["veps"][:, tc:tc + 1], stn["mu"][:, tc:tc + 1],
                             stn["mu"][:, tc:tc + 1])
                E.drain()
                E.tensor_sub(stn["veps"][:, tc:tc + 1], stn["ex2"][:, tc:tc + 1],
                             stn["veps"][:, tc:tc + 1])
                E.drain()
                E.tensor_scalar(stn["veps"][:, tc:tc + 1],
                                stn["veps"][:, tc:tc + 1], EPS_LN,
                                None, OP.add)
                return E.drain()
            ev_l3v = V_(fn_l3v, after=[ev_gate[(tc, b)] for b in range(4)],
                        tag=f"l3v{tau}_{tc}")
            ev_l3s = S_(lambda E, tc=tc: E.activation(
                stn["sqr"][:, tc:tc + 1], stn["veps"][:, tc:tc + 1], AF.Sqrt),
                after=(ev_l3v,), tag=f"l3s{tau}_{tc}")

            def fn_l3n(E, tc=tc):
                E.drain()
                E.reciprocal(stn["i0"][:, tc:tc + 1], stn["sqr"][:, tc:tc + 1])
                E.drain()
                E.tensor_mul(stn["t1"][:, tc:tc + 1], stn["i0"][:, tc:tc + 1],
                             stn["i0"][:, tc:tc + 1])
                E.drain()
                E.tensor_mul(stn["t1"][:, tc:tc + 1], stn["t1"][:, tc:tc + 1],
                             stn["veps"][:, tc:tc + 1])
                E.drain()
                E.tensor_scalar(stn["t1"][:, tc:tc + 1], stn["t1"][:, tc:tc + 1],
                                -0.5, 1.5, OP.mult, OP.add)
                E.drain()
                E.tensor_mul(stn["inv"][:, tc:tc + 1], stn["i0"][:, tc:tc + 1],
                             stn["t1"][:, tc:tc + 1])
                E.drain()
                E.tensor_mul(stn["nmi"][:, tc:tc + 1], stn["mu"][:, tc:tc + 1],
                             stn["inv"][:, tc:tc + 1])
                E.drain()
                E.tensor_scalar(stn["nmi"][:, tc:tc + 1],
                                stn["nmi"][:, tc:tc + 1], -1.0, None,
                                OP.mult)
                return E.drain()
            ev_l3n = V_(fn_l3n, after=(ev_l3s,), tag=f"l3n{tau}_{tc}")

            ev_ap3 = S_(lambda E, lpar=lpar, tc=tc: E.activation(
                outb[:], psL3[lpar][:], AF.Identity,
                scale=stn["inv"][:, tc:tc + 1], bias=stn["nmi"][:, tc:tc + 1]),
                after=(ev_l3n,), tag=f"l3a{tau}_{tc}")
            ev_l3[tc] = ev_ap3
            gi = tau * 8 + tc
            ev_store = dma("scalar", "ou", lambda E, gi=gi: [
                E.dma_start(out=y_out[gi], in_=outb[:])],
                after=(ev_ap3,), tag=f"st{tau}_{tc}")
            ev_outb_free = ev_store
        ev_prev_store[(tau + 1) % 2] = ev_store

        bar_d2 = pl.barrier(tag=f"db2{tau}")
        sdps2.close()
        sd_.close()

    def fn_fin(E):
        return E.nop()
    n_ou_0 = pl.counts[SD["ou"][0]]
    n_ou_1 = pl.counts[SD["ou"][1]]
    if n_ou_0 or n_ou_1:
        pl.op("sync", fn_fin,
              after=((SD["ou"][0], n_ou_0), (SD["ou"][1], n_ou_1)), tag="fin")
    else:
        pl.barrier(tag="fin")

    if PH < 4:
        qa_es[0].close()
        qa_es[1].close()

    pl.check()

    semh = {}
    for name in pl.sems:
        semh[name] = es.enter_context(nc.semaphore(name))
    block = es.enter_context(nc.Block())

    def make_runner(eng):
        def run(E):
            for waits, fn, sem, amt, tag, is_dma in pl.ops[eng]:
                for s, v in waits:
                    E.wait_ge(semh[s], v)
                ret = fn(E)
                if sem is None:
                    continue
                if is_dma:
                    assert isinstance(ret, list) and ret, tag
                    for inst in ret:
                        inst.then_inc(semh[sem], 16)
                else:
                    if ret is None:
                        ret = E.nop()
                    ret.then_inc(semh[sem], amt)
        return run

    block.tensor(make_runner("tensor"))
    block.vector(make_runner("vector"))
    block.scalar(make_runner("scalar"))
    block.sync(make_runner("sync"))
    block.gpsimd(make_runner("gpsimd"))

    es.close()
    return nc


def _prep_host(inputs):
    X = np.asarray(inputs["L0_attn_X"], dtype=np.float32).reshape(S, D)
    W_in = np.ascontiguousarray(np.asarray(inputs["L0_attn_W_in"], np.float32))
    b_in = np.asarray(inputs["L0_attn_b_in"], np.float32)
    W_out = np.ascontiguousarray(np.asarray(inputs["L0_attn_W_out"], np.float32))
    b_out = np.asarray(inputs["L0_attn_b_out"], np.float32)
    gam = np.asarray(inputs["L0_attn_gamma"], np.float32)
    bet = np.asarray(inputs["L0_attn_beta"], np.float32)
    W_up = np.asarray(inputs["L0_ffn_W_up"], np.float32)
    W_gate = np.asarray(inputs["L0_ffn_W_gate"], np.float32)
    W_down = np.asarray(inputs["L0_ffn_W_down"], np.float32)

    w_in_t = np.ascontiguousarray(
        W_in.reshape(DCN, 128, JCN, 128).transpose(2, 1, 0, 3)).ravel()
    w_out_t = np.ascontiguousarray(
        W_out.reshape(DCN, 128, DCN, 128).transpose(2, 1, 0, 3)).ravel()
    tu = W_up.reshape(ECN, 128, DCN, 128).transpose(0, 3, 2, 1)
    tg = W_gate.reshape(ECN, 128, DCN, 128).transpose(0, 3, 2, 1)
    wug = np.ascontiguousarray(
        np.stack([tu, tg], axis=1).transpose(0, 2, 1, 3, 4)).ravel()
    wd = np.ascontiguousarray(
        W_down.reshape(DCN, 128, 8, 8, 128).transpose(2, 0, 4, 3, 1)).ravel()

    gamma_t = np.ascontiguousarray(gam.reshape(DCN, 128).T).ravel()
    beta_t = np.ascontiguousarray(bet.reshape(DCN, 128).T).ravel()
    b_in_t = np.ascontiguousarray(b_in.reshape(JCN, 128).T).ravel()
    b_out_t = np.ascontiguousarray(b_out.reshape(DCN, 128).T).ravel()
    ident = np.eye(128, dtype=np.float32).ravel()
    permm = np.zeros((128, 128), dtype=np.float64)
    for i in range(64):
        permm[2 * i + 1, 2 * i] = -1.0
        permm[2 * i, 2 * i + 1] = 1.0
    permm = permm.astype(ml_dtypes.bfloat16).ravel()

    inv_freq = 1.0 / (10000.0 ** (np.arange(0, 1024, 2, dtype=np.float64) / 1024.0))
    per_core = []
    for c0 in range(NCORE):
        pos = np.arange(c0 * T, (c0 + 1) * T, dtype=np.float64)
        ang = pos[None, :] * inv_freq[:, None]
        cosf = np.cos(ang)
        sinf = np.sin(ang)
        cos_full = np.repeat(cosf, 2, axis=0)
        sin_full = np.repeat(sinf, 2, axis=0)
        cos_t = np.ascontiguousarray(
            cos_full.reshape(ROTC, 128, T)).astype(ml_dtypes.bfloat16).ravel()
        sin_t = np.ascontiguousarray(
            sin_full.reshape(ROTC, 128, T)).astype(ml_dtypes.bfloat16).ravel()
        xs = np.ascontiguousarray(X[c0 * T:(c0 + 1) * T, :])
        xT = np.ascontiguousarray(xs.T)
        per_core.append({
            "x": xs.ravel(), "xT": xT.ravel(),
            "w_in_t": w_in_t, "w_out_t": w_out_t, "wug_t": wug, "wd_t": wd,
            "cos_t": cos_t, "sin_t": sin_t,
            "gamma_t": gamma_t, "beta_t": beta_t, "b_in_t": b_in_t,
            "b_out_t": b_out_t, "ident": ident, "perm": permm,
        })
    return per_core


def kernel(**inputs):
    in_maps = _prep_host(inputs)
    nc = build_nc()
    res = run_bass_kernel_spmd(nc, in_maps, core_ids=list(range(NCORE)))
    out = np.concatenate(
        [res.results[c]["y"].reshape(T, D) for c in range(NCORE)], axis=0)
    return out.reshape(1, S, D).astype(np.float32)
